# version 20
# baseline (speedup 1.0000x reference)
import math
import contextlib
import itertools
import numpy as np
import concourse.bass as bass
import concourse.mybir as mybir
from concourse.alu_op_type import AluOpType
from concourse.bass_utils import run_bass_kernel_spmd

F32 = mybir.dt.float32
BF16 = mybir.dt.bfloat16
I32 = mybir.dt.int32
AX = mybir.AxisListType.X
AF = mybir.ActivationFunctionType
MAGIC = 12582912.0
CINV = 1.0 / math.sqrt(128.0)
NEG = -1.0e30

T = 2048
D = 1024
II = 3072
DK = 128
NT = T // 128
ND = D // 128
NI = II // 128
RING = 6


class Ctx:
    def __init__(self, nc, es):
        self.nc = nc
        self.streams = {k: [] for k in ("sync", "gpsimd", "vector", "scalar", "tensor")}
        self.sems = {}
        self.cnt = {}
        for name in ("La", "Lb", "Lw", "Da", "Db", "Sa", "Sb", "Ga", "Gb", "Gc", "Gd", "Gi", "Gwa", "Gwb", "CC", "sMM", "sVE", "sSC"):
            self.sems[name] = es.enter_context(nc.semaphore(name))
            self.cnt[name] = 0

    def on(self, eng, fn):
        self.streams[eng].append(fn)

    def bump(self, name, by=1):
        self.cnt[name] += by
        return self.sems[name], self.cnt[name]

    def val(self, name):
        return self.cnt[name]

    def wait(self, eng, name, value):
        if value is None or value <= 0:
            return
        h = self.sems[name]
        self.on(eng, lambda e, hh=h, v=value: e.wait_ge(hh, v))

    def emit(self, block):
        def run(name):
            def body(e):
                for fn in self.streams[name]:
                    fn(e)
            return body
        block.sync(run("sync"))
        block.gpsimd(run("gpsimd"))
        block.vector(run("vector"))
        block.scalar(run("scalar"))
        block.tensor(run("tensor"))


def build(TT=2048, upto="all"):
    T = TT
    NT = T // 128
    nc = bass.Bass("TRN2", target_bir_lowering=False)

    x_h = nc.dram_tensor("x", [T, D], F32, kind="ExternalInput").ap()
    wq_h = nc.dram_tensor("wq", [D, DK], F32, kind="ExternalInput").ap()
    wk_h = nc.dram_tensor("wk", [D, DK], F32, kind="ExternalInput").ap()
    wv_h = nc.dram_tensor("wv", [D, DK], F32, kind="ExternalInput").ap()
    wo_h = nc.dram_tensor("wo", [DK, D], F32, kind="ExternalInput").ap()
    wg_h = nc.dram_tensor("wg", [II, D], F32, kind="ExternalInput").ap()
    wu_h = nc.dram_tensor("wu", [II, D], F32, kind="ExternalInput").ap()
    wd_h = nc.dram_tensor("wd", [D, II], F32, kind="ExternalInput").ap()
    out_h = nc.dram_tensor("out", [T, D], F32, kind="ExternalOutput").ap()

    xr_h = nc.dram_tensor("xres", [T, D], F32, kind="Internal").ap()
    wgT_h = nc.dram_tensor("wgT", [NI, 128, ND * 128], BF16, kind="Internal").ap()
    wuT_h = nc.dram_tensor("wuT", [NI, 128, ND * 128], BF16, kind="Internal").ap()
    wdT_h = nc.dram_tensor("wdT", [NI, 128, D], BF16, kind="Internal").ap()
    ups_h = nc.dram_tensor("ups", [NI, 128, T], BF16, kind="Internal").ap()
    cci_t = nc.dram_tensor("cci", [1, 8], F32, kind="Internal")
    cco_t = nc.dram_tensor("cco", [1, 8], F32, kind="Internal")
    cci = cci_t.ap()
    cco = cco_t.ap()

    es = contextlib.ExitStack()

    def sbuf(name, shape, dt=F32):
        return es.enter_context(nc.sbuf_tensor(name, list(shape), dt))

    SW = max(T, 2 * D)
    G = sbuf("G", (128, NI * T), BF16)
    XT = sbuf("XT", (128, ND * T), BF16)
    ARN = sbuf("ARN", (128, max(16384, 3072 + 8 * T + 1024)), BF16)
    SCR = sbuf("SCR", (128, SW))
    SCR3 = sbuf("SCR3", (128, SW))
    SCR2 = sbuf("SCR2", (128, 2 * D))
    XL = sbuf("XL", (128, D))
    IOT = sbuf("IOT", (128, 128), I32)
    IDF = sbuf("IDF", (128, 128))
    IDB = sbuf("IDB", (128, 128), BF16)
    MASKD = sbuf("MASKD", (128, 128))
    TRS = sbuf("TRS", (128, 128))
    TCOL = sbuf("TCOL", (128, 1))
    CCR = sbuf("CCR", (1, 8))
    RS = sbuf("RS", (1, 16))
    CROW = sbuf("CROW", (1, 8))
    CONST = sbuf("CONST", (128, 16))
    ONE1 = sbuf("ONE1", (1, 128))
    MAGP = sbuf("MAGP", (128, 1))
    NMAGP = sbuf("NMAGP", (128, 1))
    EPSP = sbuf("EPSP", (128, 1))
    GMAX = sbuf("GMAX", (128, 4))
    TMPC = sbuf("TMPC", (128, 1))
    BMX = sbuf("BMX", (128, 32))
    BSC = sbuf("BSC", (128, 32))
    BRC = sbuf("BRC", (128, 32))
    ST_SUM = sbuf("ST_SUM", (128, 2))
    ST_SQ = sbuf("ST_SQ", (128, 2))
    ST_MX = sbuf("ST_MX", (128, 2))
    ST_MN = sbuf("ST_MN", (128, 2))
    ST_M2 = sbuf("ST_M2", (128, 2))
    ST_SD = sbuf("ST_SD", (128, 2))
    ST_RS = sbuf("ST_RS", (128, 2))
    ST_NB = sbuf("ST_NB", (128, 2))
    MROW = sbuf("MROW", (128, 1))
    BROW = sbuf("BROW", (128, 1))
    RSUM = sbuf("RSUM", (128, 1))
    S_SUM = sbuf("S_SUM", (128, NT))
    S_SQ = sbuf("S_SQ", (128, NT))
    S_M2 = sbuf("S_M2", (128, NT))
    S_MX = sbuf("S_MX", (128, NT))
    S_MN = sbuf("S_MN", (128, NT))
    S_SD = sbuf("S_SD", (128, NT))
    S_RS = sbuf("S_RS", (128, NT))
    S_NB = sbuf("S_NB", (128, NT))
    MROWB = sbuf("MROWB", (128, 8))
    MROW1 = sbuf("MROW1", (128, 2))
    BROWB = sbuf("BROWB", (128, 2))
    RSUMB = sbuf("RSUMB", (128, 2))
    RS2B = sbuf("RS2B", (128, 2))
    RSTT = sbuf("RSTT", (128, 2))
    RRS = sbuf("RRS", (128, NT))
    R11 = sbuf("R11", (1, 1))
    R12 = sbuf("R12", (1, 1))

    PS = es.enter_context(nc.psum_tensor("PS", [128, 3584], F32))
    PTB = es.enter_context(nc.psum_tensor("PTB", [128, 1024], BF16))

    def arn(a, b):
        return ARN[:, a:b]
    WTS = [arn(0, 1024), arn(1024, 2048)]
    WQB = arn(2048, 3072)
    o = 3072
    UPSTG = [arn(o, o + T), arn(o + T, o + 2 * T)]
    UPLD = [arn(o + 2 * T, o + 3 * T), arn(o + 3 * T, o + 4 * T)]
    KUBp = [UPSTG[0], UPSTG[1]]
    WDS = arn(0, NI * 512)
    QTv = arn(0, T)
    KTv = arn(T, 2 * T)
    VBv = arn(2 * T, 2 * T + NT * 128)
    ABv = arn(2 * T + NT * 128, 3 * T + NT * 128)
    ATv = arn(3 * T + NT * 128, 4 * T + NT * 128)
    o2 = 4 * T + NT * 128
    OTv = arn(o2, o2 + 128)
    WQKV = arn(o2 + 128, o2 + 128 + 3072)
    WOB = arn(o2 + 128 + 3072, o2 + 128 + 4096)
    ABv2 = arn(o2 + 128 + 4096, o2 + 128 + 4096 + T)
    WQB2 = arn(o2 + 128 + 4096 + T, o2 + 128 + 4096 + T + 1024)
    WQBp = [WQB, WQB2]

    c = Ctx(nc, es)
    ring_free = [[0, 0] for _ in range(RING)]
    ring_pos = [0]

    with es:
        SELFGATE = {"La", "Lb", "Da", "Db", "Sa", "Sb", "Ga", "Gb", "Gc", "Gd"}

        def load(dst, src, sem="Lw", gate=None):
            if sem in SELFGATE and c.val(sem) > 0:
                c.wait("sync", sem, c.val(sem))
            if gate:
                for nm, v in gate:
                    c.wait("sync", nm, v)
            h, v = c.bump(sem, 16)
            c.on("sync", lambda e, d=dst, s=src: e.dma_start(d, s).then_inc(h, 16))
            return (sem, v)

        def gstore(dst, src, sem="Ga", gate=None):
            if sem in SELFGATE and c.val(sem) > 0:
                c.wait("gpsimd", sem, c.val(sem))
            if gate:
                for nm, v in gate:
                    c.wait("gpsimd", nm, v)
            h, v = c.bump(sem, 16)
            c.on("gpsimd", lambda e, d=dst, s=src: e.dma_start(d, s).then_inc(h, 16))
            return (sem, v)

        def sstore(dst, src, sem="Sa"):
            if sem in SELFGATE and c.val(sem) > 0:
                c.wait("scalar", sem, c.val(sem))
            h, v = c.bump(sem, 16)
            c.on("scalar", lambda e, d=dst, s=src: e.dma_start(d, s).then_inc(h, 16))
            return (sem, v)

        def waitp(eng, pair):
            if pair and pair[1] > 0:
                c.wait(eng, pair[0], pair[1])

        LS = ["La", "Lb"]
        SS = ["Sa", "Sb"]
        GS = ["Ga", "Gb"]
        GW = ["Gwa", "Gwb"]

        def vi(fn):
            h, v = c.bump("sVE")
            if v > 1:
                c.on("vector", lambda e, hh=h, vv=v - 1: e.wait_ge(hh, vv))
            c.on("vector", lambda e, f=fn, hh=h: f(e).then_inc(hh, 1))
            return v

        def si(fn):
            h, v = c.bump("sSC")
            if v > 1:
                c.on("scalar", lambda e, hh=h, vv=v - 1: e.wait_ge(hh, vv))
            c.on("scalar", lambda e, f=fn, hh=h: f(e).then_inc(hh, 1))
            return v

        def ti(fn):
            h, v = c.bump("sMM")
            c.on("tensor", lambda e, f=fn, hh=h: f(e).then_inc(hh, 1))
            return v

        def barrier_t():
            c.wait("tensor", "sSC", c.val("sSC"))
            c.wait("tensor", "sVE", c.val("sVE"))

        c.on("gpsimd", lambda e: e.iota(IOT[:], pattern=[[1, 128]], base=0,
                                        channel_multiplier=-1).then_inc(c.sems["Gi"], 16))
        c.cnt["Gi"] += 16
        v_iota = c.val("Gi")

        vi(lambda v: v.memset(MAGP[:], MAGIC))
        vi(lambda v: v.memset(NMAGP[:], -MAGIC))
        vi(lambda v: v.memset(EPSP[:], 1e-5))
        vi(lambda v: v.memset(ONE1[:], 1.0))
        vi(lambda v: v.memset(TRS[:], 0.0))
        vi(lambda v: v.memset(GMAX[:], 0.0))
        c.wait("vector", "Gi", v_iota)
        vi(lambda v: v.tensor_scalar(IDF[:], IOT[:], 0.0, None, AluOpType.is_equal))
        vi(lambda v: v.tensor_scalar(IDB[:], IOT[:], 0.0, None, AluOpType.is_equal))
        vi(lambda v: v.tensor_scalar(MASKD[:], IOT[:], 0.0, NEG, AluOpType.is_gt,
                                     AluOpType.mult))

        def load_cast(w_src, dst, ncols, slot):
            v_l = load(SCR2[:, slot * D: slot * D + ncols]
                       .rearrange("p (cc k) -> p cc k", k=128), w_src, sem=LS[slot])
            waitp("scalar", v_l)
            return si(lambda s, d=dst, sl=slot, n=ncols:
                      s.activation(d[:, 0:n], SCR2[:, sl * D: sl * D + n], AF.Copy))


        exch_prev = {"s1": 0, "y": 0}

        def exchange(span, math_fn, nconst, const_col, defer=False):
            c0, K = span
            v1x = vi(lambda v: v.tensor_copy(TRS[:, 0:K], GMAX[:, c0:c0 + K]))
            vi(lambda v: v.memset(GMAX[:, c0:c0 + K], 0.0))
            c.wait("tensor", "sVE", v1x)
            c.wait("tensor", "sSC", exch_prev["s1"])
            c.wait("tensor", "sVE", exch_prev["y"])
            t1 = ti(lambda t: t.transpose(PS[:, 3072:3200], TRS[:], IDF[:]))
            c.wait("vector", "sMM", t1)
            v2x = vi(lambda v: v.tensor_reduce(TCOL[:], PS[:, 3072:3200], AX,
                                               AluOpType.max))
            p_ci = gstore(cci[0:1, 0:8], TCOL[0:8, 0:1], sem="Gd", gate=[("sVE", v2x)])
            waitp("gpsimd", p_ci)
            hG, v_cc = c.bump("CC", 1)
            c.on("gpsimd", lambda e, hh=hG: e.collective_compute(
                "AllReduce", AluOpType.max, replica_groups=[list(range(8))],
                ins=[cci_t.ap().opt()], outs=[cco_t.ap().opt()]).then_inc(hh, 1))
            c.wait("gpsimd", "CC", v_cc)
            hG2, v_co = c.bump("Gc", 16)
            c.on("gpsimd", lambda e, hh=hG2: e.dma_start(CCR[:, :], cco[:, :]).then_inc(hh, 16))
            c.wait("vector", "Gc", v_co)
            v3x = 0
            for op in math_fn:
                v3x = vi(op)

            def fin(v3=v3x, n=nconst, cc=const_col):
                c.wait("tensor", "sVE", v3)
                t2 = ti(lambda t: t.matmul(PS[:, 3200:3200 + n], ONE1[:, :],
                                           CROW[0:1, 0:n], start=True, stop=True))
                c.wait("scalar", "sMM", t2)
                exch_prev["s1"] = si(lambda s:
                                     s.activation(CONST[:, cc:cc + n],
                                                  PS[:, 3200:3200 + n], AF.Copy))
            if defer:
                return fin
            fin()

        def rmul(v, dst, a, b):
            return v.tensor_tensor(dst, a, b, AluOpType.mult)

        math_r0 = [
            lambda v: v.tensor_scalar(RS[0:1, 0:1], CCR[0:1, 0:1], 1.0 / 32767.0,
                                      1e-12, AluOpType.mult, AluOpType.max),
            lambda v: v.reciprocal(CROW[0:1, 0:1], RS[0:1, 0:1]),
        ]

        math_r1a = [
            lambda v: rmul(v, R11[0:1, 0:1], CCR[0:1, 0:1], RS[0:1, 0:1]),
            lambda v: v.tensor_scalar(RS[0:1, 1:2], R11[0:1, 0:1], 1.0 / 32767.0,
                                      1e-12, AluOpType.mult, AluOpType.max),
            lambda v: v.reciprocal(R11[0:1, 0:1], RS[0:1, 1:2]),
            lambda v: rmul(v, CROW[0:1, 0:1], R11[0:1, 0:1], RS[0:1, 0:1]),
            lambda v: v.tensor_copy(CROW[0:1, 1:2], RS[0:1, 1:2]),
            lambda v: v.tensor_scalar(CROW[0:1, 2:3], RS[0:1, 1:2], -MAGIC, None,
                                      AluOpType.mult),
            lambda v: v.tensor_scalar(RS[0:1, 3:4], RS[0:1, 1:2], 1.0 / 65536.0,
                                      None, AluOpType.mult),
        ]

        math_r1b2 = [
            lambda v: rmul(v, R11[0:1, 0:1], CCR[0:1, 0:1], RS[0:1, 0:1]),
            lambda v: v.tensor_scalar(RS[0:1, 2:3], R11[0:1, 0:1], 1.0 / 32767.0,
                                      1e-12, AluOpType.mult, AluOpType.max),
            lambda v: v.reciprocal(R12[0:1, 0:1], RS[0:1, 2:3]),
            lambda v: rmul(v, CROW[0:1, 0:1], R12[0:1, 0:1], RS[0:1, 0:1]),
            lambda v: rmul(v, R11[0:1, 0:1], CCR[0:1, 1:2], RS[0:1, 3:4]),
            lambda v: v.tensor_scalar(RS[0:1, 4:5], R11[0:1, 0:1], 1.0 / 32767.0,
                                      1e-12, AluOpType.mult, AluOpType.max),
            lambda v: v.reciprocal(R11[0:1, 0:1], RS[0:1, 4:5]),
            lambda v: rmul(v, CROW[0:1, 1:2], R11[0:1, 0:1], RS[0:1, 3:4]),
            lambda v: rmul(v, RS[0:1, 5:6], RS[0:1, 4:5], RS[0:1, 2:3]),
        ]

        math_r3 = [
            lambda v: rmul(v, R11[0:1, 0:1], CCR[0:1, 0:1], RS[0:1, 5:6]),
            lambda v: v.tensor_scalar(R12[0:1, 0:1], R11[0:1, 0:1], 1.0 / 32767.0,
                                      1e-12, AluOpType.mult, AluOpType.max),
            lambda v: v.reciprocal(R11[0:1, 0:1], R12[0:1, 0:1]),
            lambda v: rmul(v, CROW[0:1, 0:1], R11[0:1, 0:1], RS[0:1, 5:6]),
            lambda v: v.tensor_scalar(CROW[0:1, 1:2], R12[0:1, 0:1], 0.5, None,
                                      AluOpType.mult),
        ]


        nb = D // 32
        ptb = {"ttr": 0, "cp": 0, "vrd": 0}
        dqs_rd = [0, 0]
        dqs_sp = [None, None]
        swp = {"mm": [0, 0]}
        scr_rd = [0, 0]

        DQS = ["Da", "Db"]

        def dq_load(src_ap, sl):
            dst32 = (SCR if sl == 0 else SCR3)[:, D:2 * D]
            return load(dst32, src_ap, sem=DQS[sl],
                        gate=[("sVE", dqs_rd[sl])] if dqs_rd[sl] else None)

        def dequant_tile(v_l, sl):
            dst32 = (SCR if sl == 0 else SCR3)[:, D:2 * D]
            waitp("vector", v_l)
            vi(lambda v, d=dst32: v.tensor_reduce(
                BMX[:, 0:nb], d.rearrange("p (b w) -> p b w", w=32),
                AX, AluOpType.max, apply_absolute_value=True))
            vi(lambda v: v.tensor_scalar(BSC[:, 0:nb], BMX[:, 0:nb], 1.0 / 7.0,
                                         1e-12, AluOpType.mult, AluOpType.max))
            vi(lambda v: v.reciprocal(BRC[:, 0:nb], BSC[:, 0:nb]))
            vi(lambda v, d=dst32: v.tensor_tensor(
                d.rearrange("p (b w) -> p b w", w=32),
                d.rearrange("p (b w) -> p b w", w=32),
                BRC[:, 0:nb].broadcast_to([128, nb, 32]), AluOpType.mult))
            vi(lambda v, d=dst32: v.tensor_scalar(
                d, d, MAGIC, MAGIC, AluOpType.add, AluOpType.subtract))
            c.wait("vector", "sMM", ptb["ttr"])
            v_dq = vi(lambda v, d=dst32, ss=sl: v.scalar_tensor_tensor(
                WQBp[ss][:, 0:D].rearrange("p (b w) -> p b w", w=32),
                d.rearrange("p (b w) -> p b w", w=32), 0.0,
                BSC[:, 0:nb].broadcast_to([128, nb, 32]),
                AluOpType.add, AluOpType.mult))
            dqs_rd[sl] = v_dq
            return v_dq

        def transpose_wqb(v_dq, sl):
            c.wait("tensor", "sVE", v_dq)
            c.wait("tensor", "sSC", ptb["cp"])
            c.wait("tensor", "sVE", ptb["vrd"])
            t_tr = None
            for b in range(ND):
                t_tr = ti(lambda t, bb=b, ss=sl: t.transpose(
                    PTB[:, bb * 128:(bb + 1) * 128],
                    WQBp[ss][:, bb * 128:(bb + 1) * 128], IDB[:]))
            ptb["ttr"] = t_tr
            return t_tr

        def gen_gateup(w_src, wT_dst, stage, spills):
            def tail(it, v_dq):
                sl = it % 2
                transpose_wqb(v_dq, sl)
                c.wait("scalar", "sMM", ptb["ttr"])
                waitp("scalar", dqs_sp[sl])
                s_cp = si(lambda s, ss=sl: s.activation(
                    stage[ss][:, 0:D], PTB[:, :], AF.Copy))
                ptb["cp"] = s_cp
                sp = gstore(wT_dst[it], stage[sl][:, 0:D], sem=GW[sl],
                            gate=[("sSC", s_cp)])
                dqs_sp[sl] = sp
                spills[it] = sp

            vl = [dq_load(w_src[0:128, :], 0), dq_load(w_src[128:256, :], 1)]
            pend = None
            for it in range(NI):
                sl = it % 2
                v_dq = dequant_tile(vl[sl], sl)
                if it + 2 < NI:
                    vl[sl] = dq_load(w_src[(it + 2) * 128:(it + 3) * 128, :], sl)
                if pend is not None:
                    tail(*pend)
                pend = (it, v_dq)
                if it == NI - 1:
                    tail(*pend)
                yield

        def gen_down(stage):
            def wsrc(q):
                dp, ic = divmod(q, 3)
                return wd_h[dp * 128:(dp + 1) * 128, ic * 1024:(ic + 1) * 1024]

            def tail(q, v_dq):
                dp, ic = divmod(q, 3)
                sl = q % 2
                transpose_wqb(v_dq, sl)
                c.wait("scalar", "sMM", ptb["ttr"])
                waitp("scalar", dqs_sp[sl])
                s_cp = si(lambda s, ss=sl: s.activation(
                    stage[ss][:, 0:D], PTB[:, :], AF.Copy))
                ptb["cp"] = s_cp
                for icb in range(ND):
                    it = ic * 8 + icb
                    dqs_sp[sl] = gstore(wdT_h[it, :, dp * 128:(dp + 1) * 128],
                                        stage[sl][:, icb * 128:(icb + 1) * 128],
                                        sem=GW[sl],
                                        gate=[("sSC", s_cp)] if icb == 0 else None)

            vl = [dq_load(wsrc(0), 0), dq_load(wsrc(1), 1)]
            pend = None
            for q in range(24):
                sl = q % 2
                v_dq = dequant_tile(vl[sl], sl)
                if q + 2 < 24:
                    vl[sl] = dq_load(wsrc(q + 2), sl)
                if pend is not None:
                    tail(*pend)
                pend = (q, v_dq)
                if q == 23:
                    tail(*pend)
                yield

        ln_gate = [[], []]
        ln_drv = [0]

        def ln_stats(src_res_h, with_max):
            gate0 = [("Sa", c.val("Sa")), ("Sb", c.val("Sb"))]
            c.wait("scalar", "sMM", c.val("sMM"))
            c.wait("scalar", "sVE", scr_rd[0])
            rdA = [0, 0]
            sqA = [0, 0]
            for t in range(NT):
                sl = t % 2
                dst = SCR2[:, sl * D:(sl + 1) * D]
                gate = (list(gate0) if t < 2
                        else [("sVE", rdA[sl]), ("sSC", sqA[sl])])
                v_l = load(dst, src_res_h[t * 128:(t + 1) * 128, :], sem=LS[sl],
                           gate=gate)
                waitp("vector", v_l)
                c.wait("vector", "sSC", sqA[sl])
                v_r = vi(lambda v, d=dst, tt=t: v.tensor_reduce(
                    S_SUM[:, tt:tt + 1], d, AX, AluOpType.add))
                if with_max:
                    vi(lambda v, d=dst, tt=t: v.tensor_reduce(
                        S_MX[:, tt:tt + 1], d, AX, AluOpType.max))
                    v_r = vi(lambda v, d=dst, tt=t: v.tensor_reduce(
                        S_MN[:, tt:tt + 1], d, AX, AluOpType.min))
                rdA[sl] = v_r
                waitp("scalar", v_l)
                sqA[sl] = si(lambda s, d=dst, tt=t: s.activation(
                    SCR[:, 0:D], d, AF.Square, accum_out=S_SQ[:, tt:tt + 1]))
            ln_gate[0] = [("sVE", rdA[0]), ("sSC", sqA[0])]
            ln_gate[1] = [("sVE", rdA[1]), ("sSC", sqA[1])]
            c.wait("vector", "sSC", sqA[0])
            c.wait("vector", "sSC", sqA[1])
            vi(lambda v: v.tensor_scalar(S_SUM[:, 0:NT], S_SUM[:, 0:NT], 1.0 / D,
                                         None, AluOpType.mult))
            vi(lambda v: v.tensor_scalar(S_SQ[:, 0:NT], S_SQ[:, 0:NT], 1.0 / D,
                                         None, AluOpType.mult))
            vi(lambda v: v.tensor_tensor(S_M2[:, 0:NT], S_SUM[:, 0:NT],
                                         S_SUM[:, 0:NT], AluOpType.mult))
            v_var = vi(lambda v: v.tensor_tensor(S_SQ[:, 0:NT], S_SQ[:, 0:NT],
                                                 S_M2[:, 0:NT], AluOpType.subtract))
            c.wait("scalar", "sVE", v_var)
            s_sd = si(lambda s: s.activation(S_SD[:, 0:NT], S_SQ[:, 0:NT], AF.Sqrt,
                                             bias=EPSP[:]))
            c.wait("vector", "sSC", s_sd)
            vi(lambda v: v.reciprocal(S_RS[:, 0:NT], S_SD[:, 0:NT]))
            ln_drv[0] = vi(lambda v: v.scalar_tensor_tensor(
                S_NB[:, 0:NT], S_SUM[:, 0:NT], -1.0, S_RS[:, 0:NT],
                AluOpType.mult, AluOpType.mult))
            if with_max:
                vi(lambda v: v.tensor_tensor(S_MX[:, 0:NT], S_MX[:, 0:NT],
                                             S_SUM[:, 0:NT], AluOpType.subtract))
                vi(lambda v: v.tensor_tensor(S_MN[:, 0:NT], S_SUM[:, 0:NT],
                                             S_MN[:, 0:NT], AluOpType.subtract))
                vi(lambda v: v.tensor_tensor(S_MX[:, 0:NT], S_MX[:, 0:NT],
                                             S_MN[:, 0:NT], AluOpType.max))
                vi(lambda v: v.tensor_tensor(S_MX[:, 0:NT], S_MX[:, 0:NT],
                                             S_RS[:, 0:NT], AluOpType.mult))
                vi(lambda v: v.tensor_reduce(GMAX[:, 0:1], S_MX[:, 0:NT], AX,
                                             AluOpType.max))

        def ln_apply(src_res_h, pump=None):
            xln_done = [0, 0]
            ln_ttr = 0
            for t in range(NT):
                sl = t % 2
                dst = SCR2[:, sl * D:(sl + 1) * D]
                gate = (list(ln_gate[sl]) if t < 2
                        else [("sSC", xln_done[sl])])
                v_l = load(dst, src_res_h[t * 128:(t + 1) * 128, :], sem=LS[sl],
                           gate=gate)
                waitp("scalar", v_l)
                if t == 0:
                    c.wait("scalar", "sVE", ln_drv[0])
                c.wait("scalar", "sMM", ln_ttr)
                s_ln = si(lambda s, d=dst, tt=t: s.activation(
                    XL[:, :], d, AF.Identity, scale=S_RS[:, tt:tt + 1],
                    bias=S_NB[:, tt:tt + 1]))
                xln_done[sl] = s_ln
                for half in range(2):
                    slot = ring_pos[0] % RING
                    ring_pos[0] += 1
                    c.wait("tensor", "sSC", ring_free[slot][0])
                    c.wait("tensor", "sVE", ring_free[slot][1])
                    if half == 0:
                        c.wait("tensor", "sSC", s_ln)
                    t_tr = None
                    for b in range(4):
                        t_tr = ti(lambda tt2, bb=b, hh=half, slo=slot: tt2.transpose(
                            PS[:, slo * 512 + bb * 128: slo * 512 + (bb + 1) * 128],
                            XL[:, (hh * 4 + bb) * 128:(hh * 4 + bb + 1) * 128],
                            IDF[:]))
                    ln_ttr = t_tr
                    c.wait("scalar", "sMM", t_tr)
                    s_cp = si(lambda s, ttt=t, hh=half, slo=slot: s.activation(
                        XT[:].rearrange("p (cc t2) -> p cc t2", t2=T)
                        [:, hh * 4:(hh + 1) * 4, ttt * 128:(ttt + 1) * 128],
                        PS[:, slo * 512:(slo + 1) * 512]
                        .rearrange("p (cc j) -> p cc j", j=128), AF.Copy))
                    ring_free[slot] = [s_cp, 0]
                if pump is not None:
                    pump(t)

        def ln_pass(src_res_h, with_max, pump=None):
            ln_stats(src_res_h, with_max)
            ln_apply(src_res_h, pump=pump)

        def quantize_xt():
            for cb in range(ND):
                sl = cb % 2
                scr = SCR if sl == 0 else SCR3
                c.wait("scalar", "sVE", scr_rd[sl])
                sA = si(lambda s, cbb=cb, d=scr: s.activation(
                    d[:, 0:T], XT[:, cbb * T:(cbb + 1) * T], AF.Identity,
                    scale=CONST[:, 0:1], bias=MAGP[:]))
                c.wait("vector", "sSC", sA)
                scr_rd[sl] = vi(lambda v, cbb=cb, d=scr: v.tensor_scalar(
                    XT[:, cbb * T:(cbb + 1) * T], d[:, 0:T], MAGIC, None,
                    AluOpType.subtract))

        def mm_sweep(wT_h, is_gate, chain_fn=None, bar=True, spills=None):
            col = 0 if is_gate else 1
            if bar:
                barrier_t()
            gw = [("Gwa", c.val("Gwa")), ("Gwb", c.val("Gwb"))]
            up_spill = [("Ga", c.val("Ga")), ("Gb", c.val("Gb"))]
            up_red = [0, 0]
            v_loads = [None] * NI
            mm_done = swp["mm"]

            def lgate(j, base):
                if spills is not None and spills[j] is not None:
                    return base + [spills[j]]
                return base

            v_loads[0] = load(WTS[0], wT_h[0], sem="La",
                              gate=lgate(0, gw + [("sMM", mm_done[0])]))
            v_loads[1] = load(WTS[1], wT_h[1], sem="Lb",
                              gate=lgate(1, gw + [("sMM", mm_done[1])]))
            ntc = T // 512
            for it in range(NI):
                sl = it % 2
                waitp("tensor", v_loads[it])
                if is_gate:
                    dstv = G[:, it * T:(it + 1) * T]
                else:
                    dstv = UPSTG[it % 2]
                last_mm = None
                pair_mm = None
                last_se = None
                slots = []
                for tc in range(ntc):
                    slot = ring_pos[0] % RING
                    ring_pos[0] += 1
                    c.wait("tensor", "sSC", ring_free[slot][0])
                    c.wait("tensor", "sVE", ring_free[slot][1])
                    slots.append(slot)
                    for cb in range(ND):
                        is_last = (cb == ND - 1)
                        fn = (lambda t, sll=sl, cbb=cb, tcc=tc, slo=slot,
                              st=(cb == 0), sp=is_last:
                              t.matmul(PS[:, slo * 512:(slo + 1) * 512],
                                       WTS[sll][:, cbb * 128:(cbb + 1) * 128],
                                       XT[:, cbb * T + tcc * 512: cbb * T + (tcc + 1) * 512],
                                       start=st, stop=sp))
                        if is_last:
                            last_mm = ti(fn)
                        else:
                            c.on("tensor", fn)
                    if tc % 2 == 1 or tc == ntc - 1:
                        nb = 2 if tc % 2 == 1 else 1
                        c.wait("scalar", "sMM", last_mm)
                        if tc < 2 and not is_gate:
                            c.wait("scalar", up_spill[sl][0], up_spill[sl][1])
                            c.wait("scalar", "sVE", up_red[sl])
                            if it < 2:
                                for nm, vv in gw:
                                    c.wait("scalar", nm, vv)
                        last_se = si(lambda s, dd=dstv, tcc=tc, nbb=nb,
                                     slo=slots[tc - nb + 1]: s.activation(
                                         dd[:, (tcc - nbb + 1) * 512:(tcc + 1) * 512],
                                         PS[:, slo * 512:(slo + nbb) * 512], AF.Copy))
                        for j in range(nb):
                            ring_free[slots[tc - j]] = [last_se, 0]
                c.wait("vector", "sSC", last_se)
                v_r1 = vi(lambda v, dd=dstv: v.tensor_reduce(
                    TMPC[:], dd[:, 0:T], AX, AluOpType.max, apply_absolute_value=True))
                up_red[sl] = v_r1
                vi(lambda v, cc=col: v.tensor_tensor(
                    GMAX[:, cc:cc + 1], GMAX[:, cc:cc + 1], TMPC[:], AluOpType.max))
                mm_done[sl] = last_mm
                if not is_gate:
                    up_spill[sl] = gstore(ups_h[it], UPSTG[sl], sem=GS[sl],
                                          gate=[("sSC", last_se)])
                if it + 2 < NI:
                    v_loads[it + 2] = load(WTS[sl], wT_h[it + 2], sem=LS[sl],
                                           gate=lgate(it + 2, [("sMM", mm_done[sl])]))
                if chain_fn is not None:
                    chain_fn(it)

        sigst = {"kub": [0, 0]}

        def sig_it(it):
            sl = it % 2
            scr = SCR if sl == 0 else SCR3
            kb = UPLD[sl]
            gslice = G[:, it * T:(it + 1) * T]
            c.wait("scalar", "sVE", scr_rd[sl])
            sA = si(lambda s, gs=gslice, d=scr: s.activation(
                d[:, 0:T], gs, AF.Identity, scale=CONST[:, 1:2], bias=MAGP[:]))
            c.wait("scalar", "sVE", sigst["kub"][sl])
            sB = si(lambda s, d=scr, k=kb: s.activation(
                k[:, 0:T], d[:, 0:T], AF.Sigmoid, scale=CONST[:, 2:3],
                bias=CONST[:, 3:4]))
            c.wait("vector", "sSC", sA)
            v_kg = vi(lambda v, gs=gslice, d=scr: v.tensor_scalar(
                gs, d[:, 0:T], MAGIC, None, AluOpType.subtract))
            c.wait("vector", "sSC", sB)
            vi(lambda v, d=scr, k=kb: v.tensor_scalar(
                d[:, 0:T], k[:, 0:T], 65536.0, MAGIC,
                AluOpType.mult, AluOpType.add))
            v_sq = vi(lambda v, d=scr, k=kb: v.tensor_scalar(
                k[:, 0:T], d[:, 0:T], MAGIC, 65535.0,
                AluOpType.subtract, AluOpType.min))
            scr_rd[sl] = v_sq
            v_tt = vi(lambda v, gs=gslice, k=kb: v.tensor_tensor(
                gs, gs, k[:, 0:T], AluOpType.mult))
            sigst["kub"][sl] = v_tt
            vi(lambda v, gs=gslice: v.tensor_reduce(
                TMPC[:], gs, AX, AluOpType.max, apply_absolute_value=True))
            vi(lambda v: v.tensor_tensor(GMAX[:, 2:3], GMAX[:, 2:3], TMPC[:],
                                         AluOpType.max))

        def rint_g(col):
            for it in range(NI):
                sl = it % 2
                scr = SCR if sl == 0 else SCR3
                gslice = G[:, it * T:(it + 1) * T]
                c.wait("scalar", "sVE", scr_rd[sl])
                sA = si(lambda s, gs=gslice, d=scr, cc=col: s.activation(
                    d[:, 0:T], gs, AF.Identity, scale=CONST[:, cc:cc + 1], bias=MAGP[:]))
                c.wait("vector", "sSC", sA)
                scr_rd[sl] = vi(lambda v, gs=gslice, d=scr: v.tensor_scalar(
                    gs, d[:, 0:T], MAGIC, None, AluOpType.subtract))

        def rint_chunk(col, tc, cw, it_lo=0, it_hi=None):
            last = 0
            for it in range(it_lo, NI if it_hi is None else it_hi):
                sl = it % 2
                scr = SCR if sl == 0 else SCR3
                gs = G[:, it * T + tc * cw: it * T + (tc + 1) * cw]
                c.wait("scalar", "sVE", scr_rd[sl])
                sA = si(lambda s, g=gs, d=scr, cc=col, w=cw: s.activation(
                    d[:, 0:w], g, AF.Identity, scale=CONST[:, cc:cc + 1], bias=MAGP[:]))
                c.wait("vector", "sSC", sA)
                last = vi(lambda v, g=gs, d=scr, w=cw: v.tensor_scalar(
                    g, d[:, 0:w], MAGIC, None, AluOpType.subtract))
                scr_rd[sl] = last
            return last

        def o_phase():
            gate0 = [("Ga", c.val("Ga")), ("Gb", c.val("Gb"))]
            v_loads = [None] * NI
            sgate = [("sSC", c.val("sSC")), ("sVE", c.val("sVE"))]
            v_loads[0] = load(UPLD[0], ups_h[0], sem="La", gate=gate0 + sgate)
            v_loads[1] = load(UPLD[1], ups_h[1], sem="Lb", gate=sgate)
            ost = {}

            def oA(it):
                sl = it % 2
                scr = SCR if sl == 0 else SCR3
                kb = KUBp[sl]
                waitp("scalar", v_loads[it])
                c.wait("scalar", "sVE", scr_rd[sl])
                if it < 2:
                    for nm, vv in gate0:
                        c.wait("scalar", nm, vv)
                sA = si(lambda s, d=scr, ss=sl: s.activation(
                    d[:, 0:T], UPLD[ss][:, 0:T], AF.Identity, scale=CONST[:, 4:5],
                    bias=MAGP[:]))
                s_ku = si(lambda s, d=scr, k=kb: s.activation(
                    k[:, 0:T], d[:, 0:T], AF.Identity, bias=NMAGP[:]))
                ost[it] = s_ku
                if it + 2 < NI:
                    v_loads[it + 2] = load(UPLD[sl], ups_h[it + 2], sem=LS[sl],
                                           gate=[("sSC", sA)])

            def oB(it):
                sl = it % 2
                scr = SCR if sl == 0 else SCR3
                kb = KUBp[sl]
                gslice = G[:, it * T:(it + 1) * T]
                sA5 = si(lambda s, d=scr, gs=gslice: s.activation(
                    d[:, 0:T], gs, AF.Identity, scale=CONST[:, 5:6], bias=MAGP[:]))
                c.wait("vector", "sSC", sA5)
                v_tt = vi(lambda v, gs=gslice, d=scr, k=kb: v.scalar_tensor_tensor(
                    gs, d[:, 0:T], -MAGIC, k[:, 0:T],
                    AluOpType.add, AluOpType.mult))
                scr_rd[sl] = v_tt
                vi(lambda v, gs=gslice: v.tensor_reduce(
                    TMPC[:], gs, AX, AluOpType.max, apply_absolute_value=True))
                vi(lambda v: v.tensor_tensor(GMAX[:, 0:1], GMAX[:, 0:1], TMPC[:],
                                             AluOpType.max))

            for it in range(NI):
                oA(it)
                if it >= 1:
                    oB(it - 1)
            oB(NI - 1)

        def down_phase(src_res_h, dst_res_h, xstore, rint_col=None):
            barrier_t()
            c.wait("vector", "sSC", c.val("sSC"))
            wds_gate = [("sVE", c.val("sVE")), ("sSC", c.val("sSC")),
                        ("Gwa", c.val("Gwa")), ("Gwb", c.val("Gwb"))]
            ntc2 = max(1, T // 512)
            tpc = NT // ntc2
            ipt = min(NI, 2 * ((NI + tpc - 1) // tpc))
            rc_done = [0] * (ntc2 + 1)
            if rint_col is not None:
                rc_done[0] = rint_chunk(rint_col, 0, min(512, T), 0, NI)
            for dc in range(2):
                v_wl = 0
                for it in range(NI):
                    g = wds_gate if (it == 0) else None
                    if dc == 1 and it == 0:
                        g = [("sMM", c.val("sMM"))]
                    v_wl = load(WDS[:, it * 512:(it + 1) * 512],
                                wdT_h[it, :, dc * 512:(dc + 1) * 512], sem="Lw", gate=g)
                gate0 = [("Sa", c.val("Sa")), ("Sb", c.val("Sb")),
                         ("sSC", c.val("sSC")), ("sVE", c.val("sVE"))]
                for tt in range(NT):
                    sl = tt % 2
                    tc2 = tt // tpc
                    xg = list(gate0) if tt < 2 else [xstore[sl]]
                    v_x = load(SCR2[:, sl * D + dc * 512: sl * D + (dc + 1) * 512],
                               src_res_h[tt * 128:(tt + 1) * 128, dc * 512:(dc + 1) * 512],
                               sem=LS[sl], gate=xg)
                    slot = ring_pos[0] % RING
                    ring_pos[0] += 1
                    c.wait("tensor", "sSC", ring_free[slot][0])
                    c.wait("tensor", "sVE", ring_free[slot][1])
                    if tt == 0:
                        waitp("tensor", v_wl)
                    if dc == 0 and rint_col is not None and tt % tpc == 0:
                        c.wait("tensor", "sVE", rc_done[tc2])
                    last_mm = None
                    for it in range(NI):
                        is_last = (it == NI - 1)
                        fn = (lambda t, itt=it, ttt=tt, slo=slot, st=(it == 0), sp=is_last:
                              t.matmul(PS[:, slo * 512:(slo + 1) * 512],
                                       G[:, itt * T + ttt * 128: itt * T + (ttt + 1) * 128],
                                       WDS[:, itt * 512:(itt + 1) * 512],
                                       start=st, stop=sp))
                        if is_last:
                            last_mm = ti(fn)
                        else:
                            c.on("tensor", fn)
                    c.wait("vector", "sMM", last_mm)
                    waitp("vector", v_x)
                    v_e = vi(lambda v, slo=slot, sll=sl, dcc=dc:
                             v.scalar_tensor_tensor(
                                 SCR2[:, sll * D + dcc * 512: sll * D + (dcc + 1) * 512],
                                 PS[:, slo * 512:(slo + 1) * 512], CONST[:, 7:8],
                                 SCR2[:, sll * D + dcc * 512: sll * D + (dcc + 1) * 512],
                                 AluOpType.mult, AluOpType.add))
                    ring_free[slot] = [ring_free[slot][0], v_e]
                    c.wait("scalar", "sVE", v_e)
                    xstore[sl] = sstore(
                        dst_res_h[tt * 128:(tt + 1) * 128, dc * 512:(dc + 1) * 512],
                        SCR2[:, sl * D + dc * 512: sl * D + (dc + 1) * 512], sem=SS[sl])
                    if dc == 0 and rint_col is not None and tc2 + 1 < ntc2:
                        i0 = (tt % tpc) * ipt
                        i1 = min(NI, i0 + ipt)
                        if i0 < i1:
                            rc_done[tc2 + 1] = rint_chunk(rint_col, tc2 + 1, 512,
                                                          i0, i1)

        def attention(xstore):
            ln_pass(xr_h, False)
            wgate = [("sSC", c.val("sSC")), ("sVE", c.val("sVE"))]
            c.wait("sync", wgate[0][0], wgate[0][1])
            c.wait("sync", wgate[1][0], wgate[1][1])
            v1 = load_cast(wq_h[:, :].rearrange("(cc p) k -> p cc k", p=128),
                           WQKV[:, 0:1024], 1024, 0)
            v2 = load_cast(wk_h[:, :].rearrange("(cc p) k -> p cc k", p=128),
                           WQKV[:, 1024:2048], 1024, 1)
            c.wait("sync", "sSC", v1)
            v3 = load_cast(wv_h[:, :].rearrange("(cc p) k -> p cc k", p=128),
                           WQKV[:, 2048:3072], 1024, 0)
            c.wait("sync", "sSC", v2)
            v_lwo = load(SCR2[:, D:2 * D], wo_h[:, :], sem="Lb")
            waitp("scalar", v_lwo)
            s_wo = si(lambda s: s.activation(WOB[:, 0:D], SCR2[:, D:2 * D], AF.Copy))
            barrier_t()
            for w_i, dstv in ((0, QTv), (1, KTv)):
                for tc in range(T // 512):
                    slot = ring_pos[0] % RING
                    ring_pos[0] += 1
                    c.wait("tensor", "sSC", ring_free[slot][0])
                    c.wait("tensor", "sVE", ring_free[slot][1])
                    last_mm = None
                    for cb in range(ND):
                        is_last = (cb == ND - 1)
                        fn = (lambda t, wi=w_i, cbb=cb, tcc=tc, slo=slot,
                              st=(cb == 0), sp=is_last:
                              t.matmul(PS[:, slo * 512:(slo + 1) * 512],
                                       WQKV[:, (wi * ND + cbb) * 128:(wi * ND + cbb + 1) * 128],
                                       XT[:, cbb * T + tcc * 512: cbb * T + (tcc + 1) * 512],
                                       start=st, stop=sp))
                        if is_last:
                            last_mm = ti(fn)
                        else:
                            c.on("tensor", fn)
                    c.wait("scalar", "sMM", last_mm)
                    s_e = si(lambda s, d=dstv, tcc=tc, slo=slot: s.activation(
                        d[:, tcc * 512:(tcc + 1) * 512],
                        PS[:, slo * 512:(slo + 1) * 512], AF.Copy))
                    ring_free[slot] = [s_e, ring_free[slot][1]]
            for tt in range(NT):
                slot = ring_pos[0] % RING
                ring_pos[0] += 1
                c.wait("tensor", "sSC", ring_free[slot][0])
                c.wait("tensor", "sVE", ring_free[slot][1])
                last_mm = None
                for cb in range(ND):
                    is_last = (cb == ND - 1)
                    fn = (lambda t, cbb=cb, ttt=tt, slo=slot, st=(cb == 0), sp=is_last:
                          t.matmul(PS[:, slo * 512: slo * 512 + 128],
                                   XT[:, cbb * T + ttt * 128: cbb * T + (ttt + 1) * 128],
                                   WQKV[:, (2 * ND + cbb) * 128:(2 * ND + cbb + 1) * 128],
                                   start=st, stop=sp))
                    if is_last:
                        last_mm = ti(fn)
                    else:
                        c.on("tensor", fn)
                c.wait("scalar", "sMM", last_mm)
                s_e = si(lambda s, ttt=tt, slo=slot: s.activation(
                    VBv[:, ttt * 128:(ttt + 1) * 128],
                    PS[:, slo * 512: slo * 512 + 128], AF.Copy))
                ring_free[slot] = [s_e, ring_free[slot][1]]
            c.wait("tensor", "sSC", c.val("sSC"))
            c.wait("tensor", "sVE", c.val("sVE"))
            gate0 = [("Sa", c.val("Sa")), ("Sb", c.val("Sb")),
                     ("sSC", c.val("sSC")), ("sVE", c.val("sVE"))]
            ABp = [ABv, ABv2]
            scrA_rd = [scr_rd[0], scr_rd[1]]
            st = {}
            pv = {"s_at": 0, "ot": 0, "y": 0, "ltr": {}}
            atk_pos = [0]

            def stageA(qt):
                L = (qt + 1) * 128
                par = qt % 2
                scr = SCR if par == 0 else SCR3
                v_x = load(SCR2[:, par * D:(par + 1) * D],
                           xr_h[qt * 128:(qt + 1) * 128, :], sem=LS[par],
                           gate=list(gate0) if qt < 2 else [xstore[par]])
                nkb = (L + 511) // 512
                dkb = (qt * 128) // 512
                doff = qt * 128 - dkb * 512
                c.wait("scalar", "sVE", scrA_rd[par])
                last_sev = 0
                for kb in range(nkb):
                    w = min(512, L - kb * 512)
                    slot = atk_pos[0] % 5
                    atk_pos[0] += 1
                    c.wait("tensor", "sSC", ring_free[slot][0])
                    c.wait("tensor", "sVE", ring_free[slot][1])
                    mm = ti(lambda t, qtt=qt, kbb=kb, ww=w, slo=slot:
                            t.matmul(PS[:, slo * 512: slo * 512 + ww],
                                     QTv[:, qtt * 128:(qtt + 1) * 128],
                                     KTv[:, kbb * 512: kbb * 512 + ww],
                                     start=True, stop=True))
                    if kb == dkb:
                        c.wait("vector", "sMM", mm)
                        v_m = vi(lambda v, slo=slot, off=doff: v.tensor_tensor(
                            PS[:, slo * 512 + off: slo * 512 + off + 128],
                            PS[:, slo * 512 + off: slo * 512 + off + 128],
                            MASKD[:], AluOpType.add))
                    if kb % 2 == 0:
                        if kb == dkb:
                            c.wait("scalar", "sVE", v_m)
                        else:
                            c.wait("scalar", "sMM", mm)
                        s_ev = si(lambda s, d=scr, kbb=kb, ww=w, slo=slot:
                                  s.activation(d[:, kbb * 512: kbb * 512 + ww],
                                               PS[:, slo * 512: slo * 512 + ww],
                                               AF.Copy))
                        ring_free[slot] = [s_ev, 0]
                        c.wait("vector", "sSC", s_ev)
                        last_sev = s_ev
                    else:
                        if kb != dkb:
                            c.wait("vector", "sMM", mm)
                        v_ev = vi(lambda v, d=scr, kbb=kb, ww=w, slo=slot:
                                  v.tensor_copy(d[:, kbb * 512: kbb * 512 + ww],
                                                PS[:, slo * 512: slo * 512 + ww]))
                        ring_free[slot] = [0, v_ev]
                    vi(lambda v, d=scr, kbb=kb, ww=w, pp=par: v.tensor_reduce(
                        MROWB[:, pp * 4 + kbb: pp * 4 + kbb + 1],
                        d[:, kbb * 512: kbb * 512 + ww], AX, AluOpType.max))
                v_cmb = vi(lambda v, pp=par, n=nkb: v.tensor_reduce(
                    MROW1[:, pp:pp + 1], MROWB[:, pp * 4: pp * 4 + n], AX,
                    AluOpType.max))
                v_brow = vi(lambda v, pp=par: v.tensor_scalar(
                    BROWB[:, pp:pp + 1], MROW1[:, pp:pp + 1], -CINV, None,
                    AluOpType.mult))
                scrA_rd[par] = v_cmb
                st[qt] = {"vx": v_x, "brow": v_brow, "sev": last_sev}

            def stageB(qt):
                L = (qt + 1) * 128
                par = qt % 2
                scr = SCR if par == 0 else SCR3
                ab = ABp[par]
                nblk = qt + 1
                c.wait("scalar", "sVE", st[qt]["brow"])
                if qt >= 2:
                    c.wait("scalar", "sVE", st[qt - 2]["recip"])
                    c.wait("scalar", "sMM", pv["ltr"].get(qt - 2, 0))
                Lh = min(L, 1024)
                s_exp = si(lambda s, d=scr, a=ab, LL=Lh, pp=par: s.activation(
                    a[:, 0:LL], d[:, 0:LL], AF.Exp, scale=CINV,
                    bias=BROWB[:, pp:pp + 1], accum_out=RSUMB[:, pp:pp + 1]))
                s_exp2 = None
                if L > 1024:
                    s_exp2 = si(lambda s, d=scr, a=ab, pp=par: s.activation(
                        a[:, 1024:L], d[:, 1024:L], AF.Exp, scale=CINV,
                        bias=BROWB[:, pp:pp + 1], accum_out=RS2B[:, pp:pp + 1]))
                c.wait("vector", "sSC", s_exp2 if s_exp2 else s_exp)
                if s_exp2:
                    vi(lambda v, pp=par: v.tensor_tensor(
                        RSTT[:, pp:pp + 1], RSUMB[:, pp:pp + 1], RS2B[:, pp:pp + 1],
                        AluOpType.add))
                    v_rr = vi(lambda v, qtt=qt, pp=par: v.reciprocal(
                        RRS[:, qtt:qtt + 1], RSTT[:, pp:pp + 1]))
                else:
                    v_rr = vi(lambda v, qtt=qt, pp=par: v.reciprocal(
                        RRS[:, qtt:qtt + 1], RSUMB[:, pp:pp + 1]))
                st[qt]["recip"] = v_rr
                s_at = pv["s_at"]
                t_tr = 0
                last_o = None
                c.wait("tensor", "sSC", s_exp)
                for grp in range(0, nblk, 8):
                    gn = min(8, nblk - grp)
                    if s_exp2 and grp >= 8:
                        c.wait("tensor", "sSC", s_exp2)
                    c.wait("tensor", "sVE", s_at)
                    if qt == 0:
                        c.wait("tensor", "sSC", ptb["cp"])
                    for j in range(gn):
                        t_tr = ti(lambda t, a=ab, kk=grp + j, jj=j: t.transpose(
                            PTB[:, jj * 128:(jj + 1) * 128],
                            a[:, kk * 128:(kk + 1) * 128], IDB[:]))
                    c.wait("vector", "sMM", t_tr)
                    s_at = vi(lambda v, g0=grp, gnn=gn: v.tensor_copy(
                        ATv[:, g0 * 128:(g0 + gnn) * 128], PTB[:, 0:gnn * 128]))
                    c.wait("tensor", "sVE", s_at)
                    if grp == 0:
                        c.wait("tensor", "sSC", pv["ot"])
                    for kc in range(grp, grp + gn):
                        fn = (lambda t, kcc=kc, stt=(kc == 0),
                              sp=(kc == nblk - 1):
                              t.matmul(PS[:, 2560:2688],
                                       VBv[:, kcc * 128:(kcc + 1) * 128],
                                       ATv[:, kcc * 128:(kcc + 1) * 128],
                                       start=stt, stop=sp))
                        if kc == nblk - 1:
                            last_o = ti(fn)
                        else:
                            c.on("tensor", fn)
                pv["s_at"] = s_at
                pv["ltr"][qt] = t_tr
                c.wait("scalar", "sMM", last_o)
                s_ot = si(lambda s: s.activation(OTv[:, :], PS[:, 2560:2688], AF.Copy))
                pv["ot"] = s_ot
                c.wait("tensor", "sSC", s_ot)
                ycols = []
                for dc2 in range(2):
                    if dc2 == 0:
                        c.wait("tensor", "sVE", pv["y"])
                        if qt == 0:
                            c.wait("tensor", "sSC", exch_prev["s1"])
                        ycol = 3072
                    else:
                        slot = atk_pos[0] % 5
                        atk_pos[0] += 1
                        c.wait("tensor", "sSC", ring_free[slot][0])
                        c.wait("tensor", "sVE", ring_free[slot][1])
                        ycol = slot * 512
                    ycols.append(ycol)
                    t_y = ti(lambda t, d2=dc2, yc=ycol: t.matmul(
                        PS[:, yc:yc + 512], OTv[:, :], WOB[:, d2 * 512:(d2 + 1) * 512],
                        start=True, stop=True))
                    c.wait("vector", "sMM", t_y)
                    if dc2 == 0:
                        waitp("vector", st[qt]["vx"])
                    pv["y"] = vi(lambda v, qtt=qt, d2=dc2, pp=par, yc=ycol:
                                 v.scalar_tensor_tensor(
                                     SCR2[:, pp * D + d2 * 512: pp * D + (d2 + 1) * 512],
                                     PS[:, yc:yc + 512], RRS[:, qtt:qtt + 1],
                                     SCR2[:, pp * D + d2 * 512: pp * D + (d2 + 1) * 512],
                                     AluOpType.mult, AluOpType.add))
                    if dc2 == 1:
                        ring_free[slot] = [0, pv["y"]]
                c.wait("scalar", "sVE", pv["y"])
                exch_prev["y"] = pv["y"]
                xstore[par] = sstore(xr_h[qt * 128:(qt + 1) * 128, :],
                                     SCR2[:, par * D:(par + 1) * D], sem=SS[par])

            for qt in range(NT):
                stageA(qt)
                if qt >= 1:
                    stageB(qt - 1)
            stageB(NT - 1)
            ptb["vrd"] = pv["s_at"]
            scr_rd[0] = scrA_rd[0]
            scr_rd[1] = scrA_rd[1]

        wg_sp = [None] * NI
        wu_sp = [None] * NI
        g_wg = gen_gateup(wg_h, wgT_h, WTS, wg_sp)
        g_wd = gen_down(WTS)
        g_wu = gen_gateup(wu_h, wuT_h, UPSTG, wu_sp)
        lngen = itertools.chain(g_wg, g_wd)

        def ln_pump(i):
            for _ in range(3):
                next(lngen, None)

        def gate_chain(it):
            next(g_wu, None)

        if T < 1024:
            g_wu = gen_gateup(wu_h, wuT_h, WTS, wu_sp)
            for _ in lngen:
                pass
            for _ in g_wu:
                pass
            scr_rd[0] = max(scr_rd[0], dqs_rd[0])
            scr_rd[1] = max(scr_rd[1], dqs_rd[1])
            ln_pump = None
            gate_chain = None

        xstore = [("Sa", 0), ("Sb", 0)]

        def ffn(src_res_h, dst_res_h, ln_pump=None, gate_chain=None,
                gate_spills=None, up_spills=None):
            ln_stats(src_res_h, True)
            fin0 = exchange((0, 1), math_r0, 1, 0, defer=True)
            ln_apply(src_res_h, pump=ln_pump)
            scr_rd[0] = max(scr_rd[0], dqs_rd[0])
            scr_rd[1] = max(scr_rd[1], dqs_rd[1])
            fin0()
            quantize_xt()
            dqs_rd[0] = max(scr_rd[0], dqs_rd[0])
            dqs_rd[1] = max(scr_rd[1], dqs_rd[1])
            mm_sweep(wgT_h, True, chain_fn=gate_chain, spills=gate_spills)
            scr_rd[0] = max(scr_rd[0], dqs_rd[0])
            scr_rd[1] = max(scr_rd[1], dqs_rd[1])
            fin1a = exchange((0, 1), math_r1a, 3, 1, defer=True)

            def chain(it):
                if it == 2:
                    fin1a()
                if it >= 2:
                    sig_it(it - 2)

            mm_sweep(wuT_h, False, chain_fn=chain, bar=False, spills=up_spills)
            sig_it(NI - 2)
            sig_it(NI - 1)
            exchange((1, 2), math_r1b2, 2, 4)
            o_phase()
            exchange((0, 1), math_r3, 2, 6)
            down_phase(src_res_h, dst_res_h, xstore, rint_col=6)

        def finish_copy():
            for t in range(NT):
                sl = t % 2
                v_l = load(SCR2[:, sl * D:(sl + 1) * D],
                           x_h[t * 128:(t + 1) * 128, :], sem=LS[sl],
                           gate=[xstore[sl], ("sSC", c.val("sSC")),
                                 ("sVE", c.val("sVE")), ("sMM", c.val("sMM"))])
                waitp("scalar", v_l)
                xstore[sl] = sstore(out_h[t * 128:(t + 1) * 128, :],
                                    SCR2[:, sl * D:(sl + 1) * D], sem=SS[sl])

        if upto in ("deq", "ln1", "mm", "sig", "oph"):
            for _ in lngen:
                pass
            for _ in g_wu:
                pass
            scr_rd[0] = max(scr_rd[0], dqs_rd[0])
            scr_rd[1] = max(scr_rd[1], dqs_rd[1])
            if upto != "deq":
                ln_pass(x_h, True)
                exchange((0, 1), math_r0, 1, 0)
                quantize_xt()
            if upto in ("mm", "sig", "oph"):
                mm_sweep(wgT_h, True)
                fin1a = exchange((0, 1), math_r1a, 3, 1, defer=True)

                def chain(it):
                    if it == 2:
                        fin1a()
                    if it >= 2 and upto != "mm":
                        sig_it(it - 2)

                mm_sweep(wuT_h, False, chain_fn=chain, bar=False)
                if upto != "mm":
                    sig_it(NI - 2)
                    sig_it(NI - 1)
                exchange((1, 2), math_r1b2, 2, 4)
            if upto == "oph":
                o_phase()
                exchange((0, 1), math_r3, 2, 6)
                rint_g(6)
            finish_copy()
        elif upto == "ffn1":
            ffn(x_h, out_h, ln_pump=ln_pump, gate_chain=gate_chain,
                gate_spills=wg_sp, up_spills=wu_sp)
        elif upto == "attn":
            ffn(x_h, xr_h, ln_pump=ln_pump, gate_chain=gate_chain,
                gate_spills=wg_sp, up_spills=wu_sp)
            attention(xstore)
            for t in range(NT):
                sl = t % 2
                v_l = load(SCR2[:, sl * D:(sl + 1) * D],
                           xr_h[t * 128:(t + 1) * 128, :], sem=LS[sl],
                           gate=[xstore[sl], ("sSC", c.val("sSC")), ("sVE", c.val("sVE"))])
                waitp("scalar", v_l)
                xstore[sl] = sstore(out_h[t * 128:(t + 1) * 128, :],
                                    SCR2[:, sl * D:(sl + 1) * D], sem=SS[sl])
        else:
            ffn(x_h, xr_h, ln_pump=ln_pump, gate_chain=gate_chain,
                gate_spills=wg_sp, up_spills=wu_sp)
            attention(xstore)
            ffn(xr_h, out_h)

        c.wait("scalar", "Sa", c.val("Sa"))
        c.wait("scalar", "Sb", c.val("Sb"))
        c.wait("vector", "Sa", c.val("Sa"))
        c.wait("vector", "Sb", c.val("Sb"))

        block = es.enter_context(nc.Block())
        c.emit(block)
    return nc


_NC_CACHE = [None]


def kernel(L0_attn_x, L0_attn_Wq, L0_attn_Wk, L0_attn_Wv, L0_attn_Wo,
           L0_ffn_w_gate, L0_ffn_w_up, L0_ffn_w_down):
    if _NC_CACHE[0] is None:
        _NC_CACHE[0] = build()
    nc = _NC_CACHE[0]
    x = np.ascontiguousarray(np.asarray(L0_attn_x, dtype=np.float32))
    ws = [np.ascontiguousarray(np.asarray(w, dtype=np.float32)) for w in
          (L0_attn_Wq, L0_attn_Wk, L0_attn_Wv, L0_attn_Wo,
           L0_ffn_w_gate, L0_ffn_w_up, L0_ffn_w_down)]
    in_maps = [{"x": x[b], "wq": ws[0], "wk": ws[1], "wv": ws[2], "wo": ws[3],
                "wg": ws[4], "wu": ws[5], "wd": ws[6]} for b in range(8)]
    res = run_bass_kernel_spmd(nc, in_maps, core_ids=list(range(8)))
    return np.stack([res.results[b]["out"] for b in range(8)], axis=0)


# version 21
# speedup vs baseline: 1.0369x; 1.0369x over previous
import math
import contextlib
import itertools
import numpy as np
import concourse.bass as bass
import concourse.mybir as mybir
from concourse.alu_op_type import AluOpType
from concourse.bass_utils import run_bass_kernel_spmd

F32 = mybir.dt.float32
BF16 = mybir.dt.bfloat16
I32 = mybir.dt.int32
AX = mybir.AxisListType.X
AF = mybir.ActivationFunctionType
MAGIC = 12582912.0
CINV = 1.0 / math.sqrt(128.0)
NEG = -1.0e30

T = 2048
D = 1024
II = 3072
DK = 128
NT = T // 128
ND = D // 128
NI = II // 128
RING = 6


class Ctx:
    def __init__(self, nc, es):
        self.nc = nc
        self.streams = {k: [] for k in ("sync", "gpsimd", "vector", "scalar", "tensor")}
        self.sems = {}
        self.cnt = {}
        for name in ("La", "Lb", "Lw", "Da", "Db", "Sa", "Sb", "Ga", "Gb", "Gc", "Gd", "Gi", "Gwa", "Gwb", "CC", "sMM", "sVE", "sSC"):
            self.sems[name] = es.enter_context(nc.semaphore(name))
            self.cnt[name] = 0

    def on(self, eng, fn):
        self.streams[eng].append(fn)

    def bump(self, name, by=1):
        self.cnt[name] += by
        return self.sems[name], self.cnt[name]

    def val(self, name):
        return self.cnt[name]

    def wait(self, eng, name, value):
        if value is None or value <= 0:
            return
        h = self.sems[name]
        self.on(eng, lambda e, hh=h, v=value: e.wait_ge(hh, v))

    def emit(self, block):
        def run(name):
            def body(e):
                for fn in self.streams[name]:
                    fn(e)
            return body
        block.sync(run("sync"))
        block.gpsimd(run("gpsimd"))
        block.vector(run("vector"))
        block.scalar(run("scalar"))
        block.tensor(run("tensor"))


def build(TT=2048, upto="all"):
    T = TT
    NT = T // 128
    nc = bass.Bass("TRN2", target_bir_lowering=False)

    x_h = nc.dram_tensor("x", [T, D], F32, kind="ExternalInput").ap()
    wq_h = nc.dram_tensor("wq", [D, DK], F32, kind="ExternalInput").ap()
    wk_h = nc.dram_tensor("wk", [D, DK], F32, kind="ExternalInput").ap()
    wv_h = nc.dram_tensor("wv", [D, DK], F32, kind="ExternalInput").ap()
    wo_h = nc.dram_tensor("wo", [DK, D], F32, kind="ExternalInput").ap()
    wg_h = nc.dram_tensor("wg", [II, D], F32, kind="ExternalInput").ap()
    wu_h = nc.dram_tensor("wu", [II, D], F32, kind="ExternalInput").ap()
    wd_h = nc.dram_tensor("wd", [D, II], F32, kind="ExternalInput").ap()
    out_h = nc.dram_tensor("out", [T, D], F32, kind="ExternalOutput").ap()

    xr_h = nc.dram_tensor("xres", [T, D], F32, kind="Internal").ap()
    wgT_h = nc.dram_tensor("wgT", [NI, 128, ND * 128], BF16, kind="Internal").ap()
    wuT_h = nc.dram_tensor("wuT", [NI, 128, ND * 128], BF16, kind="Internal").ap()
    wdT_h = nc.dram_tensor("wdT", [NI, 128, D], BF16, kind="Internal").ap()
    ups_h = nc.dram_tensor("ups", [NI, 128, T], BF16, kind="Internal").ap()
    cci_t = nc.dram_tensor("cci", [1, 8], F32, kind="Internal")
    cco_t = nc.dram_tensor("cco", [1, 8], F32, kind="Internal")
    cci = cci_t.ap()
    cco = cco_t.ap()

    es = contextlib.ExitStack()

    def sbuf(name, shape, dt=F32):
        return es.enter_context(nc.sbuf_tensor(name, list(shape), dt))

    SW = max(T, 2 * D)
    G = sbuf("G", (128, NI * T), BF16)
    XT = sbuf("XT", (128, ND * T), BF16)
    ARN = sbuf("ARN", (128, max(16384, 3072 + 8 * T + 1024)), BF16)
    SCR = sbuf("SCR", (128, SW))
    SCR3 = sbuf("SCR3", (128, SW))
    SCR2 = sbuf("SCR2", (128, 2 * D))
    XL = sbuf("XL", (128, D))
    IOT = sbuf("IOT", (128, 128), I32)
    IDF = sbuf("IDF", (128, 128))
    IDB = sbuf("IDB", (128, 128), BF16)
    MASKD = sbuf("MASKD", (128, 128))
    TRS = sbuf("TRS", (128, 128))
    TCOL = sbuf("TCOL", (128, 1))
    CCR = sbuf("CCR", (1, 8))
    RS = sbuf("RS", (1, 16))
    CROW = sbuf("CROW", (1, 8))
    CONST = sbuf("CONST", (128, 16))
    ONE1 = sbuf("ONE1", (1, 128))
    MAGP = sbuf("MAGP", (128, 1))
    EPSP = sbuf("EPSP", (128, 1))
    GMAX = sbuf("GMAX", (128, 4))
    TMPC = sbuf("TMPC", (128, 1))
    BMX = sbuf("BMX", (128, 32))
    BSC = sbuf("BSC", (128, 32))
    BRC = sbuf("BRC", (128, 32))
    ST_SUM = sbuf("ST_SUM", (128, 2))
    ST_SQ = sbuf("ST_SQ", (128, 2))
    ST_MX = sbuf("ST_MX", (128, 2))
    ST_MN = sbuf("ST_MN", (128, 2))
    ST_M2 = sbuf("ST_M2", (128, 2))
    ST_SD = sbuf("ST_SD", (128, 2))
    ST_RS = sbuf("ST_RS", (128, 2))
    ST_NB = sbuf("ST_NB", (128, 2))
    MROW = sbuf("MROW", (128, 1))
    BROW = sbuf("BROW", (128, 1))
    RSUM = sbuf("RSUM", (128, 1))
    S_SUM = sbuf("S_SUM", (128, NT))
    S_SQ = sbuf("S_SQ", (128, NT))
    S_M2 = sbuf("S_M2", (128, NT))
    S_MX = sbuf("S_MX", (128, NT))
    S_MN = sbuf("S_MN", (128, NT))
    S_SD = sbuf("S_SD", (128, NT))
    S_RS = sbuf("S_RS", (128, NT))
    S_NB = sbuf("S_NB", (128, NT))
    MROWB = sbuf("MROWB", (128, 8))
    MROW1 = sbuf("MROW1", (128, 2))
    BROWB = sbuf("BROWB", (128, 2))
    RSUMB = sbuf("RSUMB", (128, 2))
    RS2B = sbuf("RS2B", (128, 2))
    RSTT = sbuf("RSTT", (128, 2))
    RRS = sbuf("RRS", (128, NT))
    R11 = sbuf("R11", (1, 1))
    R12 = sbuf("R12", (1, 1))

    PS = es.enter_context(nc.psum_tensor("PS", [128, 3584], F32))
    PTB = es.enter_context(nc.psum_tensor("PTB", [128, 1024], BF16))

    def arn(a, b):
        return ARN[:, a:b]
    WTS = [arn(0, 1024), arn(1024, 2048)]
    WQB = arn(2048, 3072)
    o = 3072
    UPSTG = [arn(o, o + T), arn(o + T, o + 2 * T)]
    UPLD = [arn(o + 2 * T, o + 3 * T), arn(o + 3 * T, o + 4 * T)]
    KUBp = [UPSTG[0], UPSTG[1]]
    WDS = arn(0, NI * 512)
    QTv = arn(0, T)
    KTv = arn(T, 2 * T)
    VBv = arn(2 * T, 2 * T + NT * 128)
    ABv = arn(2 * T + NT * 128, 3 * T + NT * 128)
    ATv = arn(3 * T + NT * 128, 4 * T + NT * 128)
    o2 = 4 * T + NT * 128
    OTv = arn(o2, o2 + 128)
    WQKV = arn(o2 + 128, o2 + 128 + 3072)
    WOB = arn(o2 + 128 + 3072, o2 + 128 + 4096)
    ABv2 = arn(o2 + 128 + 4096, o2 + 128 + 4096 + T)
    WQB2 = arn(o2 + 128 + 4096 + T, o2 + 128 + 4096 + T + 1024)
    WQBp = [WQB, WQB2]

    c = Ctx(nc, es)
    ring_free = [[0, 0] for _ in range(RING)]
    ring_pos = [0]

    with es:
        SELFGATE = {"La", "Lb", "Da", "Db", "Sa", "Sb", "Ga", "Gb", "Gc", "Gd"}

        def load(dst, src, sem="Lw", gate=None):
            if sem in SELFGATE and c.val(sem) > 0:
                c.wait("sync", sem, c.val(sem))
            if gate:
                for nm, v in gate:
                    c.wait("sync", nm, v)
            h, v = c.bump(sem, 16)
            c.on("sync", lambda e, d=dst, s=src: e.dma_start(d, s).then_inc(h, 16))
            return (sem, v)

        def gstore(dst, src, sem="Ga", gate=None):
            if sem in SELFGATE and c.val(sem) > 0:
                c.wait("gpsimd", sem, c.val(sem))
            if gate:
                for nm, v in gate:
                    c.wait("gpsimd", nm, v)
            h, v = c.bump(sem, 16)
            c.on("gpsimd", lambda e, d=dst, s=src: e.dma_start(d, s).then_inc(h, 16))
            return (sem, v)

        def sstore(dst, src, sem="Sa"):
            if sem in SELFGATE and c.val(sem) > 0:
                c.wait("scalar", sem, c.val(sem))
            h, v = c.bump(sem, 16)
            c.on("scalar", lambda e, d=dst, s=src: e.dma_start(d, s).then_inc(h, 16))
            return (sem, v)

        def waitp(eng, pair):
            if pair and pair[1] > 0:
                c.wait(eng, pair[0], pair[1])

        LS = ["La", "Lb"]
        SS = ["Sa", "Sb"]
        GS = ["Ga", "Gb"]
        GW = ["Gwa", "Gwb"]

        def vi(fn):
            h, v = c.bump("sVE")
            if v > 1:
                c.on("vector", lambda e, hh=h, vv=v - 1: e.wait_ge(hh, vv))
            c.on("vector", lambda e, f=fn, hh=h: f(e).then_inc(hh, 1))
            return v

        def si(fn):
            h, v = c.bump("sSC")
            if v > 1:
                c.on("scalar", lambda e, hh=h, vv=v - 1: e.wait_ge(hh, vv))
            c.on("scalar", lambda e, f=fn, hh=h: f(e).then_inc(hh, 1))
            return v

        def ti(fn):
            h, v = c.bump("sMM")
            c.on("tensor", lambda e, f=fn, hh=h: f(e).then_inc(hh, 1))
            return v

        def barrier_t():
            c.wait("tensor", "sSC", c.val("sSC"))
            c.wait("tensor", "sVE", c.val("sVE"))

        c.on("gpsimd", lambda e: e.iota(IOT[:], pattern=[[1, 128]], base=0,
                                        channel_multiplier=-1).then_inc(c.sems["Gi"], 16))
        c.cnt["Gi"] += 16
        v_iota = c.val("Gi")

        vi(lambda v: v.memset(MAGP[:], MAGIC))
        vi(lambda v: v.memset(EPSP[:], 1e-5))
        vi(lambda v: v.memset(ONE1[:], 1.0))
        vi(lambda v: v.memset(TRS[:], 0.0))
        vi(lambda v: v.memset(GMAX[:], 0.0))
        c.wait("vector", "Gi", v_iota)
        vi(lambda v: v.tensor_scalar(IDF[:], IOT[:], 0.0, None, AluOpType.is_equal))
        vi(lambda v: v.tensor_scalar(IDB[:], IOT[:], 0.0, None, AluOpType.is_equal))
        vi(lambda v: v.tensor_scalar(MASKD[:], IOT[:], 0.0, NEG, AluOpType.is_gt,
                                     AluOpType.mult))

        def load_cast(w_src, dst, ncols, slot):
            v_l = load(SCR2[:, slot * D: slot * D + ncols]
                       .rearrange("p (cc k) -> p cc k", k=128), w_src, sem=LS[slot])
            waitp("scalar", v_l)
            return si(lambda s, d=dst, sl=slot, n=ncols:
                      s.activation(d[:, 0:n], SCR2[:, sl * D: sl * D + n], AF.Copy))


        exch_prev = {"s1": 0, "y": 0}

        def exchange(span, math_fn, nconst, const_col, defer=False):
            c0, K = span
            v1x = vi(lambda v: v.tensor_copy(TRS[:, 0:K], GMAX[:, c0:c0 + K]))
            vi(lambda v: v.memset(GMAX[:, c0:c0 + K], 0.0))
            c.wait("tensor", "sVE", v1x)
            c.wait("tensor", "sSC", exch_prev["s1"])
            c.wait("tensor", "sVE", exch_prev["y"])
            t1 = ti(lambda t: t.transpose(PS[:, 3072:3200], TRS[:], IDF[:]))
            c.wait("vector", "sMM", t1)
            v2x = vi(lambda v: v.tensor_reduce(TCOL[:], PS[:, 3072:3200], AX,
                                               AluOpType.max))
            p_ci = gstore(cci[0:1, 0:8], TCOL[0:8, 0:1], sem="Gd", gate=[("sVE", v2x)])
            waitp("gpsimd", p_ci)
            hG, v_cc = c.bump("CC", 1)
            c.on("gpsimd", lambda e, hh=hG: e.collective_compute(
                "AllReduce", AluOpType.max, replica_groups=[list(range(8))],
                ins=[cci_t.ap().opt()], outs=[cco_t.ap().opt()]).then_inc(hh, 1))
            c.wait("gpsimd", "CC", v_cc)
            hG2, v_co = c.bump("Gc", 16)
            c.on("gpsimd", lambda e, hh=hG2: e.dma_start(CCR[:, :], cco[:, :]).then_inc(hh, 16))
            c.wait("vector", "Gc", v_co)
            v3x = 0
            for op in math_fn:
                v3x = vi(op)

            def fin(v3=v3x, n=nconst, cc=const_col):
                c.wait("tensor", "sVE", v3)
                t2 = ti(lambda t: t.matmul(PS[:, 3200:3200 + n], ONE1[:, :],
                                           CROW[0:1, 0:n], start=True, stop=True))
                c.wait("scalar", "sMM", t2)
                exch_prev["s1"] = si(lambda s:
                                     s.activation(CONST[:, cc:cc + n],
                                                  PS[:, 3200:3200 + n], AF.Copy))
            if defer:
                return fin
            fin()

        def rmul(v, dst, a, b):
            return v.tensor_tensor(dst, a, b, AluOpType.mult)

        math_r0 = [
            lambda v: v.tensor_scalar(RS[0:1, 0:1], CCR[0:1, 0:1], 1.0 / 32767.0,
                                      1e-12, AluOpType.mult, AluOpType.max),
            lambda v: v.reciprocal(CROW[0:1, 0:1], RS[0:1, 0:1]),
        ]

        math_r1a = [
            lambda v: rmul(v, R11[0:1, 0:1], CCR[0:1, 0:1], RS[0:1, 0:1]),
            lambda v: v.tensor_scalar(RS[0:1, 1:2], R11[0:1, 0:1], 1.0 / 32767.0,
                                      1e-12, AluOpType.mult, AluOpType.max),
            lambda v: v.reciprocal(R11[0:1, 0:1], RS[0:1, 1:2]),
            lambda v: rmul(v, CROW[0:1, 0:1], R11[0:1, 0:1], RS[0:1, 0:1]),
            lambda v: v.tensor_copy(CROW[0:1, 1:2], RS[0:1, 1:2]),
            lambda v: v.tensor_scalar(CROW[0:1, 2:3], RS[0:1, 1:2], -MAGIC, None,
                                      AluOpType.mult),
            lambda v: v.tensor_scalar(RS[0:1, 3:4], RS[0:1, 1:2], 1.0 / 65536.0,
                                      None, AluOpType.mult),
        ]

        math_r1b2 = [
            lambda v: rmul(v, R11[0:1, 0:1], CCR[0:1, 0:1], RS[0:1, 0:1]),
            lambda v: v.tensor_scalar(RS[0:1, 2:3], R11[0:1, 0:1], 1.0 / 32767.0,
                                      1e-12, AluOpType.mult, AluOpType.max),
            lambda v: v.reciprocal(R12[0:1, 0:1], RS[0:1, 2:3]),
            lambda v: rmul(v, CROW[0:1, 0:1], R12[0:1, 0:1], RS[0:1, 0:1]),
            lambda v: rmul(v, R11[0:1, 0:1], CCR[0:1, 1:2], RS[0:1, 3:4]),
            lambda v: v.tensor_scalar(RS[0:1, 4:5], R11[0:1, 0:1], 1.0 / 32767.0,
                                      1e-12, AluOpType.mult, AluOpType.max),
            lambda v: v.reciprocal(R11[0:1, 0:1], RS[0:1, 4:5]),
            lambda v: rmul(v, CROW[0:1, 1:2], R11[0:1, 0:1], RS[0:1, 3:4]),
            lambda v: rmul(v, RS[0:1, 5:6], RS[0:1, 4:5], RS[0:1, 2:3]),
        ]

        math_r3 = [
            lambda v: rmul(v, R11[0:1, 0:1], CCR[0:1, 0:1], RS[0:1, 5:6]),
            lambda v: v.tensor_scalar(R12[0:1, 0:1], R11[0:1, 0:1], 1.0 / 32767.0,
                                      1e-12, AluOpType.mult, AluOpType.max),
            lambda v: v.reciprocal(R11[0:1, 0:1], R12[0:1, 0:1]),
            lambda v: rmul(v, CROW[0:1, 0:1], R11[0:1, 0:1], RS[0:1, 5:6]),
            lambda v: v.tensor_scalar(CROW[0:1, 1:2], R12[0:1, 0:1], 0.5, None,
                                      AluOpType.mult),
        ]


        nb = D // 32
        ptb = {"ttr": 0, "cp": 0, "vrd": 0}
        dqs_rd = [0, 0]
        dqs_sp = [None, None]
        swp = {"mm": [0, 0]}
        scr_rd = [0, 0]

        DQS = ["Da", "Db"]

        def dq_load(src_ap, sl):
            dst32 = (SCR if sl == 0 else SCR3)[:, D:2 * D]
            return load(dst32, src_ap, sem=DQS[sl],
                        gate=[("sVE", dqs_rd[sl])] if dqs_rd[sl] else None)

        def dequant_tile(v_l, sl):
            dst32 = (SCR if sl == 0 else SCR3)[:, D:2 * D]
            waitp("vector", v_l)
            vi(lambda v, d=dst32: v.tensor_reduce(
                BMX[:, 0:nb], d.rearrange("p (b w) -> p b w", w=32),
                AX, AluOpType.max, apply_absolute_value=True))
            vi(lambda v: v.tensor_scalar(BSC[:, 0:nb], BMX[:, 0:nb], 1.0 / 7.0,
                                         1e-12, AluOpType.mult, AluOpType.max))
            vi(lambda v: v.reciprocal(BRC[:, 0:nb], BSC[:, 0:nb]))
            vi(lambda v, d=dst32: v.tensor_tensor(
                d.rearrange("p (b w) -> p b w", w=32),
                d.rearrange("p (b w) -> p b w", w=32),
                BRC[:, 0:nb].broadcast_to([128, nb, 32]), AluOpType.mult))
            vi(lambda v, d=dst32: v.tensor_scalar(
                d, d, MAGIC, MAGIC, AluOpType.add, AluOpType.subtract))
            c.wait("vector", "sMM", ptb["ttr"])
            v_dq = vi(lambda v, d=dst32, ss=sl: v.scalar_tensor_tensor(
                WQBp[ss][:, 0:D].rearrange("p (b w) -> p b w", w=32),
                d.rearrange("p (b w) -> p b w", w=32), 0.0,
                BSC[:, 0:nb].broadcast_to([128, nb, 32]),
                AluOpType.add, AluOpType.mult))
            dqs_rd[sl] = v_dq
            return v_dq

        def transpose_wqb(v_dq, sl):
            c.wait("tensor", "sVE", v_dq)
            c.wait("tensor", "sSC", ptb["cp"])
            c.wait("tensor", "sVE", ptb["vrd"])
            t_tr = None
            for b in range(ND):
                t_tr = ti(lambda t, bb=b, ss=sl: t.transpose(
                    PTB[:, bb * 128:(bb + 1) * 128],
                    WQBp[ss][:, bb * 128:(bb + 1) * 128], IDB[:]))
            ptb["ttr"] = t_tr
            return t_tr

        def gen_gateup(w_src, wT_dst, stage, spills):
            def tail(it, v_dq):
                sl = it % 2
                transpose_wqb(v_dq, sl)
                c.wait("scalar", "sMM", ptb["ttr"])
                waitp("scalar", dqs_sp[sl])
                s_cp = si(lambda s, ss=sl: s.activation(
                    stage[ss][:, 0:D], PTB[:, :], AF.Copy))
                ptb["cp"] = s_cp
                sp = gstore(wT_dst[it], stage[sl][:, 0:D], sem=GW[sl],
                            gate=[("sSC", s_cp)])
                dqs_sp[sl] = sp
                spills[it] = sp

            vl = [dq_load(w_src[0:128, :], 0), dq_load(w_src[128:256, :], 1)]
            pend = None
            for it in range(NI):
                sl = it % 2
                v_dq = dequant_tile(vl[sl], sl)
                if it + 2 < NI:
                    vl[sl] = dq_load(w_src[(it + 2) * 128:(it + 3) * 128, :], sl)
                if pend is not None:
                    tail(*pend)
                pend = (it, v_dq)
                if it == NI - 1:
                    tail(*pend)
                yield

        def gen_down(stage):
            def wsrc(q):
                dp, ic = divmod(q, 3)
                return wd_h[dp * 128:(dp + 1) * 128, ic * 1024:(ic + 1) * 1024]

            def tail(q, v_dq):
                dp, ic = divmod(q, 3)
                sl = q % 2
                transpose_wqb(v_dq, sl)
                c.wait("scalar", "sMM", ptb["ttr"])
                waitp("scalar", dqs_sp[sl])
                s_cp = si(lambda s, ss=sl: s.activation(
                    stage[ss][:, 0:D], PTB[:, :], AF.Copy))
                ptb["cp"] = s_cp
                for icb in range(ND):
                    it = ic * 8 + icb
                    dqs_sp[sl] = gstore(wdT_h[it, :, dp * 128:(dp + 1) * 128],
                                        stage[sl][:, icb * 128:(icb + 1) * 128],
                                        sem=GW[sl],
                                        gate=[("sSC", s_cp)] if icb == 0 else None)

            vl = [dq_load(wsrc(0), 0), dq_load(wsrc(1), 1)]
            pend = None
            for q in range(24):
                sl = q % 2
                v_dq = dequant_tile(vl[sl], sl)
                if q + 2 < 24:
                    vl[sl] = dq_load(wsrc(q + 2), sl)
                if pend is not None:
                    tail(*pend)
                pend = (q, v_dq)
                if q == 23:
                    tail(*pend)
                yield

        ln_gate = [[], []]
        ln_drv = [0]

        def ln_stats(src_res_h, with_max):
            gate0 = [("Sa", c.val("Sa")), ("Sb", c.val("Sb"))]
            c.wait("scalar", "sMM", c.val("sMM"))
            c.wait("scalar", "sVE", scr_rd[0])
            rdA = [0, 0]
            sqA = [0, 0]
            for t in range(NT):
                sl = t % 2
                dst = SCR2[:, sl * D:(sl + 1) * D]
                gate = (list(gate0) if t < 2
                        else [("sVE", rdA[sl]), ("sSC", sqA[sl])])
                v_l = load(dst, src_res_h[t * 128:(t + 1) * 128, :], sem=LS[sl],
                           gate=gate)
                waitp("vector", v_l)
                c.wait("vector", "sSC", sqA[sl])
                v_r = vi(lambda v, d=dst, tt=t: v.tensor_reduce(
                    S_SUM[:, tt:tt + 1], d, AX, AluOpType.add))
                if with_max:
                    vi(lambda v, d=dst, tt=t: v.tensor_reduce(
                        S_MX[:, tt:tt + 1], d, AX, AluOpType.max))
                    v_r = vi(lambda v, d=dst, tt=t: v.tensor_reduce(
                        S_MN[:, tt:tt + 1], d, AX, AluOpType.min))
                rdA[sl] = v_r
                waitp("scalar", v_l)
                sqA[sl] = si(lambda s, d=dst, tt=t: s.activation(
                    SCR[:, 0:D], d, AF.Square, accum_out=S_SQ[:, tt:tt + 1]))
            ln_gate[0] = [("sVE", rdA[0]), ("sSC", sqA[0])]
            ln_gate[1] = [("sVE", rdA[1]), ("sSC", sqA[1])]
            c.wait("vector", "sSC", sqA[0])
            c.wait("vector", "sSC", sqA[1])
            vi(lambda v: v.tensor_scalar(S_SUM[:, 0:NT], S_SUM[:, 0:NT], 1.0 / D,
                                         None, AluOpType.mult))
            vi(lambda v: v.tensor_scalar(S_SQ[:, 0:NT], S_SQ[:, 0:NT], 1.0 / D,
                                         None, AluOpType.mult))
            vi(lambda v: v.tensor_tensor(S_M2[:, 0:NT], S_SUM[:, 0:NT],
                                         S_SUM[:, 0:NT], AluOpType.mult))
            v_var = vi(lambda v: v.tensor_tensor(S_SQ[:, 0:NT], S_SQ[:, 0:NT],
                                                 S_M2[:, 0:NT], AluOpType.subtract))
            c.wait("scalar", "sVE", v_var)
            s_sd = si(lambda s: s.activation(S_SD[:, 0:NT], S_SQ[:, 0:NT], AF.Sqrt,
                                             bias=EPSP[:]))
            c.wait("vector", "sSC", s_sd)
            vi(lambda v: v.reciprocal(S_RS[:, 0:NT], S_SD[:, 0:NT]))
            ln_drv[0] = vi(lambda v: v.scalar_tensor_tensor(
                S_NB[:, 0:NT], S_SUM[:, 0:NT], -1.0, S_RS[:, 0:NT],
                AluOpType.mult, AluOpType.mult))
            if with_max:
                vi(lambda v: v.tensor_tensor(S_MX[:, 0:NT], S_MX[:, 0:NT],
                                             S_SUM[:, 0:NT], AluOpType.subtract))
                vi(lambda v: v.tensor_tensor(S_MN[:, 0:NT], S_SUM[:, 0:NT],
                                             S_MN[:, 0:NT], AluOpType.subtract))
                vi(lambda v: v.tensor_tensor(S_MX[:, 0:NT], S_MX[:, 0:NT],
                                             S_MN[:, 0:NT], AluOpType.max))
                vi(lambda v: v.tensor_tensor(S_MX[:, 0:NT], S_MX[:, 0:NT],
                                             S_RS[:, 0:NT], AluOpType.mult))
                vi(lambda v: v.tensor_reduce(GMAX[:, 0:1], S_MX[:, 0:NT], AX,
                                             AluOpType.max))

        def ln_apply(src_res_h, pump=None):
            xln_done = [0, 0]
            ln_ttr = 0
            for t in range(NT):
                sl = t % 2
                dst = SCR2[:, sl * D:(sl + 1) * D]
                gate = (list(ln_gate[sl]) if t < 2
                        else [("sSC", xln_done[sl])])
                v_l = load(dst, src_res_h[t * 128:(t + 1) * 128, :], sem=LS[sl],
                           gate=gate)
                waitp("scalar", v_l)
                if t == 0:
                    c.wait("scalar", "sVE", ln_drv[0])
                c.wait("scalar", "sMM", ln_ttr)
                s_ln = si(lambda s, d=dst, tt=t: s.activation(
                    XL[:, :], d, AF.Identity, scale=S_RS[:, tt:tt + 1],
                    bias=S_NB[:, tt:tt + 1]))
                xln_done[sl] = s_ln
                for half in range(2):
                    slot = ring_pos[0] % RING
                    ring_pos[0] += 1
                    c.wait("tensor", "sSC", ring_free[slot][0])
                    c.wait("tensor", "sVE", ring_free[slot][1])
                    if half == 0:
                        c.wait("tensor", "sSC", s_ln)
                    t_tr = None
                    for b in range(4):
                        t_tr = ti(lambda tt2, bb=b, hh=half, slo=slot: tt2.transpose(
                            PS[:, slo * 512 + bb * 128: slo * 512 + (bb + 1) * 128],
                            XL[:, (hh * 4 + bb) * 128:(hh * 4 + bb + 1) * 128],
                            IDF[:]))
                    ln_ttr = t_tr
                    c.wait("scalar", "sMM", t_tr)
                    s_cp = si(lambda s, ttt=t, hh=half, slo=slot: s.activation(
                        XT[:].rearrange("p (cc t2) -> p cc t2", t2=T)
                        [:, hh * 4:(hh + 1) * 4, ttt * 128:(ttt + 1) * 128],
                        PS[:, slo * 512:(slo + 1) * 512]
                        .rearrange("p (cc j) -> p cc j", j=128), AF.Copy))
                    ring_free[slot] = [s_cp, 0]
                if pump is not None:
                    pump(t)

        def ln_pass(src_res_h, with_max, pump=None):
            ln_stats(src_res_h, with_max)
            ln_apply(src_res_h, pump=pump)

        def quantize_xt():
            for cb in range(ND):
                sl = cb % 2
                scr = SCR if sl == 0 else SCR3
                c.wait("scalar", "sVE", scr_rd[sl])
                sA = si(lambda s, cbb=cb, d=scr: s.activation(
                    d[:, 0:T], XT[:, cbb * T:(cbb + 1) * T], AF.Identity,
                    scale=CONST[:, 0:1], bias=MAGP[:]))
                c.wait("vector", "sSC", sA)
                scr_rd[sl] = vi(lambda v, cbb=cb, d=scr: v.tensor_scalar(
                    XT[:, cbb * T:(cbb + 1) * T], d[:, 0:T], MAGIC, None,
                    AluOpType.subtract))

        def mm_sweep(wT_h, is_gate, chain_fn=None, bar=True, spills=None):
            col = 0 if is_gate else 1
            if bar:
                barrier_t()
            gw = [("Gwa", c.val("Gwa")), ("Gwb", c.val("Gwb"))]
            up_spill = [("Ga", c.val("Ga")), ("Gb", c.val("Gb"))]
            up_red = [0, 0]
            v_loads = [None] * NI
            mm_done = swp["mm"]

            def lgate(j, base):
                if spills is not None and spills[j] is not None:
                    return base + [spills[j]]
                return base

            v_loads[0] = load(WTS[0], wT_h[0], sem="La",
                              gate=lgate(0, gw + [("sMM", mm_done[0])]))
            v_loads[1] = load(WTS[1], wT_h[1], sem="Lb",
                              gate=lgate(1, gw + [("sMM", mm_done[1])]))
            ntc = T // 512
            for it in range(NI):
                sl = it % 2
                waitp("tensor", v_loads[it])
                if is_gate:
                    dstv = G[:, it * T:(it + 1) * T]
                else:
                    dstv = UPSTG[it % 2]
                last_mm = None
                pair_mm = None
                last_se = None
                slots = []
                for tc in range(ntc):
                    slot = ring_pos[0] % RING
                    ring_pos[0] += 1
                    c.wait("tensor", "sSC", ring_free[slot][0])
                    c.wait("tensor", "sVE", ring_free[slot][1])
                    slots.append(slot)
                    for cb in range(ND):
                        is_last = (cb == ND - 1)
                        fn = (lambda t, sll=sl, cbb=cb, tcc=tc, slo=slot,
                              st=(cb == 0), sp=is_last:
                              t.matmul(PS[:, slo * 512:(slo + 1) * 512],
                                       WTS[sll][:, cbb * 128:(cbb + 1) * 128],
                                       XT[:, cbb * T + tcc * 512: cbb * T + (tcc + 1) * 512],
                                       start=st, stop=sp))
                        if is_last:
                            last_mm = ti(fn)
                        else:
                            c.on("tensor", fn)
                    if tc % 2 == 1 or tc == ntc - 1:
                        nb = 2 if tc % 2 == 1 else 1
                        c.wait("scalar", "sMM", last_mm)
                        if tc < 2 and not is_gate:
                            c.wait("scalar", up_spill[sl][0], up_spill[sl][1])
                            c.wait("scalar", "sVE", up_red[sl])
                            if it < 2:
                                for nm, vv in gw:
                                    c.wait("scalar", nm, vv)
                        last_se = si(lambda s, dd=dstv, tcc=tc, nbb=nb,
                                     slo=slots[tc - nb + 1]: s.activation(
                                         dd[:, (tcc - nbb + 1) * 512:(tcc + 1) * 512],
                                         PS[:, slo * 512:(slo + nbb) * 512], AF.Copy))
                        for j in range(nb):
                            ring_free[slots[tc - j]] = [last_se, 0]
                c.wait("vector", "sSC", last_se)
                v_r1 = vi(lambda v, dd=dstv: v.tensor_reduce(
                    TMPC[:], dd[:, 0:T], AX, AluOpType.max, apply_absolute_value=True))
                up_red[sl] = v_r1
                vi(lambda v, cc=col: v.tensor_tensor(
                    GMAX[:, cc:cc + 1], GMAX[:, cc:cc + 1], TMPC[:], AluOpType.max))
                mm_done[sl] = last_mm
                if not is_gate:
                    up_spill[sl] = gstore(ups_h[it], UPSTG[sl], sem=GS[sl],
                                          gate=[("sSC", last_se)])
                if it + 2 < NI:
                    v_loads[it + 2] = load(WTS[sl], wT_h[it + 2], sem=LS[sl],
                                           gate=lgate(it + 2, [("sMM", mm_done[sl])]))
                if chain_fn is not None:
                    chain_fn(it)

        sigst = {"kub": [0, 0]}

        def sig_it(it):
            sl = it % 2
            scr = SCR if sl == 0 else SCR3
            kb = UPLD[sl]
            gslice = G[:, it * T:(it + 1) * T]
            c.wait("scalar", "sVE", scr_rd[sl])
            sA = si(lambda s, gs=gslice, d=scr: s.activation(
                d[:, 0:T], gs, AF.Identity, scale=CONST[:, 1:2], bias=MAGP[:]))
            c.wait("scalar", "sVE", sigst["kub"][sl])
            sB = si(lambda s, d=scr, k=kb: s.activation(
                k[:, 0:T], d[:, 0:T], AF.Sigmoid, scale=CONST[:, 2:3],
                bias=CONST[:, 3:4]))
            c.wait("vector", "sSC", sA)
            v_kg = vi(lambda v, gs=gslice, d=scr: v.tensor_scalar(
                gs, d[:, 0:T], MAGIC, None, AluOpType.subtract))
            c.wait("vector", "sSC", sB)
            vi(lambda v, d=scr, k=kb: v.tensor_scalar(
                d[:, 0:T], k[:, 0:T], 65536.0, MAGIC,
                AluOpType.mult, AluOpType.add))
            v_sq = vi(lambda v, d=scr, k=kb: v.tensor_scalar(
                k[:, 0:T], d[:, 0:T], MAGIC, 65535.0,
                AluOpType.subtract, AluOpType.min))
            scr_rd[sl] = v_sq
            v_tt = vi(lambda v, gs=gslice, k=kb: v.tensor_tensor(
                gs, gs, k[:, 0:T], AluOpType.mult))
            sigst["kub"][sl] = v_tt
            vi(lambda v, gs=gslice: v.tensor_reduce(
                TMPC[:], gs, AX, AluOpType.max, apply_absolute_value=True))
            vi(lambda v: v.tensor_tensor(GMAX[:, 2:3], GMAX[:, 2:3], TMPC[:],
                                         AluOpType.max))

        def rint_g(col):
            for it in range(NI):
                sl = it % 2
                scr = SCR if sl == 0 else SCR3
                gslice = G[:, it * T:(it + 1) * T]
                c.wait("scalar", "sVE", scr_rd[sl])
                sA = si(lambda s, gs=gslice, d=scr, cc=col: s.activation(
                    d[:, 0:T], gs, AF.Identity, scale=CONST[:, cc:cc + 1], bias=MAGP[:]))
                c.wait("vector", "sSC", sA)
                scr_rd[sl] = vi(lambda v, gs=gslice, d=scr: v.tensor_scalar(
                    gs, d[:, 0:T], MAGIC, None, AluOpType.subtract))

        def rint_chunk(col, tc, cw, it_lo=0, it_hi=None):
            last = 0
            for it in range(it_lo, NI if it_hi is None else it_hi):
                sl = it % 2
                scr = SCR if sl == 0 else SCR3
                gs = G[:, it * T + tc * cw: it * T + (tc + 1) * cw]
                c.wait("scalar", "sVE", scr_rd[sl])
                sA = si(lambda s, g=gs, d=scr, cc=col, w=cw: s.activation(
                    d[:, 0:w], g, AF.Identity, scale=CONST[:, cc:cc + 1], bias=MAGP[:]))
                c.wait("vector", "sSC", sA)
                last = vi(lambda v, g=gs, d=scr, w=cw: v.tensor_scalar(
                    g, d[:, 0:w], MAGIC, None, AluOpType.subtract))
                scr_rd[sl] = last
            return last

        def o_phase():
            gate0 = [("Ga", c.val("Ga")), ("Gb", c.val("Gb"))]
            v_loads = [None] * NI
            sgate = [("sSC", c.val("sSC")), ("sVE", c.val("sVE"))]
            v_loads[0] = load(UPLD[0], ups_h[0], sem="La", gate=gate0 + sgate)
            v_loads[1] = load(UPLD[1], ups_h[1], sem="Lb", gate=sgate)
            ost = {}

            def oA(it):
                sl = it % 2
                scr = SCR if sl == 0 else SCR3
                kb = KUBp[sl]
                waitp("scalar", v_loads[it])
                c.wait("scalar", "sVE", scr_rd[sl])
                if it < 2:
                    for nm, vv in gate0:
                        c.wait("scalar", nm, vv)
                sA = si(lambda s, d=scr, ss=sl: s.activation(
                    d[:, 0:T], UPLD[ss][:, 0:T], AF.Identity, scale=CONST[:, 4:5],
                    bias=MAGP[:]))
                c.wait("vector", "sSC", sA)
                v_ku = vi(lambda v, d=scr, k=kb: v.tensor_scalar(
                    k[:, 0:T], d[:, 0:T], MAGIC, None, AluOpType.subtract))
                ost[it] = v_ku
                if it + 2 < NI:
                    v_loads[it + 2] = load(UPLD[sl], ups_h[it + 2], sem=LS[sl],
                                           gate=[("sSC", sA)])

            def oB(it):
                sl = it % 2
                scr = SCR if sl == 0 else SCR3
                kb = KUBp[sl]
                gslice = G[:, it * T:(it + 1) * T]
                c.wait("scalar", "sVE", ost[it])
                sA5 = si(lambda s, d=scr, gs=gslice: s.activation(
                    d[:, 0:T], gs, AF.Identity, scale=CONST[:, 5:6], bias=MAGP[:]))
                c.wait("vector", "sSC", sA5)
                v_tt = vi(lambda v, gs=gslice, d=scr, k=kb: v.scalar_tensor_tensor(
                    gs, d[:, 0:T], -MAGIC, k[:, 0:T],
                    AluOpType.add, AluOpType.mult))
                scr_rd[sl] = v_tt
                vi(lambda v, gs=gslice: v.tensor_reduce(
                    TMPC[:], gs, AX, AluOpType.max, apply_absolute_value=True))
                vi(lambda v: v.tensor_tensor(GMAX[:, 0:1], GMAX[:, 0:1], TMPC[:],
                                             AluOpType.max))

            for it in range(NI):
                oA(it)
                if it >= 1:
                    oB(it - 1)
            oB(NI - 1)

        def down_phase(src_res_h, dst_res_h, xstore, rint_col=None):
            barrier_t()
            c.wait("vector", "sSC", c.val("sSC"))
            wds_gate = [("sVE", c.val("sVE")), ("sSC", c.val("sSC")),
                        ("Gwa", c.val("Gwa")), ("Gwb", c.val("Gwb"))]
            ntc2 = max(1, T // 512)
            tpc = NT // ntc2
            ipt = min(NI, 2 * ((NI + tpc - 1) // tpc))
            rc_done = [0] * (ntc2 + 1)
            if rint_col is not None:
                rc_done[0] = rint_chunk(rint_col, 0, min(512, T), 0, NI)
            for dc in range(2):
                v_wl = 0
                for it in range(NI):
                    g = wds_gate if (it == 0) else None
                    if dc == 1 and it == 0:
                        g = [("sMM", c.val("sMM"))]
                    v_wl = load(WDS[:, it * 512:(it + 1) * 512],
                                wdT_h[it, :, dc * 512:(dc + 1) * 512], sem="Lw", gate=g)
                gate0 = [("Sa", c.val("Sa")), ("Sb", c.val("Sb")),
                         ("sSC", c.val("sSC")), ("sVE", c.val("sVE"))]
                for tt in range(NT):
                    sl = tt % 2
                    tc2 = tt // tpc
                    xg = list(gate0) if tt < 2 else [xstore[sl]]
                    v_x = load(SCR2[:, sl * D + dc * 512: sl * D + (dc + 1) * 512],
                               src_res_h[tt * 128:(tt + 1) * 128, dc * 512:(dc + 1) * 512],
                               sem=LS[sl], gate=xg)
                    slot = ring_pos[0] % RING
                    ring_pos[0] += 1
                    c.wait("tensor", "sSC", ring_free[slot][0])
                    c.wait("tensor", "sVE", ring_free[slot][1])
                    if tt == 0:
                        waitp("tensor", v_wl)
                    if dc == 0 and rint_col is not None and tt % tpc == 0:
                        c.wait("tensor", "sVE", rc_done[tc2])
                    last_mm = None
                    for it in range(NI):
                        is_last = (it == NI - 1)
                        fn = (lambda t, itt=it, ttt=tt, slo=slot, st=(it == 0), sp=is_last:
                              t.matmul(PS[:, slo * 512:(slo + 1) * 512],
                                       G[:, itt * T + ttt * 128: itt * T + (ttt + 1) * 128],
                                       WDS[:, itt * 512:(itt + 1) * 512],
                                       start=st, stop=sp))
                        if is_last:
                            last_mm = ti(fn)
                        else:
                            c.on("tensor", fn)
                    c.wait("vector", "sMM", last_mm)
                    waitp("vector", v_x)
                    v_e = vi(lambda v, slo=slot, sll=sl, dcc=dc:
                             v.scalar_tensor_tensor(
                                 SCR2[:, sll * D + dcc * 512: sll * D + (dcc + 1) * 512],
                                 PS[:, slo * 512:(slo + 1) * 512], CONST[:, 7:8],
                                 SCR2[:, sll * D + dcc * 512: sll * D + (dcc + 1) * 512],
                                 AluOpType.mult, AluOpType.add))
                    ring_free[slot] = [ring_free[slot][0], v_e]
                    c.wait("scalar", "sVE", v_e)
                    xstore[sl] = sstore(
                        dst_res_h[tt * 128:(tt + 1) * 128, dc * 512:(dc + 1) * 512],
                        SCR2[:, sl * D + dc * 512: sl * D + (dc + 1) * 512], sem=SS[sl])
                    if dc == 0 and rint_col is not None and tc2 + 1 < ntc2:
                        i0 = (tt % tpc) * ipt
                        i1 = min(NI, i0 + ipt)
                        if i0 < i1:
                            rc_done[tc2 + 1] = rint_chunk(rint_col, tc2 + 1, 512,
                                                          i0, i1)

        def attention(xstore):
            ln_pass(xr_h, False)
            wgate = [("sSC", c.val("sSC")), ("sVE", c.val("sVE"))]
            c.wait("sync", wgate[0][0], wgate[0][1])
            c.wait("sync", wgate[1][0], wgate[1][1])
            v1 = load_cast(wq_h[:, :].rearrange("(cc p) k -> p cc k", p=128),
                           WQKV[:, 0:1024], 1024, 0)
            v2 = load_cast(wk_h[:, :].rearrange("(cc p) k -> p cc k", p=128),
                           WQKV[:, 1024:2048], 1024, 1)
            c.wait("sync", "sSC", v1)
            v3 = load_cast(wv_h[:, :].rearrange("(cc p) k -> p cc k", p=128),
                           WQKV[:, 2048:3072], 1024, 0)
            c.wait("sync", "sSC", v2)
            v_lwo = load(SCR2[:, D:2 * D], wo_h[:, :], sem="Lb")
            waitp("scalar", v_lwo)
            s_wo = si(lambda s: s.activation(WOB[:, 0:D], SCR2[:, D:2 * D], AF.Copy))
            barrier_t()
            for w_i, dstv in ((0, QTv), (1, KTv)):
                for tc in range(T // 512):
                    slot = ring_pos[0] % RING
                    ring_pos[0] += 1
                    c.wait("tensor", "sSC", ring_free[slot][0])
                    c.wait("tensor", "sVE", ring_free[slot][1])
                    last_mm = None
                    for cb in range(ND):
                        is_last = (cb == ND - 1)
                        fn = (lambda t, wi=w_i, cbb=cb, tcc=tc, slo=slot,
                              st=(cb == 0), sp=is_last:
                              t.matmul(PS[:, slo * 512:(slo + 1) * 512],
                                       WQKV[:, (wi * ND + cbb) * 128:(wi * ND + cbb + 1) * 128],
                                       XT[:, cbb * T + tcc * 512: cbb * T + (tcc + 1) * 512],
                                       start=st, stop=sp))
                        if is_last:
                            last_mm = ti(fn)
                        else:
                            c.on("tensor", fn)
                    c.wait("scalar", "sMM", last_mm)
                    s_e = si(lambda s, d=dstv, tcc=tc, slo=slot: s.activation(
                        d[:, tcc * 512:(tcc + 1) * 512],
                        PS[:, slo * 512:(slo + 1) * 512], AF.Copy))
                    ring_free[slot] = [s_e, ring_free[slot][1]]
            for tt in range(NT):
                slot = ring_pos[0] % RING
                ring_pos[0] += 1
                c.wait("tensor", "sSC", ring_free[slot][0])
                c.wait("tensor", "sVE", ring_free[slot][1])
                last_mm = None
                for cb in range(ND):
                    is_last = (cb == ND - 1)
                    fn = (lambda t, cbb=cb, ttt=tt, slo=slot, st=(cb == 0), sp=is_last:
                          t.matmul(PS[:, slo * 512: slo * 512 + 128],
                                   XT[:, cbb * T + ttt * 128: cbb * T + (ttt + 1) * 128],
                                   WQKV[:, (2 * ND + cbb) * 128:(2 * ND + cbb + 1) * 128],
                                   start=st, stop=sp))
                    if is_last:
                        last_mm = ti(fn)
                    else:
                        c.on("tensor", fn)
                c.wait("scalar", "sMM", last_mm)
                s_e = si(lambda s, ttt=tt, slo=slot: s.activation(
                    VBv[:, ttt * 128:(ttt + 1) * 128],
                    PS[:, slo * 512: slo * 512 + 128], AF.Copy))
                ring_free[slot] = [s_e, ring_free[slot][1]]
            c.wait("tensor", "sSC", c.val("sSC"))
            c.wait("tensor", "sVE", c.val("sVE"))
            gate0 = [("Sa", c.val("Sa")), ("Sb", c.val("Sb")),
                     ("sSC", c.val("sSC")), ("sVE", c.val("sVE"))]
            ABp = [ABv, ABv2]
            scrA_rd = [scr_rd[0], scr_rd[1]]
            st = {}
            pv = {"s_at": 0, "ot": 0, "y": 0, "ltr": {}}
            atk_pos = [0]

            def stageA(qt):
                L = (qt + 1) * 128
                par = qt % 2
                scr = SCR if par == 0 else SCR3
                v_x = load(SCR2[:, par * D:(par + 1) * D],
                           xr_h[qt * 128:(qt + 1) * 128, :], sem=LS[par],
                           gate=list(gate0) if qt < 2 else [xstore[par]])
                nkb = (L + 511) // 512
                dkb = (qt * 128) // 512
                doff = qt * 128 - dkb * 512
                c.wait("scalar", "sVE", scrA_rd[par])
                last_sev = 0
                for kb in range(nkb):
                    w = min(512, L - kb * 512)
                    slot = atk_pos[0] % 5
                    atk_pos[0] += 1
                    c.wait("tensor", "sSC", ring_free[slot][0])
                    c.wait("tensor", "sVE", ring_free[slot][1])
                    mm = ti(lambda t, qtt=qt, kbb=kb, ww=w, slo=slot:
                            t.matmul(PS[:, slo * 512: slo * 512 + ww],
                                     QTv[:, qtt * 128:(qtt + 1) * 128],
                                     KTv[:, kbb * 512: kbb * 512 + ww],
                                     start=True, stop=True))
                    if kb == dkb:
                        c.wait("vector", "sMM", mm)
                        v_m = vi(lambda v, slo=slot, off=doff: v.tensor_tensor(
                            PS[:, slo * 512 + off: slo * 512 + off + 128],
                            PS[:, slo * 512 + off: slo * 512 + off + 128],
                            MASKD[:], AluOpType.add))
                    if kb % 2 == 0:
                        if kb == dkb:
                            c.wait("scalar", "sVE", v_m)
                        else:
                            c.wait("scalar", "sMM", mm)
                        s_ev = si(lambda s, d=scr, kbb=kb, ww=w, slo=slot:
                                  s.activation(d[:, kbb * 512: kbb * 512 + ww],
                                               PS[:, slo * 512: slo * 512 + ww],
                                               AF.Copy))
                        ring_free[slot] = [s_ev, 0]
                        c.wait("vector", "sSC", s_ev)
                        last_sev = s_ev
                    else:
                        if kb != dkb:
                            c.wait("vector", "sMM", mm)
                        v_ev = vi(lambda v, d=scr, kbb=kb, ww=w, slo=slot:
                                  v.tensor_copy(d[:, kbb * 512: kbb * 512 + ww],
                                                PS[:, slo * 512: slo * 512 + ww]))
                        ring_free[slot] = [0, v_ev]
                    vi(lambda v, d=scr, kbb=kb, ww=w, pp=par: v.tensor_reduce(
                        MROWB[:, pp * 4 + kbb: pp * 4 + kbb + 1],
                        d[:, kbb * 512: kbb * 512 + ww], AX, AluOpType.max))
                v_cmb = vi(lambda v, pp=par, n=nkb: v.tensor_reduce(
                    MROW1[:, pp:pp + 1], MROWB[:, pp * 4: pp * 4 + n], AX,
                    AluOpType.max))
                v_brow = vi(lambda v, pp=par: v.tensor_scalar(
                    BROWB[:, pp:pp + 1], MROW1[:, pp:pp + 1], -CINV, None,
                    AluOpType.mult))
                scrA_rd[par] = v_cmb
                st[qt] = {"vx": v_x, "brow": v_brow, "sev": last_sev}

            def stageB(qt):
                L = (qt + 1) * 128
                par = qt % 2
                scr = SCR if par == 0 else SCR3
                ab = ABp[par]
                nblk = qt + 1
                c.wait("scalar", "sVE", st[qt]["brow"])
                if qt >= 2:
                    c.wait("scalar", "sVE", st[qt - 2]["recip"])
                    c.wait("scalar", "sMM", pv["ltr"].get(qt - 2, 0))
                Lh = min(L, 1024)
                s_exp = si(lambda s, d=scr, a=ab, LL=Lh, pp=par: s.activation(
                    a[:, 0:LL], d[:, 0:LL], AF.Exp, scale=CINV,
                    bias=BROWB[:, pp:pp + 1], accum_out=RSUMB[:, pp:pp + 1]))
                s_exp2 = None
                if L > 1024:
                    s_exp2 = si(lambda s, d=scr, a=ab, pp=par: s.activation(
                        a[:, 1024:L], d[:, 1024:L], AF.Exp, scale=CINV,
                        bias=BROWB[:, pp:pp + 1], accum_out=RS2B[:, pp:pp + 1]))
                c.wait("vector", "sSC", s_exp2 if s_exp2 else s_exp)
                if s_exp2:
                    vi(lambda v, pp=par: v.tensor_tensor(
                        RSTT[:, pp:pp + 1], RSUMB[:, pp:pp + 1], RS2B[:, pp:pp + 1],
                        AluOpType.add))
                    v_rr = vi(lambda v, qtt=qt, pp=par: v.reciprocal(
                        RRS[:, qtt:qtt + 1], RSTT[:, pp:pp + 1]))
                else:
                    v_rr = vi(lambda v, qtt=qt, pp=par: v.reciprocal(
                        RRS[:, qtt:qtt + 1], RSUMB[:, pp:pp + 1]))
                st[qt]["recip"] = v_rr
                s_at = pv["s_at"]
                t_tr = 0
                last_o = None
                c.wait("tensor", "sSC", s_exp)
                for grp in range(0, nblk, 8):
                    gn = min(8, nblk - grp)
                    if s_exp2 and grp >= 8:
                        c.wait("tensor", "sSC", s_exp2)
                    c.wait("tensor", "sVE", s_at)
                    if qt == 0:
                        c.wait("tensor", "sSC", ptb["cp"])
                    for j in range(gn):
                        t_tr = ti(lambda t, a=ab, kk=grp + j, jj=j: t.transpose(
                            PTB[:, jj * 128:(jj + 1) * 128],
                            a[:, kk * 128:(kk + 1) * 128], IDB[:]))
                    c.wait("vector", "sMM", t_tr)
                    s_at = vi(lambda v, g0=grp, gnn=gn: v.tensor_copy(
                        ATv[:, g0 * 128:(g0 + gnn) * 128], PTB[:, 0:gnn * 128]))
                    c.wait("tensor", "sVE", s_at)
                    if grp == 0:
                        c.wait("tensor", "sVE", pv["ot"])
                    for kc in range(grp, grp + gn):
                        fn = (lambda t, kcc=kc, stt=(kc == 0),
                              sp=(kc == nblk - 1):
                              t.matmul(PS[:, 2560:2688],
                                       VBv[:, kcc * 128:(kcc + 1) * 128],
                                       ATv[:, kcc * 128:(kcc + 1) * 128],
                                       start=stt, stop=sp))
                        if kc == nblk - 1:
                            last_o = ti(fn)
                        else:
                            c.on("tensor", fn)
                pv["s_at"] = s_at
                pv["ltr"][qt] = t_tr
                c.wait("vector", "sMM", last_o)
                s_ot = vi(lambda v: v.tensor_copy(OTv[:, :], PS[:, 2560:2688]))
                pv["ot"] = s_ot
                c.wait("tensor", "sVE", s_ot)
                ycols = []
                for dc2 in range(2):
                    if dc2 == 0:
                        c.wait("tensor", "sVE", pv["y"])
                        if qt == 0:
                            c.wait("tensor", "sSC", exch_prev["s1"])
                        ycol = 3072
                    else:
                        slot = atk_pos[0] % 5
                        atk_pos[0] += 1
                        c.wait("tensor", "sSC", ring_free[slot][0])
                        c.wait("tensor", "sVE", ring_free[slot][1])
                        ycol = slot * 512
                    ycols.append(ycol)
                    t_y = ti(lambda t, d2=dc2, yc=ycol: t.matmul(
                        PS[:, yc:yc + 512], OTv[:, :], WOB[:, d2 * 512:(d2 + 1) * 512],
                        start=True, stop=True))
                    c.wait("vector", "sMM", t_y)
                    if dc2 == 0:
                        waitp("vector", st[qt]["vx"])
                    pv["y"] = vi(lambda v, qtt=qt, d2=dc2, pp=par, yc=ycol:
                                 v.scalar_tensor_tensor(
                                     SCR2[:, pp * D + d2 * 512: pp * D + (d2 + 1) * 512],
                                     PS[:, yc:yc + 512], RRS[:, qtt:qtt + 1],
                                     SCR2[:, pp * D + d2 * 512: pp * D + (d2 + 1) * 512],
                                     AluOpType.mult, AluOpType.add))
                    if dc2 == 1:
                        ring_free[slot] = [0, pv["y"]]
                c.wait("scalar", "sVE", pv["y"])
                exch_prev["y"] = pv["y"]
                xstore[par] = sstore(xr_h[qt * 128:(qt + 1) * 128, :],
                                     SCR2[:, par * D:(par + 1) * D], sem=SS[par])

            for qt in range(NT):
                stageA(qt)
                if qt >= 1:
                    stageB(qt - 1)
            stageB(NT - 1)
            ptb["vrd"] = pv["s_at"]
            scr_rd[0] = scrA_rd[0]
            scr_rd[1] = scrA_rd[1]

        wg_sp = [None] * NI
        wu_sp = [None] * NI
        g_wg = gen_gateup(wg_h, wgT_h, WTS, wg_sp)
        g_wd = gen_down(WTS)
        g_wu = gen_gateup(wu_h, wuT_h, UPSTG, wu_sp)
        lngen = itertools.chain(g_wg, g_wd)

        def ln_pump(i):
            for _ in range(3):
                next(lngen, None)

        def gate_chain(it):
            next(g_wu, None)

        if T < 1024:
            g_wu = gen_gateup(wu_h, wuT_h, WTS, wu_sp)
            for _ in lngen:
                pass
            for _ in g_wu:
                pass
            scr_rd[0] = max(scr_rd[0], dqs_rd[0])
            scr_rd[1] = max(scr_rd[1], dqs_rd[1])
            ln_pump = None
            gate_chain = None

        xstore = [("Sa", 0), ("Sb", 0)]

        def ffn(src_res_h, dst_res_h, ln_pump=None, gate_chain=None,
                gate_spills=None, up_spills=None):
            ln_stats(src_res_h, True)
            fin0 = exchange((0, 1), math_r0, 1, 0, defer=True)
            ln_apply(src_res_h, pump=ln_pump)
            scr_rd[0] = max(scr_rd[0], dqs_rd[0])
            scr_rd[1] = max(scr_rd[1], dqs_rd[1])
            fin0()
            quantize_xt()
            dqs_rd[0] = max(scr_rd[0], dqs_rd[0])
            dqs_rd[1] = max(scr_rd[1], dqs_rd[1])
            mm_sweep(wgT_h, True, chain_fn=gate_chain, spills=gate_spills)
            scr_rd[0] = max(scr_rd[0], dqs_rd[0])
            scr_rd[1] = max(scr_rd[1], dqs_rd[1])
            fin1a = exchange((0, 1), math_r1a, 3, 1, defer=True)

            def chain(it):
                if it == 1:
                    fin1a()
                if it >= 1:
                    sig_it(it - 1)

            mm_sweep(wuT_h, False, chain_fn=chain, bar=False, spills=up_spills)
            sig_it(NI - 1)
            exchange((1, 2), math_r1b2, 2, 4)
            o_phase()
            exchange((0, 1), math_r3, 2, 6)
            down_phase(src_res_h, dst_res_h, xstore, rint_col=6)

        def finish_copy():
            for t in range(NT):
                sl = t % 2
                v_l = load(SCR2[:, sl * D:(sl + 1) * D],
                           x_h[t * 128:(t + 1) * 128, :], sem=LS[sl],
                           gate=[xstore[sl], ("sSC", c.val("sSC")),
                                 ("sVE", c.val("sVE")), ("sMM", c.val("sMM"))])
                waitp("scalar", v_l)
                xstore[sl] = sstore(out_h[t * 128:(t + 1) * 128, :],
                                    SCR2[:, sl * D:(sl + 1) * D], sem=SS[sl])

        if upto in ("deq", "ln1", "mm", "sig", "oph"):
            for _ in lngen:
                pass
            for _ in g_wu:
                pass
            scr_rd[0] = max(scr_rd[0], dqs_rd[0])
            scr_rd[1] = max(scr_rd[1], dqs_rd[1])
            if upto != "deq":
                ln_pass(x_h, True)
                exchange((0, 1), math_r0, 1, 0)
                quantize_xt()
            if upto in ("mm", "sig", "oph"):
                mm_sweep(wgT_h, True)
                fin1a = exchange((0, 1), math_r1a, 3, 1, defer=True)

                def chain(it):
                    if it == 1:
                        fin1a()
                    if it >= 1 and upto != "mm":
                        sig_it(it - 1)

                mm_sweep(wuT_h, False, chain_fn=chain, bar=False)
                if upto != "mm":
                    sig_it(NI - 1)
                exchange((1, 2), math_r1b2, 2, 4)
            if upto == "oph":
                o_phase()
                exchange((0, 1), math_r3, 2, 6)
                rint_g(6)
            finish_copy()
        elif upto == "ffn1":
            ffn(x_h, out_h, ln_pump=ln_pump, gate_chain=gate_chain,
                gate_spills=wg_sp, up_spills=wu_sp)
        elif upto == "attn":
            ffn(x_h, xr_h, ln_pump=ln_pump, gate_chain=gate_chain,
                gate_spills=wg_sp, up_spills=wu_sp)
            attention(xstore)
            for t in range(NT):
                sl = t % 2
                v_l = load(SCR2[:, sl * D:(sl + 1) * D],
                           xr_h[t * 128:(t + 1) * 128, :], sem=LS[sl],
                           gate=[xstore[sl], ("sSC", c.val("sSC")), ("sVE", c.val("sVE"))])
                waitp("scalar", v_l)
                xstore[sl] = sstore(out_h[t * 128:(t + 1) * 128, :],
                                    SCR2[:, sl * D:(sl + 1) * D], sem=SS[sl])
        else:
            ffn(x_h, xr_h, ln_pump=ln_pump, gate_chain=gate_chain,
                gate_spills=wg_sp, up_spills=wu_sp)
            attention(xstore)
            ffn(xr_h, out_h)

        c.wait("scalar", "Sa", c.val("Sa"))
        c.wait("scalar", "Sb", c.val("Sb"))
        c.wait("vector", "Sa", c.val("Sa"))
        c.wait("vector", "Sb", c.val("Sb"))

        block = es.enter_context(nc.Block())
        c.emit(block)
    return nc


_NC_CACHE = [None]


def kernel(L0_attn_x, L0_attn_Wq, L0_attn_Wk, L0_attn_Wv, L0_attn_Wo,
           L0_ffn_w_gate, L0_ffn_w_up, L0_ffn_w_down):
    if _NC_CACHE[0] is None:
        _NC_CACHE[0] = build()
    nc = _NC_CACHE[0]
    x = np.ascontiguousarray(np.asarray(L0_attn_x, dtype=np.float32))
    ws = [np.ascontiguousarray(np.asarray(w, dtype=np.float32)) for w in
          (L0_attn_Wq, L0_attn_Wk, L0_attn_Wv, L0_attn_Wo,
           L0_ffn_w_gate, L0_ffn_w_up, L0_ffn_w_down)]
    in_maps = [{"x": x[b], "wq": ws[0], "wk": ws[1], "wv": ws[2], "wo": ws[3],
                "wg": ws[4], "wu": ws[5], "wd": ws[6]} for b in range(8)]
    res = run_bass_kernel_spmd(nc, in_maps, core_ids=list(range(8)))
    return np.stack([res.results[b]["out"] for b in range(8)], axis=0)


# version 22
# speedup vs baseline: 1.0473x; 1.0101x over previous
import math
import contextlib
import itertools
import numpy as np
import concourse.bass as bass
import concourse.mybir as mybir
from concourse.alu_op_type import AluOpType
from concourse.bass_utils import run_bass_kernel_spmd

F32 = mybir.dt.float32
BF16 = mybir.dt.bfloat16
I32 = mybir.dt.int32
AX = mybir.AxisListType.X
AF = mybir.ActivationFunctionType
MAGIC = 12582912.0
CINV = 1.0 / math.sqrt(128.0)
NEG = -1.0e30

T = 2048
D = 1024
II = 3072
DK = 128
NT = T // 128
ND = D // 128
NI = II // 128
RING = 6


class Ctx:
    def __init__(self, nc, es):
        self.nc = nc
        self.streams = {k: [] for k in ("sync", "gpsimd", "vector", "scalar", "tensor")}
        self.sems = {}
        self.cnt = {}
        for name in ("La", "Lb", "Lw", "Da", "Db", "Sa", "Sb", "Ga", "Gb", "Gc", "Gd", "Gi", "Gwa", "Gwb", "CC", "sMM", "sVE", "sSC"):
            self.sems[name] = es.enter_context(nc.semaphore(name))
            self.cnt[name] = 0

    def on(self, eng, fn):
        self.streams[eng].append(fn)

    def bump(self, name, by=1):
        self.cnt[name] += by
        return self.sems[name], self.cnt[name]

    def val(self, name):
        return self.cnt[name]

    def wait(self, eng, name, value):
        if value is None or value <= 0:
            return
        h = self.sems[name]
        self.on(eng, lambda e, hh=h, v=value: e.wait_ge(hh, v))

    def emit(self, block):
        def run(name):
            def body(e):
                for fn in self.streams[name]:
                    fn(e)
            return body
        block.sync(run("sync"))
        block.gpsimd(run("gpsimd"))
        block.vector(run("vector"))
        block.scalar(run("scalar"))
        block.tensor(run("tensor"))


def build(TT=2048, upto="all"):
    T = TT
    NT = T // 128
    nc = bass.Bass("TRN2", target_bir_lowering=False)

    x_h = nc.dram_tensor("x", [T, D], F32, kind="ExternalInput").ap()
    wq_h = nc.dram_tensor("wq", [D, DK], F32, kind="ExternalInput").ap()
    wk_h = nc.dram_tensor("wk", [D, DK], F32, kind="ExternalInput").ap()
    wv_h = nc.dram_tensor("wv", [D, DK], F32, kind="ExternalInput").ap()
    wo_h = nc.dram_tensor("wo", [DK, D], F32, kind="ExternalInput").ap()
    wg_h = nc.dram_tensor("wg", [II, D], F32, kind="ExternalInput").ap()
    wu_h = nc.dram_tensor("wu", [II, D], F32, kind="ExternalInput").ap()
    wd_h = nc.dram_tensor("wd", [D, II], F32, kind="ExternalInput").ap()
    out_h = nc.dram_tensor("out", [T, D], F32, kind="ExternalOutput").ap()

    xr_h = nc.dram_tensor("xres", [T, D], F32, kind="Internal").ap()
    wgT_h = nc.dram_tensor("wgT", [NI, 128, ND * 128], BF16, kind="Internal").ap()
    wuT_h = nc.dram_tensor("wuT", [NI, 128, ND * 128], BF16, kind="Internal").ap()
    wdT_h = nc.dram_tensor("wdT", [NI, 128, D], BF16, kind="Internal").ap()
    ups_h = nc.dram_tensor("ups", [NI, 128, T], BF16, kind="Internal").ap()
    cci_t = nc.dram_tensor("cci", [1, 8], F32, kind="Internal")
    cco_t = nc.dram_tensor("cco", [1, 8], F32, kind="Internal")
    cci = cci_t.ap()
    cco = cco_t.ap()

    es = contextlib.ExitStack()

    def sbuf(name, shape, dt=F32):
        return es.enter_context(nc.sbuf_tensor(name, list(shape), dt))

    SW = max(T, 2 * D)
    G = sbuf("G", (128, NI * T), BF16)
    XT = sbuf("XT", (128, ND * T), BF16)
    ARN = sbuf("ARN", (128, max(16384, 3072 + 8 * T + 1024)), BF16)
    SCR = sbuf("SCR", (128, SW))
    SCR3 = sbuf("SCR3", (128, SW))
    SCR2 = sbuf("SCR2", (128, 2 * D))
    XL = sbuf("XL", (128, D))
    IOT = sbuf("IOT", (128, 128), I32)
    IDF = sbuf("IDF", (128, 128))
    IDB = sbuf("IDB", (128, 128), BF16)
    MASKD = sbuf("MASKD", (128, 128))
    TRS = sbuf("TRS", (128, 128))
    TCOL = sbuf("TCOL", (128, 1))
    CCR = sbuf("CCR", (1, 8))
    RS = sbuf("RS", (1, 16))
    CROW = sbuf("CROW", (1, 8))
    CONST = sbuf("CONST", (128, 16))
    ONE1 = sbuf("ONE1", (1, 128))
    MAGP = sbuf("MAGP", (128, 1))
    EPSP = sbuf("EPSP", (128, 1))
    GMAX = sbuf("GMAX", (128, 4))
    TMPC = sbuf("TMPC", (128, 1))
    BMX = sbuf("BMX", (128, 32))
    BSC = sbuf("BSC", (128, 32))
    BRC = sbuf("BRC", (128, 32))
    ST_SUM = sbuf("ST_SUM", (128, 2))
    ST_SQ = sbuf("ST_SQ", (128, 2))
    ST_MX = sbuf("ST_MX", (128, 2))
    ST_MN = sbuf("ST_MN", (128, 2))
    ST_M2 = sbuf("ST_M2", (128, 2))
    ST_SD = sbuf("ST_SD", (128, 2))
    ST_RS = sbuf("ST_RS", (128, 2))
    ST_NB = sbuf("ST_NB", (128, 2))
    MROW = sbuf("MROW", (128, 1))
    BROW = sbuf("BROW", (128, 1))
    RSUM = sbuf("RSUM", (128, 1))
    S_SUM = sbuf("S_SUM", (128, NT))
    S_SQ = sbuf("S_SQ", (128, NT))
    S_M2 = sbuf("S_M2", (128, NT))
    S_MX = sbuf("S_MX", (128, NT))
    S_MN = sbuf("S_MN", (128, NT))
    S_SD = sbuf("S_SD", (128, NT))
    S_RS = sbuf("S_RS", (128, NT))
    S_NB = sbuf("S_NB", (128, NT))
    MROWB = sbuf("MROWB", (128, 8))
    MROW1 = sbuf("MROW1", (128, 2))
    BROWB = sbuf("BROWB", (128, 2))
    RSUMB = sbuf("RSUMB", (128, 2))
    RS2B = sbuf("RS2B", (128, 2))
    RSTT = sbuf("RSTT", (128, 2))
    RRS = sbuf("RRS", (128, NT))
    R11 = sbuf("R11", (1, 1))
    R12 = sbuf("R12", (1, 1))

    PS = es.enter_context(nc.psum_tensor("PS", [128, 3584], F32))
    PTB = es.enter_context(nc.psum_tensor("PTB", [128, 1024], BF16))

    def arn(a, b):
        return ARN[:, a:b]
    WTS = [arn(0, 1024), arn(1024, 2048)]
    WQB = arn(2048, 3072)
    o = 3072
    UPSTG = [arn(o, o + T), arn(o + T, o + 2 * T)]
    UPLD = [arn(o + 2 * T, o + 3 * T), arn(o + 3 * T, o + 4 * T)]
    KUBp = [UPSTG[0], UPSTG[1]]
    WDS = arn(0, NI * 512)
    QTv = arn(0, T)
    KTv = arn(T, 2 * T)
    VBv = arn(2 * T, 2 * T + NT * 128)
    ABv = arn(2 * T + NT * 128, 3 * T + NT * 128)
    ATv = arn(3 * T + NT * 128, 4 * T + NT * 128)
    o2 = 4 * T + NT * 128
    OTv = arn(o2, o2 + 128)
    WQKV = arn(o2 + 128, o2 + 128 + 3072)
    WOB = arn(o2 + 128 + 3072, o2 + 128 + 4096)
    ABv2 = arn(o2 + 128 + 4096, o2 + 128 + 4096 + T)
    WQB2 = arn(o2 + 128 + 4096 + T, o2 + 128 + 4096 + T + 1024)
    WQBp = [WQB, WQB2]

    c = Ctx(nc, es)
    ring_free = [[0, 0] for _ in range(RING)]
    ring_pos = [0]

    with es:
        SELFGATE = {"La", "Lb", "Da", "Db", "Sa", "Sb", "Ga", "Gb", "Gc", "Gd"}

        def load(dst, src, sem="Lw", gate=None):
            if sem in SELFGATE and c.val(sem) > 0:
                c.wait("sync", sem, c.val(sem))
            if gate:
                for nm, v in gate:
                    c.wait("sync", nm, v)
            h, v = c.bump(sem, 16)
            c.on("sync", lambda e, d=dst, s=src: e.dma_start(d, s).then_inc(h, 16))
            return (sem, v)

        def gstore(dst, src, sem="Ga", gate=None):
            if sem in SELFGATE and c.val(sem) > 0:
                c.wait("gpsimd", sem, c.val(sem))
            if gate:
                for nm, v in gate:
                    c.wait("gpsimd", nm, v)
            h, v = c.bump(sem, 16)
            c.on("gpsimd", lambda e, d=dst, s=src: e.dma_start(d, s).then_inc(h, 16))
            return (sem, v)

        def sstore(dst, src, sem="Sa"):
            if sem in SELFGATE and c.val(sem) > 0:
                c.wait("scalar", sem, c.val(sem))
            h, v = c.bump(sem, 16)
            c.on("scalar", lambda e, d=dst, s=src: e.dma_start(d, s).then_inc(h, 16))
            return (sem, v)

        def waitp(eng, pair):
            if pair and pair[1] > 0:
                c.wait(eng, pair[0], pair[1])

        LS = ["La", "Lb"]
        SS = ["Sa", "Sb"]
        GS = ["Ga", "Gb"]
        GW = ["Gwa", "Gwb"]

        def vi(fn):
            h, v = c.bump("sVE")
            if v > 1:
                c.on("vector", lambda e, hh=h, vv=v - 1: e.wait_ge(hh, vv))
            c.on("vector", lambda e, f=fn, hh=h: f(e).then_inc(hh, 1))
            return v

        def si(fn):
            h, v = c.bump("sSC")
            if v > 1:
                c.on("scalar", lambda e, hh=h, vv=v - 1: e.wait_ge(hh, vv))
            c.on("scalar", lambda e, f=fn, hh=h: f(e).then_inc(hh, 1))
            return v

        def ti(fn):
            h, v = c.bump("sMM")
            c.on("tensor", lambda e, f=fn, hh=h: f(e).then_inc(hh, 1))
            return v

        def barrier_t():
            c.wait("tensor", "sSC", c.val("sSC"))
            c.wait("tensor", "sVE", c.val("sVE"))

        c.on("gpsimd", lambda e: e.iota(IOT[:], pattern=[[1, 128]], base=0,
                                        channel_multiplier=-1).then_inc(c.sems["Gi"], 16))
        c.cnt["Gi"] += 16
        v_iota = c.val("Gi")

        vi(lambda v: v.memset(MAGP[:], MAGIC))
        vi(lambda v: v.memset(EPSP[:], 1e-5))
        vi(lambda v: v.memset(ONE1[:], 1.0))
        vi(lambda v: v.memset(TRS[:], 0.0))
        vi(lambda v: v.memset(GMAX[:], 0.0))
        c.wait("vector", "Gi", v_iota)
        vi(lambda v: v.tensor_scalar(IDF[:], IOT[:], 0.0, None, AluOpType.is_equal))
        vi(lambda v: v.tensor_scalar(IDB[:], IOT[:], 0.0, None, AluOpType.is_equal))
        vi(lambda v: v.tensor_scalar(MASKD[:], IOT[:], 0.0, NEG, AluOpType.is_gt,
                                     AluOpType.mult))

        def load_cast(w_src, dst, ncols, slot):
            v_l = load(SCR2[:, slot * D: slot * D + ncols]
                       .rearrange("p (cc k) -> p cc k", k=128), w_src, sem=LS[slot])
            waitp("scalar", v_l)
            return si(lambda s, d=dst, sl=slot, n=ncols:
                      s.activation(d[:, 0:n], SCR2[:, sl * D: sl * D + n], AF.Copy))


        exch_prev = {"s1": 0, "y": 0}

        def exchange(span, math_fn, nconst, const_col, defer=False):
            c0, K = span
            v1x = vi(lambda v: v.tensor_copy(TRS[:, 0:K], GMAX[:, c0:c0 + K]))
            vi(lambda v: v.memset(GMAX[:, c0:c0 + K], 0.0))
            c.wait("tensor", "sVE", v1x)
            c.wait("tensor", "sSC", exch_prev["s1"])
            c.wait("tensor", "sVE", exch_prev["y"])
            t1 = ti(lambda t: t.transpose(PS[:, 3072:3200], TRS[:], IDF[:]))
            c.wait("vector", "sMM", t1)
            v2x = vi(lambda v: v.tensor_reduce(TCOL[:], PS[:, 3072:3200], AX,
                                               AluOpType.max))
            p_ci = gstore(cci[0:1, 0:8], TCOL[0:8, 0:1], sem="Gd", gate=[("sVE", v2x)])
            waitp("gpsimd", p_ci)
            hG, v_cc = c.bump("CC", 1)
            c.on("gpsimd", lambda e, hh=hG: e.collective_compute(
                "AllReduce", AluOpType.max, replica_groups=[list(range(8))],
                ins=[cci_t.ap().opt()], outs=[cco_t.ap().opt()]).then_inc(hh, 1))
            c.wait("gpsimd", "CC", v_cc)
            hG2, v_co = c.bump("Gc", 16)
            c.on("gpsimd", lambda e, hh=hG2: e.dma_start(CCR[:, :], cco[:, :]).then_inc(hh, 16))
            c.wait("vector", "Gc", v_co)
            v3x = 0
            for op in math_fn:
                v3x = vi(op)

            def fin(v3=v3x, n=nconst, cc=const_col):
                c.wait("tensor", "sVE", v3)
                t2 = ti(lambda t: t.matmul(PS[:, 3200:3200 + n], ONE1[:, :],
                                           CROW[0:1, 0:n], start=True, stop=True))
                c.wait("scalar", "sMM", t2)
                exch_prev["s1"] = si(lambda s:
                                     s.activation(CONST[:, cc:cc + n],
                                                  PS[:, 3200:3200 + n], AF.Copy))
            if defer:
                return fin
            fin()

        def rmul(v, dst, a, b):
            return v.tensor_tensor(dst, a, b, AluOpType.mult)

        math_r0 = [
            lambda v: v.tensor_scalar(RS[0:1, 0:1], CCR[0:1, 0:1], 1.0 / 32767.0,
                                      1e-12, AluOpType.mult, AluOpType.max),
            lambda v: v.reciprocal(CROW[0:1, 0:1], RS[0:1, 0:1]),
        ]

        math_r1a = [
            lambda v: rmul(v, R11[0:1, 0:1], CCR[0:1, 0:1], RS[0:1, 0:1]),
            lambda v: v.tensor_scalar(RS[0:1, 1:2], R11[0:1, 0:1], 1.0 / 32767.0,
                                      1e-12, AluOpType.mult, AluOpType.max),
            lambda v: v.reciprocal(R11[0:1, 0:1], RS[0:1, 1:2]),
            lambda v: rmul(v, CROW[0:1, 0:1], R11[0:1, 0:1], RS[0:1, 0:1]),
            lambda v: v.tensor_copy(CROW[0:1, 1:2], RS[0:1, 1:2]),
            lambda v: v.tensor_scalar(CROW[0:1, 2:3], RS[0:1, 1:2], -MAGIC, None,
                                      AluOpType.mult),
            lambda v: v.tensor_scalar(RS[0:1, 3:4], RS[0:1, 1:2], 1.0 / 65536.0,
                                      None, AluOpType.mult),
        ]

        math_r1b2 = [
            lambda v: rmul(v, R11[0:1, 0:1], CCR[0:1, 0:1], RS[0:1, 0:1]),
            lambda v: v.tensor_scalar(RS[0:1, 2:3], R11[0:1, 0:1], 1.0 / 32767.0,
                                      1e-12, AluOpType.mult, AluOpType.max),
            lambda v: v.reciprocal(R12[0:1, 0:1], RS[0:1, 2:3]),
            lambda v: rmul(v, CROW[0:1, 0:1], R12[0:1, 0:1], RS[0:1, 0:1]),
            lambda v: rmul(v, R11[0:1, 0:1], CCR[0:1, 1:2], RS[0:1, 3:4]),
            lambda v: v.tensor_scalar(RS[0:1, 4:5], R11[0:1, 0:1], 1.0 / 32767.0,
                                      1e-12, AluOpType.mult, AluOpType.max),
            lambda v: v.reciprocal(R11[0:1, 0:1], RS[0:1, 4:5]),
            lambda v: rmul(v, CROW[0:1, 1:2], R11[0:1, 0:1], RS[0:1, 3:4]),
            lambda v: rmul(v, RS[0:1, 5:6], RS[0:1, 4:5], RS[0:1, 2:3]),
        ]

        math_r3 = [
            lambda v: rmul(v, R11[0:1, 0:1], CCR[0:1, 0:1], RS[0:1, 5:6]),
            lambda v: v.tensor_scalar(R12[0:1, 0:1], R11[0:1, 0:1], 1.0 / 32767.0,
                                      1e-12, AluOpType.mult, AluOpType.max),
            lambda v: v.reciprocal(R11[0:1, 0:1], R12[0:1, 0:1]),
            lambda v: rmul(v, CROW[0:1, 0:1], R11[0:1, 0:1], RS[0:1, 5:6]),
            lambda v: v.tensor_scalar(CROW[0:1, 1:2], R12[0:1, 0:1], 0.5, None,
                                      AluOpType.mult),
        ]


        nb = D // 32
        ptb = {"ttr": 0, "cp": 0, "vrd": 0}
        dqs_rd = [0, 0]
        dqs_sp = [None, None]
        swp = {"mm": [0, 0]}
        scr_rd = [0, 0]

        DQS = ["Da", "Db"]

        def dq_load(src_ap, sl):
            dst32 = (SCR if sl == 0 else SCR3)[:, D:2 * D]
            return load(dst32, src_ap, sem=DQS[sl],
                        gate=[("sVE", dqs_rd[sl])] if dqs_rd[sl] else None)

        def dequant_tile(v_l, sl):
            dst32 = (SCR if sl == 0 else SCR3)[:, D:2 * D]
            waitp("vector", v_l)
            vi(lambda v, d=dst32: v.tensor_reduce(
                BMX[:, 0:nb], d.rearrange("p (b w) -> p b w", w=32),
                AX, AluOpType.max, apply_absolute_value=True))
            vi(lambda v: v.tensor_scalar(BSC[:, 0:nb], BMX[:, 0:nb], 1.0 / 7.0,
                                         1e-12, AluOpType.mult, AluOpType.max))
            vi(lambda v: v.reciprocal(BRC[:, 0:nb], BSC[:, 0:nb]))
            vi(lambda v, d=dst32: v.tensor_tensor(
                d.rearrange("p (b w) -> p b w", w=32),
                d.rearrange("p (b w) -> p b w", w=32),
                BRC[:, 0:nb].broadcast_to([128, nb, 32]), AluOpType.mult))
            vi(lambda v, d=dst32: v.tensor_scalar(
                d, d, MAGIC, MAGIC, AluOpType.add, AluOpType.subtract))
            c.wait("vector", "sMM", ptb["ttr"])
            v_dq = vi(lambda v, d=dst32, ss=sl: v.scalar_tensor_tensor(
                WQBp[ss][:, 0:D].rearrange("p (b w) -> p b w", w=32),
                d.rearrange("p (b w) -> p b w", w=32), 0.0,
                BSC[:, 0:nb].broadcast_to([128, nb, 32]),
                AluOpType.add, AluOpType.mult))
            dqs_rd[sl] = v_dq
            return v_dq

        def transpose_wqb(v_dq, sl):
            c.wait("tensor", "sVE", v_dq)
            c.wait("tensor", "sSC", ptb["cp"])
            c.wait("tensor", "sVE", ptb["vrd"])
            t_tr = None
            for b in range(ND):
                t_tr = ti(lambda t, bb=b, ss=sl: t.transpose(
                    PTB[:, bb * 128:(bb + 1) * 128],
                    WQBp[ss][:, bb * 128:(bb + 1) * 128], IDB[:]))
            ptb["ttr"] = t_tr
            return t_tr

        def gen_gateup(w_src, wT_dst, stage, spills):
            def tail(it, v_dq):
                sl = it % 2
                transpose_wqb(v_dq, sl)
                c.wait("scalar", "sMM", ptb["ttr"])
                waitp("scalar", dqs_sp[sl])
                s_cp = si(lambda s, ss=sl: s.activation(
                    stage[ss][:, 0:D], PTB[:, :], AF.Copy))
                ptb["cp"] = s_cp
                sp = gstore(wT_dst[it], stage[sl][:, 0:D], sem=GW[sl],
                            gate=[("sSC", s_cp)])
                dqs_sp[sl] = sp
                spills[it] = sp

            vl = [dq_load(w_src[0:128, :], 0), dq_load(w_src[128:256, :], 1)]
            pend = None
            for it in range(NI):
                sl = it % 2
                v_dq = dequant_tile(vl[sl], sl)
                if it + 2 < NI:
                    vl[sl] = dq_load(w_src[(it + 2) * 128:(it + 3) * 128, :], sl)
                if pend is not None:
                    tail(*pend)
                pend = (it, v_dq)
                if it == NI - 1:
                    tail(*pend)
                yield

        def gen_down(stage):
            def wsrc(q):
                dp, ic = divmod(q, 3)
                return wd_h[dp * 128:(dp + 1) * 128, ic * 1024:(ic + 1) * 1024]

            def tail(q, v_dq):
                dp, ic = divmod(q, 3)
                sl = q % 2
                transpose_wqb(v_dq, sl)
                c.wait("scalar", "sMM", ptb["ttr"])
                waitp("scalar", dqs_sp[sl])
                s_cp = si(lambda s, ss=sl: s.activation(
                    stage[ss][:, 0:D], PTB[:, :], AF.Copy))
                ptb["cp"] = s_cp
                for icb in range(ND):
                    it = ic * 8 + icb
                    dqs_sp[sl] = gstore(wdT_h[it, :, dp * 128:(dp + 1) * 128],
                                        stage[sl][:, icb * 128:(icb + 1) * 128],
                                        sem=GW[sl],
                                        gate=[("sSC", s_cp)] if icb == 0 else None)

            vl = [dq_load(wsrc(0), 0), dq_load(wsrc(1), 1)]
            pend = None
            for q in range(24):
                sl = q % 2
                v_dq = dequant_tile(vl[sl], sl)
                if q + 2 < 24:
                    vl[sl] = dq_load(wsrc(q + 2), sl)
                if pend is not None:
                    tail(*pend)
                pend = (q, v_dq)
                if q == 23:
                    tail(*pend)
                yield

        ln_gate = [[], []]
        ln_drv = [0]

        def ln_stats(src_res_h, with_max):
            gate0 = [("Sa", c.val("Sa")), ("Sb", c.val("Sb"))]
            c.wait("scalar", "sMM", c.val("sMM"))
            c.wait("scalar", "sVE", scr_rd[0])
            rdA = [0, 0]
            sqA = [0, 0]
            for t in range(NT):
                sl = t % 2
                dst = SCR2[:, sl * D:(sl + 1) * D]
                gate = (list(gate0) if t < 2
                        else [("sVE", rdA[sl]), ("sSC", sqA[sl])])
                v_l = load(dst, src_res_h[t * 128:(t + 1) * 128, :], sem=LS[sl],
                           gate=gate)
                waitp("vector", v_l)
                c.wait("vector", "sSC", sqA[sl])
                v_r = vi(lambda v, d=dst, tt=t: v.tensor_reduce(
                    S_SUM[:, tt:tt + 1], d, AX, AluOpType.add))
                if with_max:
                    vi(lambda v, d=dst, tt=t: v.tensor_reduce(
                        S_MX[:, tt:tt + 1], d, AX, AluOpType.max))
                    v_r = vi(lambda v, d=dst, tt=t: v.tensor_reduce(
                        S_MN[:, tt:tt + 1], d, AX, AluOpType.min))
                rdA[sl] = v_r
                waitp("scalar", v_l)
                sqA[sl] = si(lambda s, d=dst, tt=t: s.activation(
                    SCR[:, 0:D], d, AF.Square, accum_out=S_SQ[:, tt:tt + 1]))
            ln_gate[0] = [("sVE", rdA[0]), ("sSC", sqA[0])]
            ln_gate[1] = [("sVE", rdA[1]), ("sSC", sqA[1])]
            c.wait("vector", "sSC", sqA[0])
            c.wait("vector", "sSC", sqA[1])
            vi(lambda v: v.tensor_scalar(S_SUM[:, 0:NT], S_SUM[:, 0:NT], 1.0 / D,
                                         None, AluOpType.mult))
            vi(lambda v: v.tensor_scalar(S_SQ[:, 0:NT], S_SQ[:, 0:NT], 1.0 / D,
                                         None, AluOpType.mult))
            vi(lambda v: v.tensor_tensor(S_M2[:, 0:NT], S_SUM[:, 0:NT],
                                         S_SUM[:, 0:NT], AluOpType.mult))
            v_var = vi(lambda v: v.tensor_tensor(S_SQ[:, 0:NT], S_SQ[:, 0:NT],
                                                 S_M2[:, 0:NT], AluOpType.subtract))
            c.wait("scalar", "sVE", v_var)
            s_sd = si(lambda s: s.activation(S_SD[:, 0:NT], S_SQ[:, 0:NT], AF.Sqrt,
                                             bias=EPSP[:]))
            c.wait("vector", "sSC", s_sd)
            vi(lambda v: v.reciprocal(S_RS[:, 0:NT], S_SD[:, 0:NT]))
            ln_drv[0] = vi(lambda v: v.scalar_tensor_tensor(
                S_NB[:, 0:NT], S_SUM[:, 0:NT], -1.0, S_RS[:, 0:NT],
                AluOpType.mult, AluOpType.mult))
            if with_max:
                vi(lambda v: v.tensor_tensor(S_MX[:, 0:NT], S_MX[:, 0:NT],
                                             S_SUM[:, 0:NT], AluOpType.subtract))
                vi(lambda v: v.tensor_tensor(S_MN[:, 0:NT], S_SUM[:, 0:NT],
                                             S_MN[:, 0:NT], AluOpType.subtract))
                vi(lambda v: v.tensor_tensor(S_MX[:, 0:NT], S_MX[:, 0:NT],
                                             S_MN[:, 0:NT], AluOpType.max))
                vi(lambda v: v.tensor_tensor(S_MX[:, 0:NT], S_MX[:, 0:NT],
                                             S_RS[:, 0:NT], AluOpType.mult))
                vi(lambda v: v.tensor_reduce(GMAX[:, 0:1], S_MX[:, 0:NT], AX,
                                             AluOpType.max))

        def ln_apply(src_res_h, pump=None):
            xln_done = [0, 0]
            ln_ttr = 0
            for t in range(NT):
                sl = t % 2
                dst = SCR2[:, sl * D:(sl + 1) * D]
                gate = (list(ln_gate[sl]) if t < 2
                        else [("sSC", xln_done[sl])])
                v_l = load(dst, src_res_h[t * 128:(t + 1) * 128, :], sem=LS[sl],
                           gate=gate)
                waitp("scalar", v_l)
                if t == 0:
                    c.wait("scalar", "sVE", ln_drv[0])
                c.wait("scalar", "sMM", ln_ttr)
                s_ln = si(lambda s, d=dst, tt=t: s.activation(
                    XL[:, :], d, AF.Identity, scale=S_RS[:, tt:tt + 1],
                    bias=S_NB[:, tt:tt + 1]))
                xln_done[sl] = s_ln
                for half in range(2):
                    slot = ring_pos[0] % RING
                    ring_pos[0] += 1
                    c.wait("tensor", "sSC", ring_free[slot][0])
                    c.wait("tensor", "sVE", ring_free[slot][1])
                    if half == 0:
                        c.wait("tensor", "sSC", s_ln)
                    t_tr = None
                    for b in range(4):
                        t_tr = ti(lambda tt2, bb=b, hh=half, slo=slot: tt2.transpose(
                            PS[:, slo * 512 + bb * 128: slo * 512 + (bb + 1) * 128],
                            XL[:, (hh * 4 + bb) * 128:(hh * 4 + bb + 1) * 128],
                            IDF[:]))
                    ln_ttr = t_tr
                    c.wait("scalar", "sMM", t_tr)
                    s_cp = si(lambda s, ttt=t, hh=half, slo=slot: s.activation(
                        XT[:].rearrange("p (cc t2) -> p cc t2", t2=T)
                        [:, hh * 4:(hh + 1) * 4, ttt * 128:(ttt + 1) * 128],
                        PS[:, slo * 512:(slo + 1) * 512]
                        .rearrange("p (cc j) -> p cc j", j=128), AF.Copy))
                    ring_free[slot] = [s_cp, 0]
                if pump is not None:
                    pump(t)

        def ln_pass(src_res_h, with_max, pump=None):
            ln_stats(src_res_h, with_max)
            ln_apply(src_res_h, pump=pump)

        def quantize_xt():
            for cb in range(ND):
                sl = cb % 2
                scr = SCR if sl == 0 else SCR3
                c.wait("scalar", "sVE", scr_rd[sl])
                sA = si(lambda s, cbb=cb, d=scr: s.activation(
                    d[:, 0:T], XT[:, cbb * T:(cbb + 1) * T], AF.Identity,
                    scale=CONST[:, 0:1], bias=MAGP[:]))
                c.wait("vector", "sSC", sA)
                scr_rd[sl] = vi(lambda v, cbb=cb, d=scr: v.tensor_scalar(
                    XT[:, cbb * T:(cbb + 1) * T], d[:, 0:T], MAGIC, None,
                    AluOpType.subtract))

        def mm_sweep(wT_h, is_gate, chain_fn=None, bar=True, spills=None):
            col = 0 if is_gate else 1
            if bar:
                barrier_t()
            gw = [("Gwa", c.val("Gwa")), ("Gwb", c.val("Gwb"))]
            up_spill = [("Ga", c.val("Ga")), ("Gb", c.val("Gb"))]
            up_red = [0, 0]
            v_loads = [None] * NI
            mm_done = swp["mm"]

            def lgate(j, base):
                if spills is not None and spills[j] is not None:
                    return base + [spills[j]]
                return base

            v_loads[0] = load(WTS[0], wT_h[0], sem="La",
                              gate=lgate(0, gw + [("sMM", mm_done[0])]))
            v_loads[1] = load(WTS[1], wT_h[1], sem="Lb",
                              gate=lgate(1, gw + [("sMM", mm_done[1])]))
            ntc = T // 512
            for it in range(NI):
                sl = it % 2
                waitp("tensor", v_loads[it])
                if is_gate:
                    dstv = G[:, it * T:(it + 1) * T]
                else:
                    dstv = UPSTG[it % 2]
                last_mm = None
                pair_mm = None
                last_se = None
                slots = []
                for tc in range(ntc):
                    slot = ring_pos[0] % RING
                    ring_pos[0] += 1
                    c.wait("tensor", "sSC", ring_free[slot][0])
                    c.wait("tensor", "sVE", ring_free[slot][1])
                    slots.append(slot)
                    for cb in range(ND):
                        is_last = (cb == ND - 1)
                        fn = (lambda t, sll=sl, cbb=cb, tcc=tc, slo=slot,
                              st=(cb == 0), sp=is_last:
                              t.matmul(PS[:, slo * 512:(slo + 1) * 512],
                                       WTS[sll][:, cbb * 128:(cbb + 1) * 128],
                                       XT[:, cbb * T + tcc * 512: cbb * T + (tcc + 1) * 512],
                                       start=st, stop=sp))
                        if is_last:
                            last_mm = ti(fn)
                        else:
                            c.on("tensor", fn)
                    if tc % 2 == 1 or tc == ntc - 1:
                        nb = 2 if tc % 2 == 1 else 1
                        c.wait("scalar", "sMM", last_mm)
                        if tc < 2 and not is_gate:
                            c.wait("scalar", up_spill[sl][0], up_spill[sl][1])
                            c.wait("scalar", "sVE", up_red[sl])
                            if it < 2:
                                for nm, vv in gw:
                                    c.wait("scalar", nm, vv)
                        last_se = si(lambda s, dd=dstv, tcc=tc, nbb=nb,
                                     slo=slots[tc - nb + 1]: s.activation(
                                         dd[:, (tcc - nbb + 1) * 512:(tcc + 1) * 512],
                                         PS[:, slo * 512:(slo + nbb) * 512], AF.Copy))
                        for j in range(nb):
                            ring_free[slots[tc - j]] = [last_se, 0]
                c.wait("vector", "sSC", last_se)
                v_r1 = vi(lambda v, dd=dstv: v.tensor_reduce(
                    TMPC[:], dd[:, 0:T], AX, AluOpType.max, apply_absolute_value=True))
                up_red[sl] = v_r1
                vi(lambda v, cc=col: v.tensor_tensor(
                    GMAX[:, cc:cc + 1], GMAX[:, cc:cc + 1], TMPC[:], AluOpType.max))
                mm_done[sl] = last_mm
                if not is_gate:
                    up_spill[sl] = gstore(ups_h[it], UPSTG[sl], sem=GS[sl],
                                          gate=[("sSC", last_se)])
                if it + 2 < NI:
                    v_loads[it + 2] = load(WTS[sl], wT_h[it + 2], sem=LS[sl],
                                           gate=lgate(it + 2, [("sMM", mm_done[sl])]))
                if chain_fn is not None:
                    chain_fn(it)

        sigst = {"kub": [0, 0]}

        def sig_it(it):
            sl = it % 2
            scr = SCR if sl == 0 else SCR3
            kb = UPLD[sl]
            gslice = G[:, it * T:(it + 1) * T]
            c.wait("scalar", "sVE", scr_rd[sl])
            sA = si(lambda s, gs=gslice, d=scr: s.activation(
                d[:, 0:T], gs, AF.Identity, scale=CONST[:, 1:2], bias=MAGP[:]))
            c.wait("scalar", "sVE", sigst["kub"][sl])
            sB = si(lambda s, d=scr, k=kb: s.activation(
                k[:, 0:T], d[:, 0:T], AF.Sigmoid, scale=CONST[:, 2:3],
                bias=CONST[:, 3:4]))
            c.wait("vector", "sSC", sA)
            v_kg = vi(lambda v, gs=gslice, d=scr: v.tensor_scalar(
                gs, d[:, 0:T], MAGIC, None, AluOpType.subtract))
            c.wait("vector", "sSC", sB)
            vi(lambda v, d=scr, k=kb: v.tensor_scalar(
                d[:, 0:T], k[:, 0:T], 65536.0, MAGIC,
                AluOpType.mult, AluOpType.add))
            v_sq = vi(lambda v, d=scr, k=kb: v.tensor_scalar(
                k[:, 0:T], d[:, 0:T], MAGIC, 65535.0,
                AluOpType.subtract, AluOpType.min))
            scr_rd[sl] = v_sq
            v_tt = vi(lambda v, gs=gslice, k=kb: v.tensor_tensor(
                gs, gs, k[:, 0:T], AluOpType.mult))
            sigst["kub"][sl] = v_tt
            vi(lambda v, gs=gslice: v.tensor_reduce(
                TMPC[:], gs, AX, AluOpType.max, apply_absolute_value=True))
            vi(lambda v: v.tensor_tensor(GMAX[:, 2:3], GMAX[:, 2:3], TMPC[:],
                                         AluOpType.max))

        def rint_g(col):
            for it in range(NI):
                sl = it % 2
                scr = SCR if sl == 0 else SCR3
                gslice = G[:, it * T:(it + 1) * T]
                c.wait("scalar", "sVE", scr_rd[sl])
                sA = si(lambda s, gs=gslice, d=scr, cc=col: s.activation(
                    d[:, 0:T], gs, AF.Identity, scale=CONST[:, cc:cc + 1], bias=MAGP[:]))
                c.wait("vector", "sSC", sA)
                scr_rd[sl] = vi(lambda v, gs=gslice, d=scr: v.tensor_scalar(
                    gs, d[:, 0:T], MAGIC, None, AluOpType.subtract))

        def rint_chunk(col, tc, cw, it_lo=0, it_hi=None):
            last = 0
            for it in range(it_lo, NI if it_hi is None else it_hi):
                sl = it % 2
                scr = SCR if sl == 0 else SCR3
                gs = G[:, it * T + tc * cw: it * T + (tc + 1) * cw]
                c.wait("scalar", "sVE", scr_rd[sl])
                sA = si(lambda s, g=gs, d=scr, cc=col, w=cw: s.activation(
                    d[:, 0:w], g, AF.Identity, scale=CONST[:, cc:cc + 1], bias=MAGP[:]))
                c.wait("vector", "sSC", sA)
                last = vi(lambda v, g=gs, d=scr, w=cw: v.tensor_scalar(
                    g, d[:, 0:w], MAGIC, None, AluOpType.subtract))
                scr_rd[sl] = last
            return last

        def o_phase():
            gate0 = [("Ga", c.val("Ga")), ("Gb", c.val("Gb"))]
            v_loads = [None] * NI
            sgate = [("sSC", c.val("sSC")), ("sVE", c.val("sVE"))]
            v_loads[0] = load(UPLD[0], ups_h[0], sem="La", gate=gate0 + sgate)
            v_loads[1] = load(UPLD[1], ups_h[1], sem="Lb", gate=sgate)
            ost = {}

            def oA(it):
                sl = it % 2
                scr = SCR if sl == 0 else SCR3
                kb = KUBp[sl]
                waitp("scalar", v_loads[it])
                c.wait("scalar", "sVE", scr_rd[sl])
                if it < 2:
                    for nm, vv in gate0:
                        c.wait("scalar", nm, vv)
                sA = si(lambda s, d=scr, ss=sl: s.activation(
                    d[:, 0:T], UPLD[ss][:, 0:T], AF.Identity, scale=CONST[:, 4:5],
                    bias=MAGP[:]))
                c.wait("vector", "sSC", sA)
                v_ku = vi(lambda v, d=scr, k=kb: v.tensor_scalar(
                    k[:, 0:T], d[:, 0:T], MAGIC, None, AluOpType.subtract))
                ost[it] = v_ku
                if it + 2 < NI:
                    v_loads[it + 2] = load(UPLD[sl], ups_h[it + 2], sem=LS[sl],
                                           gate=[("sSC", sA)])

            def oB(it):
                sl = it % 2
                scr = SCR if sl == 0 else SCR3
                kb = KUBp[sl]
                gslice = G[:, it * T:(it + 1) * T]
                c.wait("scalar", "sVE", ost[it])
                sA5 = si(lambda s, d=scr, gs=gslice: s.activation(
                    d[:, 0:T], gs, AF.Identity, scale=CONST[:, 5:6], bias=MAGP[:]))
                c.wait("vector", "sSC", sA5)
                v_tt = vi(lambda v, gs=gslice, d=scr, k=kb: v.scalar_tensor_tensor(
                    gs, d[:, 0:T], -MAGIC, k[:, 0:T],
                    AluOpType.add, AluOpType.mult))
                scr_rd[sl] = v_tt
                vi(lambda v, gs=gslice: v.tensor_reduce(
                    TMPC[:], gs, AX, AluOpType.max, apply_absolute_value=True))
                vi(lambda v: v.tensor_tensor(GMAX[:, 0:1], GMAX[:, 0:1], TMPC[:],
                                             AluOpType.max))

            for it in range(NI):
                oA(it)
                if it >= 1:
                    oB(it - 1)
            oB(NI - 1)

        def down_phase(src_res_h, dst_res_h, xstore, rint_col=None):
            barrier_t()
            c.wait("vector", "sSC", c.val("sSC"))
            wds_gate = [("sVE", c.val("sVE")), ("sSC", c.val("sSC")),
                        ("Gwa", c.val("Gwa")), ("Gwb", c.val("Gwb"))]
            ntc2 = max(1, T // 512)
            tpc = NT // ntc2
            ipt = min(NI, 2 * ((NI + tpc - 1) // tpc))
            rc_done = [0] * (ntc2 + 1)
            if rint_col is not None:
                rc_done[0] = rint_chunk(rint_col, 0, min(512, T), 0, NI)
            for dc in range(2):
                v_wl = 0
                for it in range(NI):
                    g = wds_gate if (it == 0) else None
                    if dc == 1 and it == 0:
                        g = [("sMM", c.val("sMM"))]
                    v_wl = load(WDS[:, it * 512:(it + 1) * 512],
                                wdT_h[it, :, dc * 512:(dc + 1) * 512], sem="Lw", gate=g)
                gate0 = [("Sa", c.val("Sa")), ("Sb", c.val("Sb")),
                         ("sSC", c.val("sSC")), ("sVE", c.val("sVE"))]
                for tt in range(NT):
                    sl = tt % 2
                    tc2 = tt // tpc
                    xg = list(gate0) if tt < 2 else [xstore[sl]]
                    v_x = load(SCR2[:, sl * D + dc * 512: sl * D + (dc + 1) * 512],
                               src_res_h[tt * 128:(tt + 1) * 128, dc * 512:(dc + 1) * 512],
                               sem=LS[sl], gate=xg)
                    slot = ring_pos[0] % RING
                    ring_pos[0] += 1
                    c.wait("tensor", "sSC", ring_free[slot][0])
                    c.wait("tensor", "sVE", ring_free[slot][1])
                    if tt == 0:
                        waitp("tensor", v_wl)
                    if dc == 0 and rint_col is not None and tt % tpc == 0:
                        c.wait("tensor", "sVE", rc_done[tc2])
                    last_mm = None
                    for it in range(NI):
                        is_last = (it == NI - 1)
                        fn = (lambda t, itt=it, ttt=tt, slo=slot, st=(it == 0), sp=is_last:
                              t.matmul(PS[:, slo * 512:(slo + 1) * 512],
                                       G[:, itt * T + ttt * 128: itt * T + (ttt + 1) * 128],
                                       WDS[:, itt * 512:(itt + 1) * 512],
                                       start=st, stop=sp))
                        if is_last:
                            last_mm = ti(fn)
                        else:
                            c.on("tensor", fn)
                    if dc == 0 and rint_col is not None and tc2 + 1 < ntc2:
                        i0 = (tt % tpc) * ipt
                        i1 = min(NI, i0 + ipt)
                        if i0 < i1:
                            rc_done[tc2 + 1] = rint_chunk(rint_col, tc2 + 1, 512,
                                                          i0, i1)
                    c.wait("vector", "sMM", last_mm)
                    waitp("vector", v_x)
                    v_e = vi(lambda v, slo=slot, sll=sl, dcc=dc:
                             v.scalar_tensor_tensor(
                                 SCR2[:, sll * D + dcc * 512: sll * D + (dcc + 1) * 512],
                                 PS[:, slo * 512:(slo + 1) * 512], CONST[:, 7:8],
                                 SCR2[:, sll * D + dcc * 512: sll * D + (dcc + 1) * 512],
                                 AluOpType.mult, AluOpType.add))
                    ring_free[slot] = [ring_free[slot][0], v_e]
                    c.wait("scalar", "sVE", v_e)
                    xstore[sl] = sstore(
                        dst_res_h[tt * 128:(tt + 1) * 128, dc * 512:(dc + 1) * 512],
                        SCR2[:, sl * D + dc * 512: sl * D + (dc + 1) * 512], sem=SS[sl])

        def attention(xstore):
            ln_pass(xr_h, False)
            wgate = [("sSC", c.val("sSC")), ("sVE", c.val("sVE"))]
            c.wait("sync", wgate[0][0], wgate[0][1])
            c.wait("sync", wgate[1][0], wgate[1][1])
            v1 = load_cast(wq_h[:, :].rearrange("(cc p) k -> p cc k", p=128),
                           WQKV[:, 0:1024], 1024, 0)
            v2 = load_cast(wk_h[:, :].rearrange("(cc p) k -> p cc k", p=128),
                           WQKV[:, 1024:2048], 1024, 1)
            c.wait("sync", "sSC", v1)
            v3 = load_cast(wv_h[:, :].rearrange("(cc p) k -> p cc k", p=128),
                           WQKV[:, 2048:3072], 1024, 0)
            c.wait("sync", "sSC", v2)
            v_lwo = load(SCR2[:, D:2 * D], wo_h[:, :], sem="Lb")
            waitp("scalar", v_lwo)
            s_wo = si(lambda s: s.activation(WOB[:, 0:D], SCR2[:, D:2 * D], AF.Copy))
            barrier_t()
            for w_i, dstv in ((0, QTv), (1, KTv)):
                for tc in range(T // 512):
                    slot = ring_pos[0] % RING
                    ring_pos[0] += 1
                    c.wait("tensor", "sSC", ring_free[slot][0])
                    c.wait("tensor", "sVE", ring_free[slot][1])
                    last_mm = None
                    for cb in range(ND):
                        is_last = (cb == ND - 1)
                        fn = (lambda t, wi=w_i, cbb=cb, tcc=tc, slo=slot,
                              st=(cb == 0), sp=is_last:
                              t.matmul(PS[:, slo * 512:(slo + 1) * 512],
                                       WQKV[:, (wi * ND + cbb) * 128:(wi * ND + cbb + 1) * 128],
                                       XT[:, cbb * T + tcc * 512: cbb * T + (tcc + 1) * 512],
                                       start=st, stop=sp))
                        if is_last:
                            last_mm = ti(fn)
                        else:
                            c.on("tensor", fn)
                    c.wait("scalar", "sMM", last_mm)
                    s_e = si(lambda s, d=dstv, tcc=tc, slo=slot: s.activation(
                        d[:, tcc * 512:(tcc + 1) * 512],
                        PS[:, slo * 512:(slo + 1) * 512], AF.Copy))
                    ring_free[slot] = [s_e, ring_free[slot][1]]
            for tt in range(NT):
                slot = ring_pos[0] % RING
                ring_pos[0] += 1
                c.wait("tensor", "sSC", ring_free[slot][0])
                c.wait("tensor", "sVE", ring_free[slot][1])
                last_mm = None
                for cb in range(ND):
                    is_last = (cb == ND - 1)
                    fn = (lambda t, cbb=cb, ttt=tt, slo=slot, st=(cb == 0), sp=is_last:
                          t.matmul(PS[:, slo * 512: slo * 512 + 128],
                                   XT[:, cbb * T + ttt * 128: cbb * T + (ttt + 1) * 128],
                                   WQKV[:, (2 * ND + cbb) * 128:(2 * ND + cbb + 1) * 128],
                                   start=st, stop=sp))
                    if is_last:
                        last_mm = ti(fn)
                    else:
                        c.on("tensor", fn)
                c.wait("scalar", "sMM", last_mm)
                s_e = si(lambda s, ttt=tt, slo=slot: s.activation(
                    VBv[:, ttt * 128:(ttt + 1) * 128],
                    PS[:, slo * 512: slo * 512 + 128], AF.Copy))
                ring_free[slot] = [s_e, ring_free[slot][1]]
            c.wait("tensor", "sSC", c.val("sSC"))
            c.wait("tensor", "sVE", c.val("sVE"))
            gate0 = [("Sa", c.val("Sa")), ("Sb", c.val("Sb")),
                     ("sSC", c.val("sSC")), ("sVE", c.val("sVE"))]
            ABp = [ABv, ABv2]
            scrA_rd = [scr_rd[0], scr_rd[1]]
            st = {}
            pv = {"s_at": 0, "ot": 0, "y": 0, "ltr": {}}
            atk_pos = [0]

            def stageA(qt):
                L = (qt + 1) * 128
                par = qt % 2
                scr = SCR if par == 0 else SCR3
                v_x = load(SCR2[:, par * D:(par + 1) * D],
                           xr_h[qt * 128:(qt + 1) * 128, :], sem=LS[par],
                           gate=list(gate0) if qt < 2 else [xstore[par]])
                nkb = (L + 511) // 512
                dkb = (qt * 128) // 512
                doff = qt * 128 - dkb * 512
                c.wait("scalar", "sVE", scrA_rd[par])
                last_sev = 0
                for kb in range(nkb):
                    w = min(512, L - kb * 512)
                    slot = atk_pos[0] % 5
                    atk_pos[0] += 1
                    c.wait("tensor", "sSC", ring_free[slot][0])
                    c.wait("tensor", "sVE", ring_free[slot][1])
                    mm = ti(lambda t, qtt=qt, kbb=kb, ww=w, slo=slot:
                            t.matmul(PS[:, slo * 512: slo * 512 + ww],
                                     QTv[:, qtt * 128:(qtt + 1) * 128],
                                     KTv[:, kbb * 512: kbb * 512 + ww],
                                     start=True, stop=True))
                    if kb == dkb:
                        c.wait("vector", "sMM", mm)
                        v_m = vi(lambda v, slo=slot, off=doff: v.tensor_tensor(
                            PS[:, slo * 512 + off: slo * 512 + off + 128],
                            PS[:, slo * 512 + off: slo * 512 + off + 128],
                            MASKD[:], AluOpType.add))
                    if kb % 2 == 0:
                        if kb == dkb:
                            c.wait("scalar", "sVE", v_m)
                        else:
                            c.wait("scalar", "sMM", mm)
                        s_ev = si(lambda s, d=scr, kbb=kb, ww=w, slo=slot:
                                  s.activation(d[:, kbb * 512: kbb * 512 + ww],
                                               PS[:, slo * 512: slo * 512 + ww],
                                               AF.Copy))
                        ring_free[slot] = [s_ev, 0]
                        c.wait("vector", "sSC", s_ev)
                        last_sev = s_ev
                    else:
                        if kb != dkb:
                            c.wait("vector", "sMM", mm)
                        v_ev = vi(lambda v, d=scr, kbb=kb, ww=w, slo=slot:
                                  v.tensor_copy(d[:, kbb * 512: kbb * 512 + ww],
                                                PS[:, slo * 512: slo * 512 + ww]))
                        ring_free[slot] = [0, v_ev]
                    vi(lambda v, d=scr, kbb=kb, ww=w, pp=par: v.tensor_reduce(
                        MROWB[:, pp * 4 + kbb: pp * 4 + kbb + 1],
                        d[:, kbb * 512: kbb * 512 + ww], AX, AluOpType.max))
                v_cmb = vi(lambda v, pp=par, n=nkb: v.tensor_reduce(
                    MROW1[:, pp:pp + 1], MROWB[:, pp * 4: pp * 4 + n], AX,
                    AluOpType.max))
                v_brow = vi(lambda v, pp=par: v.tensor_scalar(
                    BROWB[:, pp:pp + 1], MROW1[:, pp:pp + 1], -CINV, None,
                    AluOpType.mult))
                scrA_rd[par] = v_cmb
                st[qt] = {"vx": v_x, "brow": v_brow, "sev": last_sev}

            def stageB(qt):
                L = (qt + 1) * 128
                par = qt % 2
                scr = SCR if par == 0 else SCR3
                ab = ABp[par]
                nblk = qt + 1
                c.wait("scalar", "sVE", st[qt]["brow"])
                if qt >= 2:
                    c.wait("scalar", "sVE", st[qt - 2]["recip"])
                    c.wait("scalar", "sMM", pv["ltr"].get(qt - 2, 0))
                Lh = min(L, 1024)
                s_exp = si(lambda s, d=scr, a=ab, LL=Lh, pp=par: s.activation(
                    a[:, 0:LL], d[:, 0:LL], AF.Exp, scale=CINV,
                    bias=BROWB[:, pp:pp + 1], accum_out=RSUMB[:, pp:pp + 1]))
                s_exp2 = None
                if L > 1024:
                    s_exp2 = si(lambda s, d=scr, a=ab, pp=par: s.activation(
                        a[:, 1024:L], d[:, 1024:L], AF.Exp, scale=CINV,
                        bias=BROWB[:, pp:pp + 1], accum_out=RS2B[:, pp:pp + 1]))
                c.wait("vector", "sSC", s_exp2 if s_exp2 else s_exp)
                if s_exp2:
                    vi(lambda v, pp=par: v.tensor_tensor(
                        RSTT[:, pp:pp + 1], RSUMB[:, pp:pp + 1], RS2B[:, pp:pp + 1],
                        AluOpType.add))
                    v_rr = vi(lambda v, qtt=qt, pp=par: v.reciprocal(
                        RRS[:, qtt:qtt + 1], RSTT[:, pp:pp + 1]))
                else:
                    v_rr = vi(lambda v, qtt=qt, pp=par: v.reciprocal(
                        RRS[:, qtt:qtt + 1], RSUMB[:, pp:pp + 1]))
                st[qt]["recip"] = v_rr
                s_at = pv["s_at"]
                t_tr = 0
                last_o = None
                c.wait("tensor", "sSC", s_exp)
                for grp in range(0, nblk, 8):
                    gn = min(8, nblk - grp)
                    if s_exp2 and grp >= 8:
                        c.wait("tensor", "sSC", s_exp2)
                    c.wait("tensor", "sVE", s_at)
                    if qt == 0:
                        c.wait("tensor", "sSC", ptb["cp"])
                    for j in range(gn):
                        t_tr = ti(lambda t, a=ab, kk=grp + j, jj=j: t.transpose(
                            PTB[:, jj * 128:(jj + 1) * 128],
                            a[:, kk * 128:(kk + 1) * 128], IDB[:]))
                    c.wait("vector", "sMM", t_tr)
                    s_at = vi(lambda v, g0=grp, gnn=gn: v.tensor_copy(
                        ATv[:, g0 * 128:(g0 + gnn) * 128], PTB[:, 0:gnn * 128]))
                    c.wait("tensor", "sVE", s_at)
                    if grp == 0:
                        c.wait("tensor", "sVE", pv["ot"])
                    for kc in range(grp, grp + gn):
                        fn = (lambda t, kcc=kc, stt=(kc == 0),
                              sp=(kc == nblk - 1):
                              t.matmul(PS[:, 2560:2688],
                                       VBv[:, kcc * 128:(kcc + 1) * 128],
                                       ATv[:, kcc * 128:(kcc + 1) * 128],
                                       start=stt, stop=sp))
                        if kc == nblk - 1:
                            last_o = ti(fn)
                        else:
                            c.on("tensor", fn)
                pv["s_at"] = s_at
                pv["ltr"][qt] = t_tr
                c.wait("vector", "sMM", last_o)
                s_ot = vi(lambda v: v.tensor_copy(OTv[:, :], PS[:, 2560:2688]))
                pv["ot"] = s_ot
                c.wait("tensor", "sVE", s_ot)
                ycols = []
                for dc2 in range(2):
                    if dc2 == 0:
                        c.wait("tensor", "sVE", pv["y"])
                        if qt == 0:
                            c.wait("tensor", "sSC", exch_prev["s1"])
                        ycol = 3072
                    else:
                        slot = atk_pos[0] % 5
                        atk_pos[0] += 1
                        c.wait("tensor", "sSC", ring_free[slot][0])
                        c.wait("tensor", "sVE", ring_free[slot][1])
                        ycol = slot * 512
                    ycols.append(ycol)
                    t_y = ti(lambda t, d2=dc2, yc=ycol: t.matmul(
                        PS[:, yc:yc + 512], OTv[:, :], WOB[:, d2 * 512:(d2 + 1) * 512],
                        start=True, stop=True))
                    c.wait("vector", "sMM", t_y)
                    if dc2 == 0:
                        waitp("vector", st[qt]["vx"])
                    pv["y"] = vi(lambda v, qtt=qt, d2=dc2, pp=par, yc=ycol:
                                 v.scalar_tensor_tensor(
                                     SCR2[:, pp * D + d2 * 512: pp * D + (d2 + 1) * 512],
                                     PS[:, yc:yc + 512], RRS[:, qtt:qtt + 1],
                                     SCR2[:, pp * D + d2 * 512: pp * D + (d2 + 1) * 512],
                                     AluOpType.mult, AluOpType.add))
                    if dc2 == 1:
                        ring_free[slot] = [0, pv["y"]]
                c.wait("scalar", "sVE", pv["y"])
                exch_prev["y"] = pv["y"]
                xstore[par] = sstore(xr_h[qt * 128:(qt + 1) * 128, :],
                                     SCR2[:, par * D:(par + 1) * D], sem=SS[par])

            for qt in range(NT):
                stageA(qt)
                if qt >= 1:
                    stageB(qt - 1)
            stageB(NT - 1)
            ptb["vrd"] = pv["s_at"]
            scr_rd[0] = scrA_rd[0]
            scr_rd[1] = scrA_rd[1]

        wg_sp = [None] * NI
        wu_sp = [None] * NI
        g_wg = gen_gateup(wg_h, wgT_h, WTS, wg_sp)
        g_wd = gen_down(WTS)
        g_wu = gen_gateup(wu_h, wuT_h, UPSTG, wu_sp)
        lngen = itertools.chain(g_wg, g_wd)

        def ln_pump(i):
            for _ in range(3):
                next(lngen, None)

        def gate_chain(it):
            next(g_wu, None)

        if T < 1024:
            g_wu = gen_gateup(wu_h, wuT_h, WTS, wu_sp)
            for _ in lngen:
                pass
            for _ in g_wu:
                pass
            scr_rd[0] = max(scr_rd[0], dqs_rd[0])
            scr_rd[1] = max(scr_rd[1], dqs_rd[1])
            ln_pump = None
            gate_chain = None

        xstore = [("Sa", 0), ("Sb", 0)]

        def ffn(src_res_h, dst_res_h, ln_pump=None, gate_chain=None,
                gate_spills=None, up_spills=None):
            ln_stats(src_res_h, True)
            fin0 = exchange((0, 1), math_r0, 1, 0, defer=True)
            ln_apply(src_res_h, pump=ln_pump)
            scr_rd[0] = max(scr_rd[0], dqs_rd[0])
            scr_rd[1] = max(scr_rd[1], dqs_rd[1])
            fin0()
            quantize_xt()
            dqs_rd[0] = max(scr_rd[0], dqs_rd[0])
            dqs_rd[1] = max(scr_rd[1], dqs_rd[1])
            mm_sweep(wgT_h, True, chain_fn=gate_chain, spills=gate_spills)
            scr_rd[0] = max(scr_rd[0], dqs_rd[0])
            scr_rd[1] = max(scr_rd[1], dqs_rd[1])
            fin1a = exchange((0, 1), math_r1a, 3, 1, defer=True)

            def chain(it):
                if it == 1:
                    fin1a()
                if it >= 1:
                    sig_it(it - 1)

            mm_sweep(wuT_h, False, chain_fn=chain, bar=False, spills=up_spills)
            sig_it(NI - 1)
            exchange((1, 2), math_r1b2, 2, 4)
            o_phase()
            exchange((0, 1), math_r3, 2, 6)
            down_phase(src_res_h, dst_res_h, xstore, rint_col=6)

        def finish_copy():
            for t in range(NT):
                sl = t % 2
                v_l = load(SCR2[:, sl * D:(sl + 1) * D],
                           x_h[t * 128:(t + 1) * 128, :], sem=LS[sl],
                           gate=[xstore[sl], ("sSC", c.val("sSC")),
                                 ("sVE", c.val("sVE")), ("sMM", c.val("sMM"))])
                waitp("scalar", v_l)
                xstore[sl] = sstore(out_h[t * 128:(t + 1) * 128, :],
                                    SCR2[:, sl * D:(sl + 1) * D], sem=SS[sl])

        if upto in ("deq", "ln1", "mm", "sig", "oph"):
            for _ in lngen:
                pass
            for _ in g_wu:
                pass
            scr_rd[0] = max(scr_rd[0], dqs_rd[0])
            scr_rd[1] = max(scr_rd[1], dqs_rd[1])
            if upto != "deq":
                ln_pass(x_h, True)
                exchange((0, 1), math_r0, 1, 0)
                quantize_xt()
            if upto in ("mm", "sig", "oph"):
                mm_sweep(wgT_h, True)
                fin1a = exchange((0, 1), math_r1a, 3, 1, defer=True)

                def chain(it):
                    if it == 1:
                        fin1a()
                    if it >= 1 and upto != "mm":
                        sig_it(it - 1)

                mm_sweep(wuT_h, False, chain_fn=chain, bar=False)
                if upto != "mm":
                    sig_it(NI - 1)
                exchange((1, 2), math_r1b2, 2, 4)
            if upto == "oph":
                o_phase()
                exchange((0, 1), math_r3, 2, 6)
                rint_g(6)
            finish_copy()
        elif upto == "ffn1":
            ffn(x_h, out_h, ln_pump=ln_pump, gate_chain=gate_chain,
                gate_spills=wg_sp, up_spills=wu_sp)
        elif upto == "attn":
            ffn(x_h, xr_h, ln_pump=ln_pump, gate_chain=gate_chain,
                gate_spills=wg_sp, up_spills=wu_sp)
            attention(xstore)
            for t in range(NT):
                sl = t % 2
                v_l = load(SCR2[:, sl * D:(sl + 1) * D],
                           xr_h[t * 128:(t + 1) * 128, :], sem=LS[sl],
                           gate=[xstore[sl], ("sSC", c.val("sSC")), ("sVE", c.val("sVE"))])
                waitp("scalar", v_l)
                xstore[sl] = sstore(out_h[t * 128:(t + 1) * 128, :],
                                    SCR2[:, sl * D:(sl + 1) * D], sem=SS[sl])
        else:
            ffn(x_h, xr_h, ln_pump=ln_pump, gate_chain=gate_chain,
                gate_spills=wg_sp, up_spills=wu_sp)
            attention(xstore)
            ffn(xr_h, out_h)

        c.wait("scalar", "Sa", c.val("Sa"))
        c.wait("scalar", "Sb", c.val("Sb"))
        c.wait("vector", "Sa", c.val("Sa"))
        c.wait("vector", "Sb", c.val("Sb"))

        block = es.enter_context(nc.Block())
        c.emit(block)
    return nc


_NC_CACHE = [None]


def kernel(L0_attn_x, L0_attn_Wq, L0_attn_Wk, L0_attn_Wv, L0_attn_Wo,
           L0_ffn_w_gate, L0_ffn_w_up, L0_ffn_w_down):
    if _NC_CACHE[0] is None:
        _NC_CACHE[0] = build()
    nc = _NC_CACHE[0]
    x = np.ascontiguousarray(np.asarray(L0_attn_x, dtype=np.float32))
    ws = [np.ascontiguousarray(np.asarray(w, dtype=np.float32)) for w in
          (L0_attn_Wq, L0_attn_Wk, L0_attn_Wv, L0_attn_Wo,
           L0_ffn_w_gate, L0_ffn_w_up, L0_ffn_w_down)]
    in_maps = [{"x": x[b], "wq": ws[0], "wk": ws[1], "wv": ws[2], "wo": ws[3],
                "wg": ws[4], "wu": ws[5], "wd": ws[6]} for b in range(8)]
    res = run_bass_kernel_spmd(nc, in_maps, core_ids=list(range(8)))
    return np.stack([res.results[b]["out"] for b in range(8)], axis=0)


# version 23 (final)
# speedup vs baseline: 1.0521x; 1.0046x over previous
import math
import contextlib
import itertools
import numpy as np
import concourse.bass as bass
import concourse.mybir as mybir
from concourse.alu_op_type import AluOpType
from concourse.bass_utils import run_bass_kernel_spmd

F32 = mybir.dt.float32
BF16 = mybir.dt.bfloat16
I32 = mybir.dt.int32
AX = mybir.AxisListType.X
AF = mybir.ActivationFunctionType
MAGIC = 12582912.0
CINV = 1.0 / math.sqrt(128.0)
NEG = -1.0e30

T = 2048
D = 1024
II = 3072
DK = 128
NT = T // 128
ND = D // 128
NI = II // 128
RING = 6


class Ctx:
    def __init__(self, nc, es):
        self.nc = nc
        self.streams = {k: [] for k in ("sync", "gpsimd", "vector", "scalar", "tensor")}
        self.sems = {}
        self.cnt = {}
        for name in ("La", "Lb", "Lw", "Da", "Db", "Sa", "Sb", "Ga", "Gb", "Gc", "Gd", "Gi", "Gwa", "Gwb", "CC", "sMM", "sVE", "sSC"):
            self.sems[name] = es.enter_context(nc.semaphore(name))
            self.cnt[name] = 0

    def on(self, eng, fn):
        self.streams[eng].append(fn)

    def bump(self, name, by=1):
        self.cnt[name] += by
        return self.sems[name], self.cnt[name]

    def val(self, name):
        return self.cnt[name]

    def wait(self, eng, name, value):
        if value is None or value <= 0:
            return
        h = self.sems[name]
        self.on(eng, lambda e, hh=h, v=value: e.wait_ge(hh, v))

    def emit(self, block):
        def run(name):
            def body(e):
                for fn in self.streams[name]:
                    fn(e)
            return body
        block.sync(run("sync"))
        block.gpsimd(run("gpsimd"))
        block.vector(run("vector"))
        block.scalar(run("scalar"))
        block.tensor(run("tensor"))


def build(TT=2048, upto="all"):
    T = TT
    NT = T // 128
    nc = bass.Bass("TRN2", target_bir_lowering=False)

    x_h = nc.dram_tensor("x", [T, D], F32, kind="ExternalInput").ap()
    wq_h = nc.dram_tensor("wq", [D, DK], F32, kind="ExternalInput").ap()
    wk_h = nc.dram_tensor("wk", [D, DK], F32, kind="ExternalInput").ap()
    wv_h = nc.dram_tensor("wv", [D, DK], F32, kind="ExternalInput").ap()
    wo_h = nc.dram_tensor("wo", [DK, D], F32, kind="ExternalInput").ap()
    wg_h = nc.dram_tensor("wg", [II, D], F32, kind="ExternalInput").ap()
    wu_h = nc.dram_tensor("wu", [II, D], F32, kind="ExternalInput").ap()
    wd_h = nc.dram_tensor("wd", [D, II], F32, kind="ExternalInput").ap()
    out_h = nc.dram_tensor("out", [T, D], F32, kind="ExternalOutput").ap()

    xr_h = nc.dram_tensor("xres", [T, D], F32, kind="Internal").ap()
    wgT_h = nc.dram_tensor("wgT", [NI, 128, ND * 128], BF16, kind="Internal").ap()
    wuT_h = nc.dram_tensor("wuT", [NI, 128, ND * 128], BF16, kind="Internal").ap()
    wdT_h = nc.dram_tensor("wdT", [NI, 128, D], BF16, kind="Internal").ap()
    ups_h = nc.dram_tensor("ups", [NI, 128, T], BF16, kind="Internal").ap()
    cci_t = nc.dram_tensor("cci", [1, 8], F32, kind="Internal")
    cco_t = nc.dram_tensor("cco", [1, 8], F32, kind="Internal")
    cci = cci_t.ap()
    cco = cco_t.ap()

    es = contextlib.ExitStack()

    def sbuf(name, shape, dt=F32):
        return es.enter_context(nc.sbuf_tensor(name, list(shape), dt))

    SW = max(T, 2 * D)
    G = sbuf("G", (128, NI * T), BF16)
    XT = sbuf("XT", (128, ND * T), BF16)
    ARN = sbuf("ARN", (128, max(16384, 3072 + 8 * T + 1024)), BF16)
    SCR = sbuf("SCR", (128, SW))
    SCR3 = sbuf("SCR3", (128, SW))
    SCR2 = sbuf("SCR2", (128, 2 * D))
    XL = sbuf("XL", (128, D))
    IOT = sbuf("IOT", (128, 128), I32)
    IDF = sbuf("IDF", (128, 128))
    IDB = sbuf("IDB", (128, 128), BF16)
    MASKD = sbuf("MASKD", (128, 128))
    TRS = sbuf("TRS", (128, 128))
    TCOL = sbuf("TCOL", (128, 1))
    CCR = sbuf("CCR", (1, 8))
    RS = sbuf("RS", (1, 16))
    CROW = sbuf("CROW", (1, 8))
    CONST = sbuf("CONST", (128, 16))
    ONE1 = sbuf("ONE1", (1, 128))
    MAGP = sbuf("MAGP", (128, 1))
    EPSP = sbuf("EPSP", (128, 1))
    GMAX = sbuf("GMAX", (128, 4))
    TMPC = sbuf("TMPC", (128, 1))
    BMX = sbuf("BMX", (128, 32))
    BSC = sbuf("BSC", (128, 32))
    BRC = sbuf("BRC", (128, 32))
    ST_SUM = sbuf("ST_SUM", (128, 2))
    ST_SQ = sbuf("ST_SQ", (128, 2))
    ST_MX = sbuf("ST_MX", (128, 2))
    ST_MN = sbuf("ST_MN", (128, 2))
    ST_M2 = sbuf("ST_M2", (128, 2))
    ST_SD = sbuf("ST_SD", (128, 2))
    ST_RS = sbuf("ST_RS", (128, 2))
    ST_NB = sbuf("ST_NB", (128, 2))
    MROW = sbuf("MROW", (128, 1))
    BROW = sbuf("BROW", (128, 1))
    RSUM = sbuf("RSUM", (128, 1))
    S_SUM = sbuf("S_SUM", (128, NT))
    S_SQ = sbuf("S_SQ", (128, NT))
    S_M2 = sbuf("S_M2", (128, NT))
    S_MX = sbuf("S_MX", (128, NT))
    S_MN = sbuf("S_MN", (128, NT))
    S_SD = sbuf("S_SD", (128, NT))
    S_RS = sbuf("S_RS", (128, NT))
    S_NB = sbuf("S_NB", (128, NT))
    MROWB = sbuf("MROWB", (128, 8))
    MROW1 = sbuf("MROW1", (128, 2))
    BROWB = sbuf("BROWB", (128, 2))
    RSUMB = sbuf("RSUMB", (128, 2))
    RS2B = sbuf("RS2B", (128, 2))
    RSTT = sbuf("RSTT", (128, 2))
    RRS = sbuf("RRS", (128, NT))
    R11 = sbuf("R11", (1, 1))
    R12 = sbuf("R12", (1, 1))

    PS = es.enter_context(nc.psum_tensor("PS", [128, 3584], F32))
    PTB = es.enter_context(nc.psum_tensor("PTB", [128, 1024], BF16))

    def arn(a, b):
        return ARN[:, a:b]
    WTS = [arn(0, 1024), arn(1024, 2048)]
    WQB = arn(2048, 3072)
    o = 3072
    UPSTG = [arn(o, o + T), arn(o + T, o + 2 * T)]
    UPLD = [arn(o + 2 * T, o + 3 * T), arn(o + 3 * T, o + 4 * T)]
    KUBp = [UPSTG[0], UPSTG[1]]
    WDS = arn(0, NI * 512)
    QTv = arn(0, T)
    KTv = arn(T, 2 * T)
    VBv = arn(2 * T, 2 * T + NT * 128)
    ABv = arn(2 * T + NT * 128, 3 * T + NT * 128)
    ATv = arn(3 * T + NT * 128, 4 * T + NT * 128)
    o2 = 4 * T + NT * 128
    OTv = arn(o2, o2 + 128)
    WQKV = arn(o2 + 128, o2 + 128 + 3072)
    WOB = arn(o2 + 128 + 3072, o2 + 128 + 4096)
    ABv2 = arn(o2 + 128 + 4096, o2 + 128 + 4096 + T)
    WQB2 = arn(o2 + 128 + 4096 + T, o2 + 128 + 4096 + T + 1024)
    WQBp = [WQB, WQB2]

    c = Ctx(nc, es)
    ring_free = [[0, 0] for _ in range(RING)]
    ring_pos = [0]

    with es:
        SELFGATE = {"La", "Lb", "Da", "Db", "Sa", "Sb", "Ga", "Gb", "Gc", "Gd"}

        def load(dst, src, sem="Lw", gate=None):
            if sem in SELFGATE and c.val(sem) > 0:
                c.wait("sync", sem, c.val(sem))
            if gate:
                for nm, v in gate:
                    c.wait("sync", nm, v)
            h, v = c.bump(sem, 16)
            c.on("sync", lambda e, d=dst, s=src: e.dma_start(d, s).then_inc(h, 16))
            return (sem, v)

        def gstore(dst, src, sem="Ga", gate=None):
            if sem in SELFGATE and c.val(sem) > 0:
                c.wait("gpsimd", sem, c.val(sem))
            if gate:
                for nm, v in gate:
                    c.wait("gpsimd", nm, v)
            h, v = c.bump(sem, 16)
            c.on("gpsimd", lambda e, d=dst, s=src: e.dma_start(d, s).then_inc(h, 16))
            return (sem, v)

        def sstore(dst, src, sem="Sa"):
            if sem in SELFGATE and c.val(sem) > 0:
                c.wait("scalar", sem, c.val(sem))
            h, v = c.bump(sem, 16)
            c.on("scalar", lambda e, d=dst, s=src: e.dma_start(d, s).then_inc(h, 16))
            return (sem, v)

        def waitp(eng, pair):
            if pair and pair[1] > 0:
                c.wait(eng, pair[0], pair[1])

        LS = ["La", "Lb"]
        SS = ["Sa", "Sb"]
        GS = ["Ga", "Gb"]
        GW = ["Gwa", "Gwb"]

        def vi(fn):
            h, v = c.bump("sVE")
            if v > 1:
                c.on("vector", lambda e, hh=h, vv=v - 1: e.wait_ge(hh, vv))
            c.on("vector", lambda e, f=fn, hh=h: f(e).then_inc(hh, 1))
            return v

        def si(fn):
            h, v = c.bump("sSC")
            if v > 1:
                c.on("scalar", lambda e, hh=h, vv=v - 1: e.wait_ge(hh, vv))
            c.on("scalar", lambda e, f=fn, hh=h: f(e).then_inc(hh, 1))
            return v

        def ti(fn):
            h, v = c.bump("sMM")
            c.on("tensor", lambda e, f=fn, hh=h: f(e).then_inc(hh, 1))
            return v

        def barrier_t():
            c.wait("tensor", "sSC", c.val("sSC"))
            c.wait("tensor", "sVE", c.val("sVE"))

        c.on("gpsimd", lambda e: e.iota(IOT[:], pattern=[[1, 128]], base=0,
                                        channel_multiplier=-1).then_inc(c.sems["Gi"], 16))
        c.cnt["Gi"] += 16
        v_iota = c.val("Gi")

        vi(lambda v: v.memset(MAGP[:], MAGIC))
        vi(lambda v: v.memset(EPSP[:], 1e-5))
        vi(lambda v: v.memset(ONE1[:], 1.0))
        vi(lambda v: v.memset(TRS[:], 0.0))
        vi(lambda v: v.memset(GMAX[:], 0.0))
        c.wait("vector", "Gi", v_iota)
        vi(lambda v: v.tensor_scalar(IDF[:], IOT[:], 0.0, None, AluOpType.is_equal))
        vi(lambda v: v.tensor_scalar(IDB[:], IOT[:], 0.0, None, AluOpType.is_equal))
        vi(lambda v: v.tensor_scalar(MASKD[:], IOT[:], 0.0, NEG, AluOpType.is_gt,
                                     AluOpType.mult))

        def load_cast(w_src, dst, ncols, slot):
            v_l = load(SCR2[:, slot * D: slot * D + ncols]
                       .rearrange("p (cc k) -> p cc k", k=128), w_src, sem=LS[slot])
            waitp("scalar", v_l)
            return si(lambda s, d=dst, sl=slot, n=ncols:
                      s.activation(d[:, 0:n], SCR2[:, sl * D: sl * D + n], AF.Copy))


        exch_prev = {"s1": 0, "y": 0}

        def exchange(span, math_fn, nconst, const_col, defer=False):
            c0, K = span
            v1x = vi(lambda v: v.tensor_copy(TRS[:, 0:K], GMAX[:, c0:c0 + K]))
            vi(lambda v: v.memset(GMAX[:, c0:c0 + K], 0.0))
            c.wait("tensor", "sVE", v1x)
            c.wait("tensor", "sSC", exch_prev["s1"])
            c.wait("tensor", "sVE", exch_prev["y"])
            t1 = ti(lambda t: t.transpose(PS[:, 3072:3200], TRS[:], IDF[:]))
            c.wait("vector", "sMM", t1)
            v2x = vi(lambda v: v.tensor_reduce(TCOL[:], PS[:, 3072:3200], AX,
                                               AluOpType.max))
            p_ci = gstore(cci[0:1, 0:8], TCOL[0:8, 0:1], sem="Gd", gate=[("sVE", v2x)])
            waitp("gpsimd", p_ci)
            hG, v_cc = c.bump("CC", 1)
            c.on("gpsimd", lambda e, hh=hG: e.collective_compute(
                "AllReduce", AluOpType.max, replica_groups=[list(range(8))],
                ins=[cci_t.ap().opt()], outs=[cco_t.ap().opt()]).then_inc(hh, 1))
            c.wait("gpsimd", "CC", v_cc)
            hG2, v_co = c.bump("Gc", 16)
            c.on("gpsimd", lambda e, hh=hG2: e.dma_start(CCR[:, :], cco[:, :]).then_inc(hh, 16))
            c.wait("vector", "Gc", v_co)
            v3x = 0
            for op in math_fn:
                v3x = vi(op)

            def fin(v3=v3x, n=nconst, cc=const_col):
                c.wait("tensor", "sVE", v3)
                t2 = ti(lambda t: t.matmul(PS[:, 3200:3200 + n], ONE1[:, :],
                                           CROW[0:1, 0:n], start=True, stop=True))
                c.wait("scalar", "sMM", t2)
                exch_prev["s1"] = si(lambda s:
                                     s.activation(CONST[:, cc:cc + n],
                                                  PS[:, 3200:3200 + n], AF.Copy))
            if defer:
                return fin
            fin()

        def rmul(v, dst, a, b):
            return v.tensor_tensor(dst, a, b, AluOpType.mult)

        math_r0 = [
            lambda v: v.tensor_scalar(RS[0:1, 0:1], CCR[0:1, 0:1], 1.0 / 32767.0,
                                      1e-12, AluOpType.mult, AluOpType.max),
            lambda v: v.reciprocal(CROW[0:1, 0:1], RS[0:1, 0:1]),
        ]

        math_r1a = [
            lambda v: rmul(v, R11[0:1, 0:1], CCR[0:1, 0:1], RS[0:1, 0:1]),
            lambda v: v.tensor_scalar(RS[0:1, 1:2], R11[0:1, 0:1], 1.0 / 32767.0,
                                      1e-12, AluOpType.mult, AluOpType.max),
            lambda v: v.reciprocal(R11[0:1, 0:1], RS[0:1, 1:2]),
            lambda v: rmul(v, CROW[0:1, 0:1], R11[0:1, 0:1], RS[0:1, 0:1]),
            lambda v: v.tensor_copy(CROW[0:1, 1:2], RS[0:1, 1:2]),
            lambda v: v.tensor_scalar(CROW[0:1, 2:3], RS[0:1, 1:2], -MAGIC, None,
                                      AluOpType.mult),
            lambda v: v.tensor_scalar(RS[0:1, 3:4], RS[0:1, 1:2], 1.0 / 65536.0,
                                      None, AluOpType.mult),
        ]

        math_r1b2 = [
            lambda v: rmul(v, R11[0:1, 0:1], CCR[0:1, 0:1], RS[0:1, 0:1]),
            lambda v: v.tensor_scalar(RS[0:1, 2:3], R11[0:1, 0:1], 1.0 / 32767.0,
                                      1e-12, AluOpType.mult, AluOpType.max),
            lambda v: v.reciprocal(R12[0:1, 0:1], RS[0:1, 2:3]),
            lambda v: rmul(v, CROW[0:1, 0:1], R12[0:1, 0:1], RS[0:1, 0:1]),
            lambda v: rmul(v, R11[0:1, 0:1], CCR[0:1, 1:2], RS[0:1, 3:4]),
            lambda v: v.tensor_scalar(RS[0:1, 4:5], R11[0:1, 0:1], 1.0 / 32767.0,
                                      1e-12, AluOpType.mult, AluOpType.max),
            lambda v: v.reciprocal(R11[0:1, 0:1], RS[0:1, 4:5]),
            lambda v: rmul(v, CROW[0:1, 1:2], R11[0:1, 0:1], RS[0:1, 3:4]),
            lambda v: rmul(v, RS[0:1, 5:6], RS[0:1, 4:5], RS[0:1, 2:3]),
        ]

        math_r3 = [
            lambda v: rmul(v, R11[0:1, 0:1], CCR[0:1, 0:1], RS[0:1, 5:6]),
            lambda v: v.tensor_scalar(R12[0:1, 0:1], R11[0:1, 0:1], 1.0 / 32767.0,
                                      1e-12, AluOpType.mult, AluOpType.max),
            lambda v: v.reciprocal(R11[0:1, 0:1], R12[0:1, 0:1]),
            lambda v: rmul(v, CROW[0:1, 0:1], R11[0:1, 0:1], RS[0:1, 5:6]),
            lambda v: v.tensor_scalar(CROW[0:1, 1:2], R12[0:1, 0:1], 0.5, None,
                                      AluOpType.mult),
        ]


        nb = D // 32
        ptb = {"ttr": 0, "cp": 0, "vrd": 0}
        dqs_rd = [0, 0]
        dqs_sp = [None, None]
        swp = {"mm": [0, 0]}
        scr_rd = [0, 0]

        DQS = ["Da", "Db"]

        def dq_load(src_ap, sl):
            dst32 = (SCR if sl == 0 else SCR3)[:, D:2 * D]
            return load(dst32, src_ap, sem=DQS[sl],
                        gate=[("sVE", dqs_rd[sl])] if dqs_rd[sl] else None)

        def dequant_tile(v_l, sl):
            dst32 = (SCR if sl == 0 else SCR3)[:, D:2 * D]
            waitp("vector", v_l)
            vi(lambda v, d=dst32: v.tensor_reduce(
                BMX[:, 0:nb], d.rearrange("p (b w) -> p b w", w=32),
                AX, AluOpType.max, apply_absolute_value=True))
            vi(lambda v: v.tensor_scalar(BSC[:, 0:nb], BMX[:, 0:nb], 1.0 / 7.0,
                                         1e-12, AluOpType.mult, AluOpType.max))
            vi(lambda v: v.reciprocal(BRC[:, 0:nb], BSC[:, 0:nb]))
            vi(lambda v, d=dst32: v.tensor_tensor(
                d.rearrange("p (b w) -> p b w", w=32),
                d.rearrange("p (b w) -> p b w", w=32),
                BRC[:, 0:nb].broadcast_to([128, nb, 32]), AluOpType.mult))
            vi(lambda v, d=dst32: v.tensor_scalar(
                d, d, MAGIC, MAGIC, AluOpType.add, AluOpType.subtract))
            c.wait("vector", "sMM", ptb["ttr"])
            v_dq = vi(lambda v, d=dst32, ss=sl: v.scalar_tensor_tensor(
                WQBp[ss][:, 0:D].rearrange("p (b w) -> p b w", w=32),
                d.rearrange("p (b w) -> p b w", w=32), 0.0,
                BSC[:, 0:nb].broadcast_to([128, nb, 32]),
                AluOpType.add, AluOpType.mult))
            dqs_rd[sl] = v_dq
            return v_dq

        def transpose_wqb(v_dq, sl):
            c.wait("tensor", "sVE", v_dq)
            c.wait("tensor", "sSC", ptb["cp"])
            c.wait("tensor", "sVE", ptb["vrd"])
            t_tr = None
            for b in range(ND):
                t_tr = ti(lambda t, bb=b, ss=sl: t.transpose(
                    PTB[:, bb * 128:(bb + 1) * 128],
                    WQBp[ss][:, bb * 128:(bb + 1) * 128], IDB[:]))
            ptb["ttr"] = t_tr
            return t_tr

        def gen_gateup(w_src, wT_dst, stage, spills):
            def tail(it, v_dq):
                sl = it % 2
                transpose_wqb(v_dq, sl)
                c.wait("scalar", "sMM", ptb["ttr"])
                waitp("scalar", dqs_sp[sl])
                s_cp = si(lambda s, ss=sl: s.activation(
                    stage[ss][:, 0:D], PTB[:, :], AF.Copy))
                ptb["cp"] = s_cp
                sp = gstore(wT_dst[it], stage[sl][:, 0:D], sem=GW[sl],
                            gate=[("sSC", s_cp)])
                dqs_sp[sl] = sp
                spills[it] = sp

            vl = [dq_load(w_src[0:128, :], 0), dq_load(w_src[128:256, :], 1)]
            pend = None
            for it in range(NI):
                sl = it % 2
                v_dq = dequant_tile(vl[sl], sl)
                if it + 2 < NI:
                    vl[sl] = dq_load(w_src[(it + 2) * 128:(it + 3) * 128, :], sl)
                if pend is not None:
                    tail(*pend)
                pend = (it, v_dq)
                if it == NI - 1:
                    tail(*pend)
                yield

        def gen_down(stage):
            def wsrc(q):
                dp, ic = divmod(q, 3)
                return wd_h[dp * 128:(dp + 1) * 128, ic * 1024:(ic + 1) * 1024]

            def tail(q, v_dq):
                dp, ic = divmod(q, 3)
                sl = q % 2
                transpose_wqb(v_dq, sl)
                c.wait("scalar", "sMM", ptb["ttr"])
                waitp("scalar", dqs_sp[sl])
                s_cp = si(lambda s, ss=sl: s.activation(
                    stage[ss][:, 0:D], PTB[:, :], AF.Copy))
                ptb["cp"] = s_cp
                for icb in range(ND):
                    it = ic * 8 + icb
                    dqs_sp[sl] = gstore(wdT_h[it, :, dp * 128:(dp + 1) * 128],
                                        stage[sl][:, icb * 128:(icb + 1) * 128],
                                        sem=GW[sl],
                                        gate=[("sSC", s_cp)] if icb == 0 else None)

            vl = [dq_load(wsrc(0), 0), dq_load(wsrc(1), 1)]
            pend = None
            for q in range(24):
                sl = q % 2
                v_dq = dequant_tile(vl[sl], sl)
                if q + 2 < 24:
                    vl[sl] = dq_load(wsrc(q + 2), sl)
                if pend is not None:
                    tail(*pend)
                pend = (q, v_dq)
                if q == 23:
                    tail(*pend)
                yield

        ln_gate = [[], []]
        ln_drv = [0]

        def ln_stats(src_res_h, with_max):
            gate0 = [("Sa", c.val("Sa")), ("Sb", c.val("Sb"))]
            c.wait("scalar", "sMM", c.val("sMM"))
            c.wait("scalar", "sVE", scr_rd[0])
            rdA = [0, 0]
            sqA = [0, 0]
            for t in range(NT):
                sl = t % 2
                dst = SCR2[:, sl * D:(sl + 1) * D]
                gate = (list(gate0) if t < 2
                        else [("sVE", rdA[sl]), ("sSC", sqA[sl])])
                v_l = load(dst, src_res_h[t * 128:(t + 1) * 128, :], sem=LS[sl],
                           gate=gate)
                waitp("vector", v_l)
                c.wait("vector", "sSC", sqA[sl])
                v_r = vi(lambda v, d=dst, tt=t: v.tensor_reduce(
                    S_SUM[:, tt:tt + 1], d, AX, AluOpType.add))
                if with_max:
                    vi(lambda v, d=dst, tt=t: v.tensor_reduce(
                        S_MX[:, tt:tt + 1], d, AX, AluOpType.max))
                    v_r = vi(lambda v, d=dst, tt=t: v.tensor_reduce(
                        S_MN[:, tt:tt + 1], d, AX, AluOpType.min))
                rdA[sl] = v_r
                waitp("scalar", v_l)
                sqA[sl] = si(lambda s, d=dst, tt=t: s.activation(
                    SCR[:, 0:D], d, AF.Square, accum_out=S_SQ[:, tt:tt + 1]))
            ln_gate[0] = [("sVE", rdA[0]), ("sSC", sqA[0])]
            ln_gate[1] = [("sVE", rdA[1]), ("sSC", sqA[1])]
            c.wait("vector", "sSC", sqA[0])
            c.wait("vector", "sSC", sqA[1])
            vi(lambda v: v.tensor_scalar(S_SUM[:, 0:NT], S_SUM[:, 0:NT], 1.0 / D,
                                         None, AluOpType.mult))
            vi(lambda v: v.tensor_scalar(S_SQ[:, 0:NT], S_SQ[:, 0:NT], 1.0 / D,
                                         None, AluOpType.mult))
            vi(lambda v: v.tensor_tensor(S_M2[:, 0:NT], S_SUM[:, 0:NT],
                                         S_SUM[:, 0:NT], AluOpType.mult))
            v_var = vi(lambda v: v.tensor_tensor(S_SQ[:, 0:NT], S_SQ[:, 0:NT],
                                                 S_M2[:, 0:NT], AluOpType.subtract))
            c.wait("scalar", "sVE", v_var)
            s_sd = si(lambda s: s.activation(S_SD[:, 0:NT], S_SQ[:, 0:NT], AF.Sqrt,
                                             bias=EPSP[:]))
            c.wait("vector", "sSC", s_sd)
            vi(lambda v: v.reciprocal(S_RS[:, 0:NT], S_SD[:, 0:NT]))
            ln_drv[0] = vi(lambda v: v.scalar_tensor_tensor(
                S_NB[:, 0:NT], S_SUM[:, 0:NT], -1.0, S_RS[:, 0:NT],
                AluOpType.mult, AluOpType.mult))
            if with_max:
                vi(lambda v: v.tensor_tensor(S_MX[:, 0:NT], S_MX[:, 0:NT],
                                             S_SUM[:, 0:NT], AluOpType.subtract))
                vi(lambda v: v.tensor_tensor(S_MN[:, 0:NT], S_SUM[:, 0:NT],
                                             S_MN[:, 0:NT], AluOpType.subtract))
                vi(lambda v: v.tensor_tensor(S_MX[:, 0:NT], S_MX[:, 0:NT],
                                             S_MN[:, 0:NT], AluOpType.max))
                vi(lambda v: v.tensor_tensor(S_MX[:, 0:NT], S_MX[:, 0:NT],
                                             S_RS[:, 0:NT], AluOpType.mult))
                vi(lambda v: v.tensor_reduce(GMAX[:, 0:1], S_MX[:, 0:NT], AX,
                                             AluOpType.max))

        def ln_apply(src_res_h, pump=None):
            xln_done = [0, 0]
            ln_ttr = 0
            for t in range(NT):
                sl = t % 2
                dst = SCR2[:, sl * D:(sl + 1) * D]
                gate = (list(ln_gate[sl]) if t < 2
                        else [("sSC", xln_done[sl])])
                v_l = load(dst, src_res_h[t * 128:(t + 1) * 128, :], sem=LS[sl],
                           gate=gate)
                waitp("scalar", v_l)
                if t == 0:
                    c.wait("scalar", "sVE", ln_drv[0])
                c.wait("scalar", "sMM", ln_ttr)
                s_ln = si(lambda s, d=dst, tt=t: s.activation(
                    XL[:, :], d, AF.Identity, scale=S_RS[:, tt:tt + 1],
                    bias=S_NB[:, tt:tt + 1]))
                xln_done[sl] = s_ln
                for half in range(2):
                    slot = ring_pos[0] % RING
                    ring_pos[0] += 1
                    c.wait("tensor", "sSC", ring_free[slot][0])
                    c.wait("tensor", "sVE", ring_free[slot][1])
                    if half == 0:
                        c.wait("tensor", "sSC", s_ln)
                    t_tr = None
                    for b in range(4):
                        t_tr = ti(lambda tt2, bb=b, hh=half, slo=slot: tt2.transpose(
                            PS[:, slo * 512 + bb * 128: slo * 512 + (bb + 1) * 128],
                            XL[:, (hh * 4 + bb) * 128:(hh * 4 + bb + 1) * 128],
                            IDF[:]))
                    ln_ttr = t_tr
                    c.wait("scalar", "sMM", t_tr)
                    s_cp = si(lambda s, ttt=t, hh=half, slo=slot: s.activation(
                        XT[:].rearrange("p (cc t2) -> p cc t2", t2=T)
                        [:, hh * 4:(hh + 1) * 4, ttt * 128:(ttt + 1) * 128],
                        PS[:, slo * 512:(slo + 1) * 512]
                        .rearrange("p (cc j) -> p cc j", j=128), AF.Copy))
                    ring_free[slot] = [s_cp, 0]
                if pump is not None:
                    pump(t)

        def ln_pass(src_res_h, with_max, pump=None):
            ln_stats(src_res_h, with_max)
            ln_apply(src_res_h, pump=pump)

        def quantize_xt():
            for cb in range(ND):
                sl = cb % 2
                scr = SCR if sl == 0 else SCR3
                c.wait("scalar", "sVE", scr_rd[sl])
                sA = si(lambda s, cbb=cb, d=scr: s.activation(
                    d[:, 0:T], XT[:, cbb * T:(cbb + 1) * T], AF.Identity,
                    scale=CONST[:, 0:1], bias=MAGP[:]))
                c.wait("vector", "sSC", sA)
                scr_rd[sl] = vi(lambda v, cbb=cb, d=scr: v.tensor_scalar(
                    XT[:, cbb * T:(cbb + 1) * T], d[:, 0:T], MAGIC, None,
                    AluOpType.subtract))

        def mm_sweep(wT_h, is_gate, chain_fn=None, bar=True, spills=None):
            col = 0 if is_gate else 1
            if bar:
                barrier_t()
            gw = [("Gwa", c.val("Gwa")), ("Gwb", c.val("Gwb"))]
            up_spill = [("Ga", c.val("Ga")), ("Gb", c.val("Gb"))]
            up_red = [0, 0]
            v_loads = [None] * NI
            mm_done = swp["mm"]

            def lgate(j, base):
                if spills is not None and spills[j] is not None:
                    return base + [spills[j]]
                return base

            v_loads[0] = load(WTS[0], wT_h[0], sem="La",
                              gate=lgate(0, gw + [("sMM", mm_done[0])]))
            v_loads[1] = load(WTS[1], wT_h[1], sem="Lb",
                              gate=lgate(1, gw + [("sMM", mm_done[1])]))
            ntc = T // 512
            for it in range(NI):
                sl = it % 2
                waitp("tensor", v_loads[it])
                if is_gate:
                    dstv = G[:, it * T:(it + 1) * T]
                else:
                    dstv = UPSTG[it % 2]
                last_mm = None
                pair_mm = None
                last_se = None
                slots = []
                for tc in range(ntc):
                    slot = ring_pos[0] % RING
                    ring_pos[0] += 1
                    c.wait("tensor", "sSC", ring_free[slot][0])
                    c.wait("tensor", "sVE", ring_free[slot][1])
                    slots.append(slot)
                    for cb in range(ND):
                        is_last = (cb == ND - 1)
                        fn = (lambda t, sll=sl, cbb=cb, tcc=tc, slo=slot,
                              st=(cb == 0), sp=is_last:
                              t.matmul(PS[:, slo * 512:(slo + 1) * 512],
                                       WTS[sll][:, cbb * 128:(cbb + 1) * 128],
                                       XT[:, cbb * T + tcc * 512: cbb * T + (tcc + 1) * 512],
                                       start=st, stop=sp))
                        if is_last:
                            last_mm = ti(fn)
                        else:
                            c.on("tensor", fn)
                    if tc % 2 == 1 or tc == ntc - 1:
                        nb = 2 if tc % 2 == 1 else 1
                        c.wait("scalar", "sMM", last_mm)
                        if tc < 2 and not is_gate:
                            c.wait("scalar", up_spill[sl][0], up_spill[sl][1])
                            c.wait("scalar", "sVE", up_red[sl])
                            if it < 2:
                                for nm, vv in gw:
                                    c.wait("scalar", nm, vv)
                        last_se = si(lambda s, dd=dstv, tcc=tc, nbb=nb,
                                     slo=slots[tc - nb + 1]: s.activation(
                                         dd[:, (tcc - nbb + 1) * 512:(tcc + 1) * 512],
                                         PS[:, slo * 512:(slo + nbb) * 512], AF.Copy))
                        for j in range(nb):
                            ring_free[slots[tc - j]] = [last_se, 0]
                c.wait("vector", "sSC", last_se)
                v_r1 = vi(lambda v, dd=dstv: v.tensor_reduce(
                    TMPC[:], dd[:, 0:T], AX, AluOpType.max, apply_absolute_value=True))
                up_red[sl] = v_r1
                vi(lambda v, cc=col: v.tensor_tensor(
                    GMAX[:, cc:cc + 1], GMAX[:, cc:cc + 1], TMPC[:], AluOpType.max))
                mm_done[sl] = last_mm
                if not is_gate:
                    up_spill[sl] = gstore(ups_h[it], UPSTG[sl], sem=GS[sl],
                                          gate=[("sSC", last_se)])
                if it + 2 < NI:
                    v_loads[it + 2] = load(WTS[sl], wT_h[it + 2], sem=LS[sl],
                                           gate=lgate(it + 2, [("sMM", mm_done[sl])]))
                if chain_fn is not None:
                    chain_fn(it)

        sigst = {"kub": [0, 0]}

        def sig_it(it):
            sl = it % 2
            scr = SCR if sl == 0 else SCR3
            kb = UPLD[sl]
            gslice = G[:, it * T:(it + 1) * T]
            c.wait("scalar", "sVE", scr_rd[sl])
            sA = si(lambda s, gs=gslice, d=scr: s.activation(
                d[:, 0:T], gs, AF.Identity, scale=CONST[:, 1:2], bias=MAGP[:]))
            c.wait("scalar", "sVE", sigst["kub"][sl])
            sB = si(lambda s, d=scr, k=kb: s.activation(
                k[:, 0:T], d[:, 0:T], AF.Sigmoid, scale=CONST[:, 2:3],
                bias=CONST[:, 3:4]))
            c.wait("vector", "sSC", sA)
            v_kg = vi(lambda v, gs=gslice, d=scr: v.tensor_scalar(
                gs, d[:, 0:T], MAGIC, None, AluOpType.subtract))
            c.wait("vector", "sSC", sB)
            vi(lambda v, d=scr, k=kb: v.tensor_scalar(
                d[:, 0:T], k[:, 0:T], 65536.0, MAGIC,
                AluOpType.mult, AluOpType.add))
            v_sq = vi(lambda v, d=scr, k=kb: v.tensor_scalar(
                k[:, 0:T], d[:, 0:T], MAGIC, 65535.0,
                AluOpType.subtract, AluOpType.min))
            scr_rd[sl] = v_sq
            v_tt = vi(lambda v, gs=gslice, k=kb: v.tensor_tensor(
                gs, gs, k[:, 0:T], AluOpType.mult))
            sigst["kub"][sl] = v_tt
            vi(lambda v, gs=gslice: v.tensor_reduce(
                TMPC[:], gs, AX, AluOpType.max, apply_absolute_value=True))
            vi(lambda v: v.tensor_tensor(GMAX[:, 2:3], GMAX[:, 2:3], TMPC[:],
                                         AluOpType.max))

        def rint_g(col):
            for it in range(NI):
                sl = it % 2
                scr = SCR if sl == 0 else SCR3
                gslice = G[:, it * T:(it + 1) * T]
                c.wait("scalar", "sVE", scr_rd[sl])
                sA = si(lambda s, gs=gslice, d=scr, cc=col: s.activation(
                    d[:, 0:T], gs, AF.Identity, scale=CONST[:, cc:cc + 1], bias=MAGP[:]))
                c.wait("vector", "sSC", sA)
                scr_rd[sl] = vi(lambda v, gs=gslice, d=scr: v.tensor_scalar(
                    gs, d[:, 0:T], MAGIC, None, AluOpType.subtract))

        def rint_chunk(col, tc, cw, it_lo=0, it_hi=None):
            last = 0
            for it in range(it_lo, NI if it_hi is None else it_hi):
                sl = it % 2
                scr = SCR if sl == 0 else SCR3
                gs = G[:, it * T + tc * cw: it * T + (tc + 1) * cw]
                c.wait("scalar", "sVE", scr_rd[sl])
                sA = si(lambda s, g=gs, d=scr, cc=col, w=cw: s.activation(
                    d[:, 0:w], g, AF.Identity, scale=CONST[:, cc:cc + 1], bias=MAGP[:]))
                c.wait("vector", "sSC", sA)
                last = vi(lambda v, g=gs, d=scr, w=cw: v.tensor_scalar(
                    g, d[:, 0:w], MAGIC, None, AluOpType.subtract))
                scr_rd[sl] = last
            return last

        def o_phase():
            gate0 = [("Ga", c.val("Ga")), ("Gb", c.val("Gb"))]
            v_loads = [None] * NI
            sgate = [("sSC", c.val("sSC")), ("sVE", c.val("sVE"))]
            v_loads[0] = load(UPLD[0], ups_h[0], sem="La", gate=gate0 + sgate)
            v_loads[1] = load(UPLD[1], ups_h[1], sem="Lb", gate=sgate)
            ost = {}

            def oA(it):
                sl = it % 2
                scr = SCR if sl == 0 else SCR3
                kb = KUBp[sl]
                waitp("scalar", v_loads[it])
                c.wait("scalar", "sVE", scr_rd[sl])
                if it < 2:
                    for nm, vv in gate0:
                        c.wait("scalar", nm, vv)
                sA = si(lambda s, d=scr, ss=sl: s.activation(
                    d[:, 0:T], UPLD[ss][:, 0:T], AF.Identity, scale=CONST[:, 4:5],
                    bias=MAGP[:]))
                c.wait("vector", "sSC", sA)
                v_ku = vi(lambda v, d=scr, k=kb: v.tensor_scalar(
                    k[:, 0:T], d[:, 0:T], MAGIC, None, AluOpType.subtract))
                ost[it] = v_ku
                if it + 2 < NI:
                    v_loads[it + 2] = load(UPLD[sl], ups_h[it + 2], sem=LS[sl],
                                           gate=[("sSC", sA)])

            def oB(it):
                sl = it % 2
                scr = SCR if sl == 0 else SCR3
                kb = KUBp[sl]
                gslice = G[:, it * T:(it + 1) * T]
                c.wait("scalar", "sVE", ost[it])
                sA5 = si(lambda s, d=scr, gs=gslice: s.activation(
                    d[:, 0:T], gs, AF.Identity, scale=CONST[:, 5:6], bias=MAGP[:]))
                c.wait("vector", "sSC", sA5)
                v_tt = vi(lambda v, gs=gslice, d=scr, k=kb: v.scalar_tensor_tensor(
                    gs, d[:, 0:T], -MAGIC, k[:, 0:T],
                    AluOpType.add, AluOpType.mult))
                scr_rd[sl] = v_tt
                vi(lambda v, gs=gslice: v.tensor_reduce(
                    TMPC[:], gs, AX, AluOpType.max, apply_absolute_value=True))
                vi(lambda v: v.tensor_tensor(GMAX[:, 0:1], GMAX[:, 0:1], TMPC[:],
                                             AluOpType.max))

            for it in range(NI):
                oA(it)
                if it >= 1:
                    oB(it - 1)
            oB(NI - 1)

        def down_phase(src_res_h, dst_res_h, xstore, rint_col=None):
            barrier_t()
            c.wait("vector", "sSC", c.val("sSC"))
            wds_gate = [("sVE", c.val("sVE")), ("sSC", c.val("sSC")),
                        ("Gwa", c.val("Gwa")), ("Gwb", c.val("Gwb"))]
            ntc2 = max(1, T // 512)
            tpc = NT // ntc2
            ipt = NI
            rc_done = [0] * (ntc2 + 1)
            if rint_col is not None:
                rc_done[0] = rint_chunk(rint_col, 0, min(512, T), 0, NI)
            for dc in range(2):
                v_wl = 0
                for it in range(NI):
                    g = wds_gate if (it == 0) else None
                    if dc == 1 and it == 0:
                        g = [("sMM", c.val("sMM"))]
                    v_wl = load(WDS[:, it * 512:(it + 1) * 512],
                                wdT_h[it, :, dc * 512:(dc + 1) * 512], sem="Lw", gate=g)
                gate0 = [("Sa", c.val("Sa")), ("Sb", c.val("Sb")),
                         ("sSC", c.val("sSC")), ("sVE", c.val("sVE"))]
                for tt in range(NT):
                    sl = tt % 2
                    tc2 = tt // tpc
                    xg = list(gate0) if tt < 2 else [xstore[sl]]
                    v_x = load(SCR2[:, sl * D + dc * 512: sl * D + (dc + 1) * 512],
                               src_res_h[tt * 128:(tt + 1) * 128, dc * 512:(dc + 1) * 512],
                               sem=LS[sl], gate=xg)
                    slot = ring_pos[0] % RING
                    ring_pos[0] += 1
                    c.wait("tensor", "sSC", ring_free[slot][0])
                    c.wait("tensor", "sVE", ring_free[slot][1])
                    if tt == 0:
                        waitp("tensor", v_wl)
                    if dc == 0 and rint_col is not None and tt % tpc == 0:
                        c.wait("tensor", "sVE", rc_done[tc2])
                    last_mm = None
                    for it in range(NI):
                        is_last = (it == NI - 1)
                        fn = (lambda t, itt=it, ttt=tt, slo=slot, st=(it == 0), sp=is_last:
                              t.matmul(PS[:, slo * 512:(slo + 1) * 512],
                                       G[:, itt * T + ttt * 128: itt * T + (ttt + 1) * 128],
                                       WDS[:, itt * 512:(itt + 1) * 512],
                                       start=st, stop=sp))
                        if is_last:
                            last_mm = ti(fn)
                        else:
                            c.on("tensor", fn)
                    if dc == 0 and rint_col is not None and tc2 + 1 < ntc2:
                        i0 = (tt % tpc) * ipt
                        i1 = min(NI, i0 + ipt)
                        if i0 < i1:
                            rc_done[tc2 + 1] = rint_chunk(rint_col, tc2 + 1, 512,
                                                          i0, i1)
                    c.wait("vector", "sMM", last_mm)
                    waitp("vector", v_x)
                    v_e = vi(lambda v, slo=slot, sll=sl, dcc=dc:
                             v.scalar_tensor_tensor(
                                 SCR2[:, sll * D + dcc * 512: sll * D + (dcc + 1) * 512],
                                 PS[:, slo * 512:(slo + 1) * 512], CONST[:, 7:8],
                                 SCR2[:, sll * D + dcc * 512: sll * D + (dcc + 1) * 512],
                                 AluOpType.mult, AluOpType.add))
                    ring_free[slot] = [ring_free[slot][0], v_e]
                    c.wait("scalar", "sVE", v_e)
                    xstore[sl] = sstore(
                        dst_res_h[tt * 128:(tt + 1) * 128, dc * 512:(dc + 1) * 512],
                        SCR2[:, sl * D + dc * 512: sl * D + (dc + 1) * 512], sem=SS[sl])

        def attention(xstore):
            ln_pass(xr_h, False)
            wgate = [("sSC", c.val("sSC")), ("sVE", c.val("sVE"))]
            c.wait("sync", wgate[0][0], wgate[0][1])
            c.wait("sync", wgate[1][0], wgate[1][1])
            v1 = load_cast(wq_h[:, :].rearrange("(cc p) k -> p cc k", p=128),
                           WQKV[:, 0:1024], 1024, 0)
            v2 = load_cast(wk_h[:, :].rearrange("(cc p) k -> p cc k", p=128),
                           WQKV[:, 1024:2048], 1024, 1)
            c.wait("sync", "sSC", v1)
            v3 = load_cast(wv_h[:, :].rearrange("(cc p) k -> p cc k", p=128),
                           WQKV[:, 2048:3072], 1024, 0)
            c.wait("sync", "sSC", v2)
            v_lwo = load(SCR2[:, D:2 * D], wo_h[:, :], sem="Lb")
            waitp("scalar", v_lwo)
            s_wo = si(lambda s: s.activation(WOB[:, 0:D], SCR2[:, D:2 * D], AF.Copy))
            barrier_t()
            for w_i, dstv in ((0, QTv), (1, KTv)):
                for tc in range(T // 512):
                    slot = ring_pos[0] % RING
                    ring_pos[0] += 1
                    c.wait("tensor", "sSC", ring_free[slot][0])
                    c.wait("tensor", "sVE", ring_free[slot][1])
                    last_mm = None
                    for cb in range(ND):
                        is_last = (cb == ND - 1)
                        fn = (lambda t, wi=w_i, cbb=cb, tcc=tc, slo=slot,
                              st=(cb == 0), sp=is_last:
                              t.matmul(PS[:, slo * 512:(slo + 1) * 512],
                                       WQKV[:, (wi * ND + cbb) * 128:(wi * ND + cbb + 1) * 128],
                                       XT[:, cbb * T + tcc * 512: cbb * T + (tcc + 1) * 512],
                                       start=st, stop=sp))
                        if is_last:
                            last_mm = ti(fn)
                        else:
                            c.on("tensor", fn)
                    c.wait("scalar", "sMM", last_mm)
                    s_e = si(lambda s, d=dstv, tcc=tc, slo=slot: s.activation(
                        d[:, tcc * 512:(tcc + 1) * 512],
                        PS[:, slo * 512:(slo + 1) * 512], AF.Copy))
                    ring_free[slot] = [s_e, ring_free[slot][1]]
            for tt in range(NT):
                slot = ring_pos[0] % RING
                ring_pos[0] += 1
                c.wait("tensor", "sSC", ring_free[slot][0])
                c.wait("tensor", "sVE", ring_free[slot][1])
                last_mm = None
                for cb in range(ND):
                    is_last = (cb == ND - 1)
                    fn = (lambda t, cbb=cb, ttt=tt, slo=slot, st=(cb == 0), sp=is_last:
                          t.matmul(PS[:, slo * 512: slo * 512 + 128],
                                   XT[:, cbb * T + ttt * 128: cbb * T + (ttt + 1) * 128],
                                   WQKV[:, (2 * ND + cbb) * 128:(2 * ND + cbb + 1) * 128],
                                   start=st, stop=sp))
                    if is_last:
                        last_mm = ti(fn)
                    else:
                        c.on("tensor", fn)
                c.wait("scalar", "sMM", last_mm)
                s_e = si(lambda s, ttt=tt, slo=slot: s.activation(
                    VBv[:, ttt * 128:(ttt + 1) * 128],
                    PS[:, slo * 512: slo * 512 + 128], AF.Copy))
                ring_free[slot] = [s_e, ring_free[slot][1]]
            c.wait("tensor", "sSC", c.val("sSC"))
            c.wait("tensor", "sVE", c.val("sVE"))
            gate0 = [("Sa", c.val("Sa")), ("Sb", c.val("Sb")),
                     ("sSC", c.val("sSC")), ("sVE", c.val("sVE"))]
            ABp = [ABv, ABv2]
            scrA_rd = [scr_rd[0], scr_rd[1]]
            st = {}
            pv = {"s_at": 0, "ot": 0, "y": 0, "ltr": {}}
            atk_pos = [0]

            def stageA(qt):
                L = (qt + 1) * 128
                par = qt % 2
                scr = SCR if par == 0 else SCR3
                v_x = load(SCR2[:, par * D:(par + 1) * D],
                           xr_h[qt * 128:(qt + 1) * 128, :], sem=LS[par],
                           gate=list(gate0) if qt < 2 else [xstore[par]])
                nkb = (L + 511) // 512
                dkb = (qt * 128) // 512
                doff = qt * 128 - dkb * 512
                c.wait("scalar", "sVE", scrA_rd[par])
                last_sev = 0
                for kb in range(nkb):
                    w = min(512, L - kb * 512)
                    slot = atk_pos[0] % 5
                    atk_pos[0] += 1
                    c.wait("tensor", "sSC", ring_free[slot][0])
                    c.wait("tensor", "sVE", ring_free[slot][1])
                    mm = ti(lambda t, qtt=qt, kbb=kb, ww=w, slo=slot:
                            t.matmul(PS[:, slo * 512: slo * 512 + ww],
                                     QTv[:, qtt * 128:(qtt + 1) * 128],
                                     KTv[:, kbb * 512: kbb * 512 + ww],
                                     start=True, stop=True))
                    if kb == dkb:
                        c.wait("vector", "sMM", mm)
                        v_m = vi(lambda v, slo=slot, off=doff: v.tensor_tensor(
                            PS[:, slo * 512 + off: slo * 512 + off + 128],
                            PS[:, slo * 512 + off: slo * 512 + off + 128],
                            MASKD[:], AluOpType.add))
                    if kb % 2 == 0:
                        if kb == dkb:
                            c.wait("scalar", "sVE", v_m)
                        else:
                            c.wait("scalar", "sMM", mm)
                        s_ev = si(lambda s, d=scr, kbb=kb, ww=w, slo=slot:
                                  s.activation(d[:, kbb * 512: kbb * 512 + ww],
                                               PS[:, slo * 512: slo * 512 + ww],
                                               AF.Copy))
                        ring_free[slot] = [s_ev, 0]
                        c.wait("vector", "sSC", s_ev)
                        last_sev = s_ev
                    else:
                        if kb != dkb:
                            c.wait("vector", "sMM", mm)
                        v_ev = vi(lambda v, d=scr, kbb=kb, ww=w, slo=slot:
                                  v.tensor_copy(d[:, kbb * 512: kbb * 512 + ww],
                                                PS[:, slo * 512: slo * 512 + ww]))
                        ring_free[slot] = [0, v_ev]
                    vi(lambda v, d=scr, kbb=kb, ww=w, pp=par: v.tensor_reduce(
                        MROWB[:, pp * 4 + kbb: pp * 4 + kbb + 1],
                        d[:, kbb * 512: kbb * 512 + ww], AX, AluOpType.max))
                v_cmb = vi(lambda v, pp=par, n=nkb: v.tensor_reduce(
                    MROW1[:, pp:pp + 1], MROWB[:, pp * 4: pp * 4 + n], AX,
                    AluOpType.max))
                v_brow = vi(lambda v, pp=par: v.tensor_scalar(
                    BROWB[:, pp:pp + 1], MROW1[:, pp:pp + 1], -CINV, None,
                    AluOpType.mult))
                scrA_rd[par] = v_cmb
                st[qt] = {"vx": v_x, "brow": v_brow, "sev": last_sev}

            def stageB(qt):
                L = (qt + 1) * 128
                par = qt % 2
                scr = SCR if par == 0 else SCR3
                ab = ABp[par]
                nblk = qt + 1
                c.wait("scalar", "sVE", st[qt]["brow"])
                if qt >= 2:
                    c.wait("scalar", "sVE", st[qt - 2]["recip"])
                    c.wait("scalar", "sMM", pv["ltr"].get(qt - 2, 0))
                Lh = min(L, 1024)
                s_exp = si(lambda s, d=scr, a=ab, LL=Lh, pp=par: s.activation(
                    a[:, 0:LL], d[:, 0:LL], AF.Exp, scale=CINV,
                    bias=BROWB[:, pp:pp + 1], accum_out=RSUMB[:, pp:pp + 1]))
                s_exp2 = None
                if L > 1024:
                    s_exp2 = si(lambda s, d=scr, a=ab, pp=par: s.activation(
                        a[:, 1024:L], d[:, 1024:L], AF.Exp, scale=CINV,
                        bias=BROWB[:, pp:pp + 1], accum_out=RS2B[:, pp:pp + 1]))
                c.wait("vector", "sSC", s_exp2 if s_exp2 else s_exp)
                if s_exp2:
                    vi(lambda v, pp=par: v.tensor_tensor(
                        RSTT[:, pp:pp + 1], RSUMB[:, pp:pp + 1], RS2B[:, pp:pp + 1],
                        AluOpType.add))
                    v_rr = vi(lambda v, qtt=qt, pp=par: v.reciprocal(
                        RRS[:, qtt:qtt + 1], RSTT[:, pp:pp + 1]))
                else:
                    v_rr = vi(lambda v, qtt=qt, pp=par: v.reciprocal(
                        RRS[:, qtt:qtt + 1], RSUMB[:, pp:pp + 1]))
                st[qt]["recip"] = v_rr
                s_at = pv["s_at"]
                t_tr = 0
                last_o = None
                c.wait("tensor", "sSC", s_exp)
                for grp in range(0, nblk, 8):
                    gn = min(8, nblk - grp)
                    if s_exp2 and grp >= 8:
                        c.wait("tensor", "sSC", s_exp2)
                    c.wait("tensor", "sVE", s_at)
                    if qt == 0:
                        c.wait("tensor", "sSC", ptb["cp"])
                    for j in range(gn):
                        t_tr = ti(lambda t, a=ab, kk=grp + j, jj=j: t.transpose(
                            PTB[:, jj * 128:(jj + 1) * 128],
                            a[:, kk * 128:(kk + 1) * 128], IDB[:]))
                    c.wait("vector", "sMM", t_tr)
                    s_at = vi(lambda v, g0=grp, gnn=gn: v.tensor_copy(
                        ATv[:, g0 * 128:(g0 + gnn) * 128], PTB[:, 0:gnn * 128]))
                    c.wait("tensor", "sVE", s_at)
                    if grp == 0:
                        c.wait("tensor", "sVE", pv["ot"])
                    for kc in range(grp, grp + gn):
                        fn = (lambda t, kcc=kc, stt=(kc == 0),
                              sp=(kc == nblk - 1):
                              t.matmul(PS[:, 2560:2688],
                                       VBv[:, kcc * 128:(kcc + 1) * 128],
                                       ATv[:, kcc * 128:(kcc + 1) * 128],
                                       start=stt, stop=sp))
                        if kc == nblk - 1:
                            last_o = ti(fn)
                        else:
                            c.on("tensor", fn)
                pv["s_at"] = s_at
                pv["ltr"][qt] = t_tr
                c.wait("vector", "sMM", last_o)
                s_ot = vi(lambda v: v.tensor_copy(OTv[:, :], PS[:, 2560:2688]))
                pv["ot"] = s_ot
                c.wait("tensor", "sVE", s_ot)
                ycols = []
                for dc2 in range(2):
                    if dc2 == 0:
                        c.wait("tensor", "sVE", pv["y"])
                        if qt == 0:
                            c.wait("tensor", "sSC", exch_prev["s1"])
                        ycol = 3072
                    else:
                        slot = atk_pos[0] % 5
                        atk_pos[0] += 1
                        c.wait("tensor", "sSC", ring_free[slot][0])
                        c.wait("tensor", "sVE", ring_free[slot][1])
                        ycol = slot * 512
                    ycols.append(ycol)
                    t_y = ti(lambda t, d2=dc2, yc=ycol: t.matmul(
                        PS[:, yc:yc + 512], OTv[:, :], WOB[:, d2 * 512:(d2 + 1) * 512],
                        start=True, stop=True))
                    c.wait("vector", "sMM", t_y)
                    if dc2 == 0:
                        waitp("vector", st[qt]["vx"])
                    pv["y"] = vi(lambda v, qtt=qt, d2=dc2, pp=par, yc=ycol:
                                 v.scalar_tensor_tensor(
                                     SCR2[:, pp * D + d2 * 512: pp * D + (d2 + 1) * 512],
                                     PS[:, yc:yc + 512], RRS[:, qtt:qtt + 1],
                                     SCR2[:, pp * D + d2 * 512: pp * D + (d2 + 1) * 512],
                                     AluOpType.mult, AluOpType.add))
                    if dc2 == 1:
                        ring_free[slot] = [0, pv["y"]]
                c.wait("scalar", "sVE", pv["y"])
                exch_prev["y"] = pv["y"]
                xstore[par] = sstore(xr_h[qt * 128:(qt + 1) * 128, :],
                                     SCR2[:, par * D:(par + 1) * D], sem=SS[par])

            for qt in range(NT):
                stageA(qt)
                if qt >= 1:
                    stageB(qt - 1)
            stageB(NT - 1)
            ptb["vrd"] = pv["s_at"]
            scr_rd[0] = scrA_rd[0]
            scr_rd[1] = scrA_rd[1]

        wg_sp = [None] * NI
        wu_sp = [None] * NI
        g_wg = gen_gateup(wg_h, wgT_h, WTS, wg_sp)
        g_wd = gen_down(WTS)
        g_wu = gen_gateup(wu_h, wuT_h, UPSTG, wu_sp)
        lngen = itertools.chain(g_wg, g_wd)

        def ln_pump(i):
            for _ in range(3):
                next(lngen, None)

        def gate_chain(it):
            next(g_wu, None)

        if T < 1024:
            g_wu = gen_gateup(wu_h, wuT_h, WTS, wu_sp)
            for _ in lngen:
                pass
            for _ in g_wu:
                pass
            scr_rd[0] = max(scr_rd[0], dqs_rd[0])
            scr_rd[1] = max(scr_rd[1], dqs_rd[1])
            ln_pump = None
            gate_chain = None

        xstore = [("Sa", 0), ("Sb", 0)]

        def ffn(src_res_h, dst_res_h, ln_pump=None, gate_chain=None,
                gate_spills=None, up_spills=None):
            ln_stats(src_res_h, True)
            fin0 = exchange((0, 1), math_r0, 1, 0, defer=True)
            ln_apply(src_res_h, pump=ln_pump)
            scr_rd[0] = max(scr_rd[0], dqs_rd[0])
            scr_rd[1] = max(scr_rd[1], dqs_rd[1])
            fin0()
            quantize_xt()
            dqs_rd[0] = max(scr_rd[0], dqs_rd[0])
            dqs_rd[1] = max(scr_rd[1], dqs_rd[1])
            mm_sweep(wgT_h, True, chain_fn=gate_chain, spills=gate_spills)
            scr_rd[0] = max(scr_rd[0], dqs_rd[0])
            scr_rd[1] = max(scr_rd[1], dqs_rd[1])
            fin1a = exchange((0, 1), math_r1a, 3, 1, defer=True)

            def chain(it):
                if it == 1:
                    fin1a()
                if it >= 1:
                    sig_it(it - 1)

            mm_sweep(wuT_h, False, chain_fn=chain, bar=False, spills=up_spills)
            sig_it(NI - 1)
            exchange((1, 2), math_r1b2, 2, 4)
            o_phase()
            exchange((0, 1), math_r3, 2, 6)
            down_phase(src_res_h, dst_res_h, xstore, rint_col=6)

        def finish_copy():
            for t in range(NT):
                sl = t % 2
                v_l = load(SCR2[:, sl * D:(sl + 1) * D],
                           x_h[t * 128:(t + 1) * 128, :], sem=LS[sl],
                           gate=[xstore[sl], ("sSC", c.val("sSC")),
                                 ("sVE", c.val("sVE")), ("sMM", c.val("sMM"))])
                waitp("scalar", v_l)
                xstore[sl] = sstore(out_h[t * 128:(t + 1) * 128, :],
                                    SCR2[:, sl * D:(sl + 1) * D], sem=SS[sl])

        if upto in ("deq", "ln1", "mm", "sig", "oph"):
            for _ in lngen:
                pass
            for _ in g_wu:
                pass
            scr_rd[0] = max(scr_rd[0], dqs_rd[0])
            scr_rd[1] = max(scr_rd[1], dqs_rd[1])
            if upto != "deq":
                ln_pass(x_h, True)
                exchange((0, 1), math_r0, 1, 0)
                quantize_xt()
            if upto in ("mm", "sig", "oph"):
                mm_sweep(wgT_h, True)
                fin1a = exchange((0, 1), math_r1a, 3, 1, defer=True)

                def chain(it):
                    if it == 1:
                        fin1a()
                    if it >= 1 and upto != "mm":
                        sig_it(it - 1)

                mm_sweep(wuT_h, False, chain_fn=chain, bar=False)
                if upto != "mm":
                    sig_it(NI - 1)
                exchange((1, 2), math_r1b2, 2, 4)
            if upto == "oph":
                o_phase()
                exchange((0, 1), math_r3, 2, 6)
                rint_g(6)
            finish_copy()
        elif upto == "ffn1":
            ffn(x_h, out_h, ln_pump=ln_pump, gate_chain=gate_chain,
                gate_spills=wg_sp, up_spills=wu_sp)
        elif upto == "attn":
            ffn(x_h, xr_h, ln_pump=ln_pump, gate_chain=gate_chain,
                gate_spills=wg_sp, up_spills=wu_sp)
            attention(xstore)
            for t in range(NT):
                sl = t % 2
                v_l = load(SCR2[:, sl * D:(sl + 1) * D],
                           xr_h[t * 128:(t + 1) * 128, :], sem=LS[sl],
                           gate=[xstore[sl], ("sSC", c.val("sSC")), ("sVE", c.val("sVE"))])
                waitp("scalar", v_l)
                xstore[sl] = sstore(out_h[t * 128:(t + 1) * 128, :],
                                    SCR2[:, sl * D:(sl + 1) * D], sem=SS[sl])
        else:
            ffn(x_h, xr_h, ln_pump=ln_pump, gate_chain=gate_chain,
                gate_spills=wg_sp, up_spills=wu_sp)
            attention(xstore)
            ffn(xr_h, out_h)

        c.wait("scalar", "Sa", c.val("Sa"))
        c.wait("scalar", "Sb", c.val("Sb"))
        c.wait("vector", "Sa", c.val("Sa"))
        c.wait("vector", "Sb", c.val("Sb"))

        block = es.enter_context(nc.Block())
        c.emit(block)
    return nc


_NC_CACHE = [None]


def kernel(L0_attn_x, L0_attn_Wq, L0_attn_Wk, L0_attn_Wv, L0_attn_Wo,
           L0_ffn_w_gate, L0_ffn_w_up, L0_ffn_w_down):
    if _NC_CACHE[0] is None:
        _NC_CACHE[0] = build()
    nc = _NC_CACHE[0]
    x = np.ascontiguousarray(np.asarray(L0_attn_x, dtype=np.float32))
    ws = [np.ascontiguousarray(np.asarray(w, dtype=np.float32)) for w in
          (L0_attn_Wq, L0_attn_Wk, L0_attn_Wv, L0_attn_Wo,
           L0_ffn_w_gate, L0_ffn_w_up, L0_ffn_w_down)]
    in_maps = [{"x": x[b], "wq": ws[0], "wk": ws[1], "wv": ws[2], "wo": ws[3],
                "wg": ws[4], "wu": ws[5], "wd": ws[6]} for b in range(8)]
    res = run_bass_kernel_spmd(nc, in_maps, core_ids=list(range(8)))
    return np.stack([res.results[b]["out"] for b in range(8)], axis=0)
